# version 12
# baseline (speedup 1.0000x reference)
import numpy as np
import ml_dtypes
import concourse.bass as bass
import concourse.mybir as mybir
from contextlib import ExitStack

F32, BF16 = mybir.dt.float32, mybir.dt.bfloat16
AF, ALU = mybir.ActivationFunctionType, mybir.AluOpType
AX = mybir.AxisListType.X
bfd = ml_dtypes.bfloat16

HID, FC, H, DK, CH, TILE = 1024, 8, 8, 128, 128, 512
NSEQ, TSEQ = 2, 2048
CPT = TILE // CH
NAT, NFT = 8, 8
E, ECH, NX = 512, 4, 18
EPS = 1e-5
SCL = float(DK) ** -0.5
NCHS = 16

QP, KP, KTr, VTr, ONr, HBr = 0, 4096, 8192, 12288, 16384, 20480
A1SL, ATSL, O2SL, STG, XSQA = 24576, 25600, 27648, 29696, 33792
U2N = 36864
WQ, WK, WV, WO = 0, 8192, 16384, 24576
USL = [0, 4096, 24576]
DSL = [8192, 12288, 28672]
H2r = [16384, 20480]
U1N = 32768
TA = [0, 512]
TB = [1024, 1536]
RC = [2048, 2560]
OST = [3072, 3584]
OCP = [4096, 4608]
SCRFN = 5120
GE, GC, GM, GS = 0, 64, 128, 192
GATEN = 256
C_ID, C_O1024, C_O128, C_MSK = 0, 128, 256, 384
C_EPS, C_OROW, C_OK1, C_ZERO, C_ONES = 896, 1024, 1536, 1664, 1792
C_SEL = 1920
CONSTN = 4096


class Cfg:
    def __init__(self, n_attn_tiles=NAT, n_ffn_tiles=NFT, dbg=False):
        self.n_attn_tiles = n_attn_tiles
        self.n_ffn_tiles = n_ffn_tiles
        self.dbg = dbg


class Ctr:
    __slots__ = ("name", "n", "sem")
    def __init__(self, name):
        self.name, self.n, self.sem = name, 0, None


class MK:
    def __init__(self):
        self.ops = {e: [] for e in "TVSQY"}
        self.ctrs = []
        self.last_wait = {e: {} for e in "TVSQY"}
        self.ec = {e: self.ctr("ec" + e) for e in "TVSQY"}

    def ctr(self, name):
        c = Ctr(name)
        self.ctrs.append(c)
        return c

    def emit(self, eng, fn, waits=(), sig=False, dma=None, inc=None):
        ws = []
        lw = self.last_wait[eng]
        for t in waits:
            if t is None:
                continue
            c, v = t
            if v <= 0 or lw.get(c, 0) >= v:
                continue
            lw[c] = v
            ws.append((c, v))
        if inc is not None:
            inc[0].n += inc[1]
            inc = (inc[0], inc[1])
        elif dma is not None:
            dma.n += 16
            inc = (dma, 16)
        elif sig:
            c = self.ec[eng]
            c.n += 1
            inc = (c, 1)
        self.ops[eng].append((fn, ws, inc))
        if inc is not None:
            return (inc[0], inc[0].n)
        if dma is not None:
            return (dma, dma.n)
        if sig:
            return (self.ec[eng], self.ec[eng].n)
        return None

    def check(self):
        vals = {id(c): 0 for c in self.ctrs}
        pos = {e: 0 for e in self.ops}
        progress = True
        while progress:
            progress = False
            for e, lst in self.ops.items():
                while pos[e] < len(lst):
                    fn, ws, inc = lst[pos[e]]
                    if any(vals[id(c)] < v for c, v in ws):
                        break
                    if inc is not None:
                        vals[id(inc[0])] += inc[1]
                    pos[e] += 1
                    progress = True
        stuck = {e: p for e, p in pos.items() if p < len(self.ops[e])}
        if stuck:
            lines = []
            for e, p in stuck.items():
                fn, ws, inc = self.ops[e][p]
                blocked = [(c.name, v, vals[id(c)]) for c, v in ws if vals[id(c)] < v]
                lines.append(f"engine {e} op#{p}/{len(self.ops[e])} blocked on {blocked} inc={inc[0].name if inc else None}")
            for c in self.ctrs:
                if c.name in ("ecT", "ecS", "ecV", "upc", "dnc"):
                    lines.append(f"ctr {c.name}: final_n={c.n} sim_val={vals[id(c)]}")
            raise RuntimeError("schedule deadlock:\n" + "\n".join(lines))

    def replay(self, block):
        def runner(e):
            def run(eng):
                for fn, ws, inc in self.ops[e]:
                    for c, v in ws:
                        eng.wait_ge(c.sem, v)
                    ins = fn(eng)
                    if inc is not None:
                        ins.then_inc(inc[0].sem, inc[1])
            return run
        block.gpsimd(runner("Q"))
        block.sync(runner("Y"))
        block.tensor(runner("T"))
        block.vector(runner("V"))
        block.scalar(runner("S"))


def consts_arrays():
    out = np.zeros((128, CONSTN), dtype=bfd)
    out[:, C_ID:C_ID + 128] = np.eye(128, dtype=bfd)
    out[:, C_O1024:C_O1024 + 128] = np.full((128, 128), 1.0 / 1024.0, dtype=bfd)
    out[:, C_O128:C_O128 + 128] = np.full((128, 128), 1.0 / 128.0, dtype=bfd)
    mt = np.tile(np.triu(np.ones((128, 128), np.float32)), (1, 4)).astype(bfd)
    out[:, C_MSK:C_MSK + 512] = mt
    out[0:1, C_EPS:C_EPS + 128] = np.full((1, 128), EPS, np.float32).astype(bfd)
    out[0:1, C_OROW:C_OROW + 512] = np.ones((1, 512), dtype=bfd)
    out[0:1, C_OK1:C_OK1 + 128] = np.ones((1, 128), dtype=bfd)
    out[:, C_ONES:C_ONES + 128] = np.ones((128, 128), dtype=bfd)
    sel = np.zeros((16, 16 * 128), dtype=bfd)
    for e in range(16):
        sel[e, e * 128:(e + 1) * 128] = 1.0
    out[0:16, C_SEL:C_SEL + 2048] = sel
    return out


def build(cfg):
    mk = MK()
    nc = bass.Bass("TRN2", target_bir_lowering=False, debug=False,
                   detect_race_conditions=False)

    def dram_in(name, shape):
        return nc.dram_tensor(name, list(shape), F32, kind="ExternalInput").ap()

    xTh = dram_in("xT", [HID, NSEQ * TSEQ])
    cTh = dram_in("cT", [HID, NSEQ])
    wqTh, wkTh = dram_in("wqT", [HID, HID]), dram_in("wkT", [HID, HID])
    wvTh, woTh = dram_in("wvT", [HID, HID]), dram_in("woT", [HID, HID])
    fm1qh, fm2qh = dram_in("fm1qT", [DK, DK]), dram_in("fm2qT", [DK, DK])
    fm1kh, fm2kh = dram_in("fm1kT", [DK, DK]), dram_in("fm2kT", [DK, DK])
    fbq1h, fbq2h = dram_in("fmq_b1", [DK, 1]), dram_in("fmq_b2", [DK, 1])
    fbk1h, fbk2h = dram_in("fmk_b1", [DK, 1]), dram_in("fmk_b2", [DK, 1])
    rmswh = dram_in("rms_w", [DK, 1])
    suTh, sdTh = dram_in("suT", [2, HID, E]), dram_in("sdT", [2, E, HID])
    ruTh, rdTh = dram_in("ruT", [16, HID, E]), dram_in("rdT", [16, E, HID])
    rwTh = dram_in("routerT", [HID, 16])
    wmodh = dram_in("wmod", [HID, 6 * HID])

    outT = nc.dram_tensor("outT", [HID, NSEQ * TSEQ], F32, kind="ExternalOutput").ap()
    scr_kind = "ExternalOutput" if cfg.dbg else "Internal"
    wup = nc.dram_tensor("wup", [NX, 128, FC * E], BF16, kind="Internal").ap()
    wdn = nc.dram_tensor("wdn", [NX, 128, ECH * HID], BF16, kind="Internal").ap()
    xscr = nc.dram_tensor("xscr", [128, FC, NSEQ * TSEQ], BF16, kind=scr_kind).ap()
    constd = nc.inline_tensor(consts_arrays(), "constd").ap()
    idf2 = nc.inline_tensor(np.eye(2, dtype=np.float32), "idf2").ap()
    idf128 = nc.inline_tensor(np.eye(128, dtype=np.float32), "idf128").ap()
    self4 = np.zeros((4, 512), np.float32)
    for e in range(4):
        self4[e, e * 128:(e + 1) * 128] = 1.0
    self4 = nc.inline_tensor(self4, "self4").ap()
    self8 = np.zeros((8, 1024), dtype=bfd)
    for e in range(8):
        self8[e, e * 128:(e + 1) * 128] = 1.0
    self8 = nc.inline_tensor(self8, "self8").ap()

    with ExitStack() as st:
        U1 = st.enter_context(nc.sbuf_tensor("U1", [128, U1N], BF16))[:]
        U2 = st.enter_context(nc.sbuf_tensor("U2", [128, U2N], BF16))[:]
        XT = st.enter_context(nc.sbuf_tensor("XTb", [128, 2 * FC * TILE], BF16))[:]
        SCRF = st.enter_context(nc.sbuf_tensor("SCRF", [128, SCRFN], F32))[:]
        GATE = st.enter_context(nc.sbuf_tensor("GATE", [128, GATEN], F32))[:]
        CON = st.enter_context(nc.sbuf_tensor("CON", [128, CONSTN], BF16))[:]
        MODF = st.enter_context(nc.sbuf_tensor("MODF", [128, 96], F32))[:]
        DERV = st.enter_context(nc.sbuf_tensor("DERV", [128, 64], F32))[:]
        SMALL = st.enter_context(nc.sbuf_tensor("SMALL", [128, 16], F32))[:]
        CTF = st.enter_context(nc.sbuf_tensor("CTF", [128, 16], F32))[:]
        CTB = st.enter_context(nc.sbuf_tensor("CTB", [128, 16], BF16))[:]
        RT = st.enter_context(nc.sbuf_tensor("RT", [128, 128], BF16))[:]
        FMW = st.enter_context(nc.sbuf_tensor("FMW", [128, 512], BF16))[:]
        MODTMP = st.enter_context(nc.sbuf_tensor("MODTMP", [2, 512], F32))[:]
        SILF = st.enter_context(nc.sbuf_tensor("SILF", [128, 16], F32))[:]
        IDF = st.enter_context(nc.sbuf_tensor("IDF", [2, 2], F32))[:]
        IDFB = st.enter_context(nc.sbuf_tensor("IDFB", [128, 128], F32))[:]
        SBFS = st.enter_context(nc.sbuf_tensor("SBFS", [128, 2048], BF16))[:]
        GTT = st.enter_context(nc.sbuf_tensor("GTT", [16, 1024], BF16))[:]
        GBS = st.enter_context(nc.sbuf_tensor("GBS", [128, 512], BF16))[:]
        GATB = st.enter_context(nc.sbuf_tensor("GATB", [128, 64], BF16))[:]
        STATS = st.enter_context(nc.sbuf_tensor("STATS", [128, 128], F32))[:]
        ROWB = st.enter_context(nc.sbuf_tensor("ROWB", [4, 2560], F32))[:]
        RW8 = st.enter_context(nc.sbuf_tensor("RW8", [8, 128], BF16))[:]
        CONF8 = st.enter_context(nc.sbuf_tensor("CONF8", [8, 1024], BF16))[:]
        SELF4 = st.enter_context(nc.sbuf_tensor("SELF4", [4, 512], F32))[:]
        PB = [st.enter_context(nc.psum_tensor(f"PB{i}", [128, 512], F32))[:]
              for i in range(8)]

        qld = mk.ctr("qld")
        yconst = mk.ctr("yconst")
        qx = [mk.ctr("qx0"), mk.ctr("qx1")]
        yx = [mk.ctr("yx0"), mk.ctr("yx1")]
        yxp = [mk.ctr("yxp0"), mk.ctr("yxp1")]
        wmq = [mk.ctr("wmq0"), mk.ctr("wmq1")]
        qsc = [mk.ctr("qs0"), mk.ctr("qs1")]
        ysm = [mk.ctr("ys0"), mk.ctr("ys1")]
        yws = [mk.ctr(f"yws{i}") for i in range(4)]
        qws = [mk.ctr(f"qws{i}") for i in range(2)]
        upc = mk.ctr("upc")
        dnc = mk.ctr("dnc")
        so = [mk.ctr("so0"), mk.ctr("so1")]
        ydbg = mk.ctr("ydbg")

        def T(fn, waits=(), sig=False):
            return mk.emit("T", fn, waits, sig=sig)
        def V(fn, waits=(), sig=False):
            return mk.emit("V", fn, waits, sig=sig)
        def S(fn, waits=(), sig=False):
            return mk.emit("S", fn, waits, sig=sig)
        def Q(fn, waits=(), dma=None):
            return mk.emit("Q", fn, waits, dma=dma)
        def Y(fn, waits=(), dma=None):
            return mk.emit("Y", fn, waits, dma=dma)

        dbg_names = []
        def dump(name, ap, waits):
            if not cfg.dbg:
                return
            t = nc.dram_tensor("dbg_" + name, list(ap.shape), ap.dtype,
                               kind="ExternalOutput").ap()
            dbg_names.append("dbg_" + name)
            idx = tuple(slice(None) for _ in ap.shape)
            Y(lambda y, o=t, i=ap: y.dma_start(out=o[idx] if o.ndim > 1 else o, in_=i),
              waits, dma=ydbg)

        pb_free = {i: None for i in range(8)}
        xt_free = [None, None]
        lnp = {"aff": [None] * 8, "msq6": None, "msq7": None}

        Y(lambda y: y.dma_start(out=CON[:, :], in_=constd[:, :]), dma=yconst)
        Y(lambda y: y.dma_start(out=IDF[:, :], in_=idf2[:, :]), dma=yconst)
        Y(lambda y: y.dma_start(out=IDFB[:, :], in_=idf128[:, :]), dma=yconst)
        Y(lambda y: y.dma_start(out=SELF4[:, :], in_=self4[:, :]), dma=yconst)
        Y(lambda y: y.dma_start(out=CONF8[:, :], in_=self8[:, :]), dma=yconst)
        t_const = (yconst, yconst.n)

        def cast_load(dst, src):
            return Q(lambda q, d=dst, s=src: q.dma_start(out=d, in_=s), dma=qld)

        cast_load(U1[:, WQ:WQ + 8192].rearrange("p (kc d) -> p kc d", kc=8),
                  wqTh.rearrange("(kc p) d -> p kc d", p=128))
        cast_load(U1[:, WK:WK + 8192].rearrange("p (kc d) -> p kc d", kc=8),
                  wkTh.rearrange("(kc p) d -> p kc d", p=128))
        cast_load(U1[:, WV:WV + 8192].rearrange("p (kc d) -> p kc d", kc=8),
                  wvTh.rearrange("(kc p) d -> p kc d", p=128))
        cast_load(U1[:, WO:WO + 8192].rearrange("p (kc d) -> p kc d", kc=8),
                  woTh.rearrange("(kc p) d -> p kc d", p=128))
        cast_load(FMW[:, 0:128], fm1qh[:, :])
        cast_load(FMW[:, 128:256], fm2qh[:, :])
        cast_load(FMW[:, 256:384], fm1kh[:, :])
        cast_load(FMW[:, 384:512], fm2kh[:, :])
        cast_load(RT[:, 0:128].rearrange("p (kc e) -> p kc e", kc=8),
                  rwTh.rearrange("(kc p) e -> p kc e", p=128))
        for dst, src in ((SMALL[:, 0:1], fbq1h), (SMALL[:, 1:2], fbq2h),
                         (SMALL[:, 2:3], fbk1h), (SMALL[:, 3:4], fbk2h),
                         (SMALL[:, 4:5], rmswh)):
            Q(lambda q, d=dst, s=src: q.dma_start(out=d, in_=s[:, :]), dma=qld)
        Q(lambda q: q.dma_start(out=CTF[:, :].rearrange("p (ic b) -> p ic b", ic=8),
                                in_=cTh.rearrange("(ic p) b -> p ic b", p=128)), dma=qld)
        t_qld = (qld, qld.n)

        def load_xtile_attn(g):
            sl = g % 2
            dst = XT[:, sl * 4096:(sl + 1) * 4096].rearrange("p (fc t) -> p fc t", fc=8)
            src = xTh.rearrange("(fc p) t -> p fc t", p=128)[:, :, g * TILE:(g + 1) * TILE]
            wfree = xt_free[sl] if isinstance(xt_free[sl], list) else [xt_free[sl]]
            Q(lambda q, d=dst, s=src: q.dma_start(out=d, in_=s),
              wfree, dma=qx[sl])

        if cfg.n_attn_tiles > 0:
            load_xtile_attn(0)
        if cfg.n_attn_tiles > 1:
            load_xtile_attn(1)
        need_stats0 = cfg.n_attn_tiles > 0

        t_siluf = S(lambda s: s.activation(SILF[:, :], CTF[:, :], AF.Silu),
                    [t_qld], sig=True)
        t_silu = V(lambda v: v.tensor_copy(CTB[:, :], SILF[:, :]),
                   [t_siluf], sig=True)
        t_fmw = V(lambda v: v.tensor_scalar_mul(FMW[:, 0:128], FMW[:, 0:128], SCL),
                  [t_qld], sig=True)
        t_b1q = V(lambda v: v.tensor_scalar_mul(SMALL[:, 0:1], SMALL[:, 0:1], SCL),
                  [t_qld], sig=True)
        t_wo_scaled = None
        for kc in range(8):
            t_wo_scaled = S(lambda s, kc=kc: s.activation(
                U1[:, WO + kc * 1024:WO + (kc + 1) * 1024],
                U1[:, WO + kc * 1024:WO + (kc + 1) * 1024],
                AF.Identity, scale=SMALL[:, 4:5]), [t_qld], sig=True)

        slab_last = {}
        t_modf = None
        t_ev = None
        ttr_all = None
        for pc in range(12):
            isf = pc in (4, 5, 10, 11)
            par = pc % 2
            slab_id = ("F" if isf else "B") + str(par)
            src_p = wmodh.rearrange("(ic p) j -> p ic j", p=128)[
                :, :, pc * 512:(pc + 1) * 512]
            if isf:
                base = 8192 + par * 8192
                slab = U2[:, base:base + 8192].bitcast(F32).rearrange(
                    "p (ic j) -> p ic j", ic=8)
                slab_mm = U2[:, base:base + 8192].bitcast(F32)
            else:
                base = par * 4096
                slab = U2[:, base:base + 4096].rearrange("p (ic j) -> p ic j", ic=8)
                slab_mm = U2[:, base:base + 4096]
            wq_waits = [slab_last.get(slab_id)]
            Q(lambda q, d=slab, s=src_p: q.dma_start(out=d, in_=s),
              wq_waits, dma=wmq[par])
            t_wm = (wmq[par], wmq[par].n)
            tmm = None
            for ic in range(8):
                w = [t_wm, t_silu, t_siluf]
                if ic == 0 and t_ev is not None:
                    w.append(t_ev)
                tmm = T(lambda t, ic=ic, isf=isf, sm=slab_mm: t.matmul(
                    PB[6][0:2, :],
                    (SILF if isf else CTB)[:, ic * 2:ic * 2 + 2],
                    sm[:, ic * 512:(ic + 1) * 512],
                    start=(ic == 0), stop=(ic == 7), skip_group_check=True),
                    w, sig=(ic == 7))
            slab_last[slab_id] = tmm
            t_ev = S(lambda s: s.copy(MODTMP[0:2, :], PB[6][0:2, :]), [tmm], sig=True)
            for k in range(4):
                w = [t_ev, t_const]
                ttr_all = T(lambda t, k=k, pc=pc: t.transpose(
                    PB[7][:, (pc * 4 + k) * 2:(pc * 4 + k) * 2 + 2],
                    MODTMP[0:2, k * 128:(k + 1) * 128],
                    IDF[0:2, 0:2]), w, sig=(k == 3))
        t_modf = S(lambda s: s.copy(MODF[:, 0:96], PB[7][:, 0:96]),
                   [ttr_all], sig=True)
        t_d = []
        t_d.append(V(lambda v: v.tensor_scalar_add(DERV[:, 0:16], MODF[:, 16:32], 1.0),
                     [t_modf], sig=True))
        t_d.append(V(lambda v: v.tensor_scalar_add(DERV[:, 32:48], MODF[:, 64:80], 1.0),
                     [t_modf], sig=True))
        t_mod_done = None
        for (src0, dst0) in ((32, 16), (80, 48)):
            ta = S(lambda s, src0=src0: s.activation(
                GATE[:, GE:GE + 16], MODF[:, src0:src0 + 16], AF.Abs),
                [t_modf] + ([t_mod_done] if t_mod_done else []), sig=True)
            tb = V(lambda v: v.tensor_scalar_max(GATE[:, GE:GE + 16],
                                                 GATE[:, GE:GE + 16], 0.25),
                   [ta] + t_d, sig=True)
            tc_ = V(lambda v, src0=src0: v.tensor_scalar(
                GATE[:, GC:GC + 16], MODF[:, src0:src0 + 16], 0.0, None, ALU.is_ge),
                [t_modf], sig=True)
            td = V(lambda v: v.tensor_scalar(GATE[:, GC:GC + 16], GATE[:, GC:GC + 16],
                                             2.0, -1.0, ALU.mult, ALU.add),
                   [tc_], sig=True)
            t_mod_done = V(lambda v, dst0=dst0: v.tensor_tensor(
                DERV[:, dst0:dst0 + 16], GATE[:, GE:GE + 16], GATE[:, GC:GC + 16],
                ALU.mult), [tb, td], sig=True)

        dump("modf", MODF, [t_modf])
        dump("derv", DERV, [t_mod_done])

        stage_list = []
        for x in range(NX):
            src = (suTh[x] if x < 2 else ruTh[x - 2]).rearrange("(kc p) e -> p kc e", p=128)
            stage_list.append((src[:, 0:4], wup[x][:, 0:2048], 4))
            stage_list.append((src[:, 4:8], wup[x][:, 2048:4096], 4))
        for x in range(NX):
            src = (sdTh[x] if x < 2 else rdTh[x - 2]).rearrange("(ec p) h -> p ec h", p=128)
            stage_list.append((src[:, 0:2], wdn[x][:, 0:2048], 2))
            stage_list.append((src[:, 2:4], wdn[x][:, 2048:4096], 2))
        sst = {"i": 0, "wr": []}

        def emit_staging(nmax):
            cnt = 0
            while sst["i"] < len(stage_list) and cnt < nmax:
                i = sst["i"]
                src, dst, a = stage_list[i]
                par = i % 2
                slab = U2[:, STG + par * 2048:STG + (par + 1) * 2048]
                w = [sst["wr"][i - 2]] if i >= 2 else []
                Q(lambda q, d=slab, s=src, a=a: q.dma_start(
                    out=d.rearrange("p (a b) -> p a b", a=a), in_=s), w, dma=qsc[par])
                t_rd = (qsc[par], qsc[par].n)
                Y(lambda y, d=dst, s=slab: y.dma_start(out=d, in_=s),
                  [t_rd], dma=ysm[par])
                sst["wr"].append((ysm[par], ysm[par].n))
                sst["i"] += 1
                cnt += 1

        xsqwar = {0: None, 1: None}
        rowb_war = {}
        stat_toks = {}

        def emit_stats(kind, g, xt_ap, x_ready, bank_idx, extra=()):
            pb = PB[bank_idx]
            mcol = (0 if kind == "a" else 64) + g * 4
            rcol = (32 if kind == "a" else 96) + g * 4
            tz = T(lambda t: t.matmul(
                pb[:, 0:8], CON[0:1, C_ZERO:C_ZERO + 128],
                CON[0:1, C_OROW:C_OROW + 8],
                start=True, stop=False, skip_group_check=True),
                [t_const, pb_free[bank_idx]], sig=False)
            t_msqf = [None] * 8
            tmm = None
            for fc in range(8):
                for st_ in range(4):
                    w = [x_ready] if (fc == 0 and st_ == 0) else []
                    tmm = T(lambda t, st_=st_, fc=fc: t.matmul(
                        pb[:, st_:st_ + 1],
                        xt_ap[:, fc, st_ * 128:(st_ + 1) * 128],
                        CON[:, C_O1024:C_O1024 + 1],
                        start=False, stop=False, skip_group_check=True),
                        w, sig=(st_ == 3 and fc == 7))
            t_m_done = tmm
            for fc in range(8):
                sl = fc % 2
                w = [x_ready, xsqwar[sl] if fc < 2 else t_msqf[fc - 2]] + list(extra)
                t_xsq = V(lambda v, fc=fc, sl=sl: v.tensor_tensor(
                    U2[:, XSQA + sl * 512:XSQA + (sl + 1) * 512],
                    xt_ap[:, fc], xt_ap[:, fc], ALU.mult), w, sig=True)
                for st_ in range(4):
                    w = [t_xsq] if st_ == 0 else []
                    tmm = T(lambda t, st_=st_, fc=fc, sl=sl: t.matmul(
                        pb[:, 4 + st_:5 + st_],
                        U2[:, XSQA + sl * 512 + st_ * 128:XSQA + sl * 512 + (st_ + 1) * 128],
                        CON[:, C_O1024:C_O1024 + 1],
                        start=False, stop=False, skip_group_check=True),
                        w, sig=(st_ == 3))
                t_msqf[fc] = tmm
            xsqwar[0], xsqwar[1] = t_msqf[6], t_msqf[7]
            t_mc = V(lambda v, mcol=mcol: v.tensor_copy(STATS[:, mcol:mcol + 4],
                                                        pb[:, 0:4]),
                     [t_m_done, t_msqf[7]], sig=True)
            t_m2 = V(lambda v, mcol=mcol: v.tensor_tensor(
                GATE[:, GS + 16:GS + 20],
                STATS[:, mcol:mcol + 4], STATS[:, mcol:mcol + 4], ALU.mult),
                [t_mc], sig=True)
            t_vv = V(lambda v: v.scalar_tensor_tensor(
                GATE[:, GS + 20:GS + 24], pb[:, 4:8], EPS,
                GATE[:, GS + 16:GS + 20], ALU.add, ALU.subtract),
                [t_m2, t_msqf[7]], sig=True)
            t_rc = V(lambda v: v.reciprocal(GATE[:, GS + 20:GS + 24],
                                            GATE[:, GS + 20:GS + 24]), [t_vv], sig=True)
            pb_free[bank_idx] = t_rc
            t_sq = S(lambda s, rcol=rcol: s.sqrt(STATS[:, rcol:rcol + 4],
                                                 GATE[:, GS + 20:GS + 24]),
                     [t_rc], sig=True)
            rb = (g % 2) * 256 if kind == "a" else 512 + g * 256
            ttm = T(lambda t, mcol=mcol: t.transpose(
                pb[0:4, 8:136], STATS[:, mcol:mcol + 4], IDFB[:, :]),
                [t_mc, t_vv], sig=True)
            tem = S(lambda s, rb=rb: s.copy(ROWB[0:4, rb:rb + 128], pb[0:4, 8:136]),
                    [ttm, rowb_war.get((kind, g % 2))], sig=True)
            ttr = T(lambda t, rcol=rcol: t.transpose(
                pb[0:4, 136:264], STATS[:, rcol:rcol + 4], IDFB[:, :]),
                [t_sq], sig=True)
            ter = S(lambda s, rb=rb: s.copy(ROWB[0:4, rb + 128:rb + 256],
                                            pb[0:4, 136:264]), [ttr], sig=True)
            pb_free[bank_idx] = ter
            stat_toks[(kind, g)] = (tem, ter, rb)
            return tmm, ter

        def emit_apply(kind, g, xt_ap, hbuf, hbase, onep_base, sh_base, b_,
                       x_ready, pb_m, pb_m_idx, pb_v, pb_v_idx, h_war,
                       aff_on_s=False):
            tem, ter, rb = stat_toks[(kind, g)]
            tbm = None
            tbr = None
            for st_ in range(4):
                w = [tem]
                if st_ == 0:
                    w.append(pb_free[pb_m_idx])
                tbm = T(lambda t, st_=st_, rb=rb: t.matmul(
                    pb_m[:, st_ * 128:(st_ + 1) * 128],
                    SELF4[0:4, st_ * 128:(st_ + 1) * 128],
                    ROWB[0:4, rb:rb + 128],
                    start=True, stop=True, skip_group_check=True),
                    w, sig=(st_ == 3))
                w = [ter]
                if st_ == 0:
                    w.append(pb_free[pb_v_idx])
                tbr = T(lambda t, st_=st_, rb=rb: t.matmul(
                    pb_v[:, st_ * 128:(st_ + 1) * 128],
                    SELF4[0:4, st_ * 128:(st_ + 1) * 128],
                    ROWB[0:4, rb + 128:rb + 256],
                    start=True, stop=True, skip_group_check=True),
                    w, sig=(st_ == 3))
            rowb_war[(kind, g % 2)] = tbr
            t_aff = [None] * 8
            for fc in range(8):
                sl = fc % 2
                w = [tbm, x_ready]
                if fc >= 2:
                    w.append(t_aff[fc - 2])
                else:
                    w.append(lnp["aff"][6 + fc])
                t_sub = V(lambda v, fc=fc, sl=sl: v.tensor_tensor(
                    SCRF[:, TA[sl]:TA[sl] + 512], xt_ap[:, fc], pb_m, ALU.subtract),
                    w, sig=True)
                t_mul = V(lambda v, fc=fc, sl=sl: v.tensor_tensor(
                    SCRF[:, TB[sl]:TB[sl] + 512], SCRF[:, TA[sl]:TA[sl] + 512],
                    pb_v, ALU.mult), [tbr, t_sub], sig=True)
                if aff_on_s:
                    t_aff[fc] = S(lambda s, fc=fc, sl=sl: s.activation(
                        hbuf[:, hbase + fc * 512:hbase + (fc + 1) * 512],
                        SCRF[:, TB[sl]:TB[sl] + 512], AF.Identity,
                        bias=MODF[:, sh_base + fc * 2 + b_:sh_base + fc * 2 + b_ + 1],
                        scale=DERV[:, onep_base + fc * 2 + b_:onep_base + fc * 2 + b_ + 1]),
                        [t_mul, t_mod_done, h_war if fc == 0 else None], sig=True)
                else:
                    t_aff[fc] = V(lambda v, fc=fc, sl=sl: v.tensor_scalar(
                        hbuf[:, hbase + fc * 512:hbase + (fc + 1) * 512],
                        SCRF[:, TB[sl]:TB[sl] + 512],
                        DERV[:, onep_base + fc * 2 + b_:onep_base + fc * 2 + b_ + 1],
                        MODF[:, sh_base + fc * 2 + b_:sh_base + fc * 2 + b_ + 1],
                        ALU.mult, ALU.add),
                        [t_mod_done, h_war if fc == 0 else None], sig=True)
            lnp["aff"] = t_aff
            pb_free[pb_m_idx] = t_aff[7]
            pb_free[pb_v_idx] = t_aff[7]
            return t_aff

        atn = {"g": None, "t_on_last": None, "cp_tok": {}, "t_kv": {},
               "t_o": {}, "t_o_b": {}, "xpost": [None] * NAT,
               "h_last": None}
        if need_stats0:
            emit_stats("a", 0, XT[:, 0:4096].rearrange("p (fc t) -> p fc t", fc=8),
                       (qx[0], 16), 1)

        def emit_wo(gp):
            if gp is None:
                return
            sl = gp % 2
            s_ = gp // 4
            res_toks = []
            for dc in range(8):
                bidx = 2 + dc % 2
                tmm = None
                for hc in range(8):
                    w = [atn["t_on_last"], t_wo_scaled]
                    if hc == 0:
                        w.append(pb_free[bidx])
                    rhs = U2[:, ONr:ONr + 4096].rearrange(
                        "p (c h t) -> p c h t", c=4, h=8)[:, :, hc, :]
                    tmm = T(lambda t, dc=dc, hc=hc, bidx=bidx, rhs=rhs: t.matmul(
                        PB[bidx],
                        U1[:, WO + hc * 1024 + dc * 128:WO + hc * 1024 + (dc + 1) * 128],
                        rhs, start=(hc == 0), stop=(hc == 7), skip_group_check=True),
                        w, sig=(hc == 7))
                tres = V(lambda v, dc=dc, sl=sl, bidx=bidx, s_=s_: v.scalar_tensor_tensor(
                    XT[:, sl * 4096 + dc * 512:sl * 4096 + (dc + 1) * 512],
                    PB[bidx], DERV[:, 16 + dc * 2 + s_:16 + dc * 2 + s_ + 1],
                    XT[:, sl * 4096 + dc * 512:sl * 4096 + (dc + 1) * 512],
                    ALU.mult, ALU.add), [tmm, t_mod_done], sig=True)
                pb_free[bidx] = tres
                res_toks.append(tres)
                atn.setdefault("wo_toks", {}).setdefault(gp, []).append(tmm)
            Y(lambda y, sl=sl, g=gp: y.dma_start(
                out=xscr[:, :, g * TILE:(g + 1) * TILE],
                in_=XT[:, sl * 4096:(sl + 1) * 4096].rearrange("p (fc t) -> p fc t", fc=8)),
              res_toks, dma=yxp[sl])
            atn["xpost"][gp] = (yxp[sl], yxp[sl].n)
            if cfg.n_ffn_tiles > gp:
                xap = XT[:, sl * 4096:(sl + 1) * 4096].rearrange(
                    "p (fc t) -> p fc t", fc=8)
                tstat, _ = emit_stats("f", gp, xap, res_toks[-1], 1)
                xt_free[sl] = [(yxp[sl], yxp[sl].n), tstat]
            else:
                xt_free[sl] = (yxp[sl], yxp[sl].n)

        def proj_featmajor(dst_base, wbase, t_h):
            isq = dst_base == QP
            fmo = 0 if isq else 256
            bias1 = SMALL[:, 0:1] if isq else SMALL[:, 2:3]
            bias2 = SMALL[:, 1:2] if isq else SMALL[:, 3:4]
            BKS = [4, 5, 2]
            t_comb = [None] * 8
            t_ev = [None] * 8
            t_a1 = [None] * 8
            t_p1 = [None] * 8
            t_p2 = [None] * 8

            def q_mms(dc):
                bidx = BKS[dc % 3]
                tq = None
                for kc in range(8):
                    w = [t_h[kc]]
                    if kc == 0:
                        w.append(pb_free[bidx])
                    tq = T(lambda t, dc=dc, kc=kc, bidx=bidx, wbase=wbase: t.matmul(
                        PB[bidx],
                        U1[:, wbase + kc * 1024 + dc * 128:wbase + kc * 1024 + (dc + 1) * 128],
                        U2[:, HBr + kc * 512:HBr + (kc + 1) * 512],
                        start=(kc == 0), stop=(kc == 7), skip_group_check=True),
                        w, sig=(kc == 7))
                t_ev[dc] = S(lambda s, dc=dc, bidx=bidx, dst_base=dst_base: s.copy(
                    U2[:, dst_base + dc * 512:dst_base + (dc + 1) * 512], PB[bidx]),
                    [tq], sig=True)

            def p1_mm(dc):
                bidx = BKS[dc % 3]
                t_p1[dc] = T(lambda t, dc=dc, bidx=bidx, fmo=fmo, dst_base=dst_base:
                             t.matmul(
                    PB[bidx], FMW[:, fmo:fmo + 128],
                    U2[:, dst_base + dc * 512:dst_base + (dc + 1) * 512],
                    start=True, stop=True, skip_group_check=True),
                    [t_ev[dc], t_fmw, t_b1q], sig=True)
                w = [t_p1[dc]]
                if dc >= 2:
                    w.append(t_comb[dc - 2])
                t_a1[dc] = S(lambda s, dc=dc, bidx=bidx, b1=bias1: s.activation(
                    U2[:, A1SL + (dc % 2) * 512:A1SL + (dc % 2 + 1) * 512],
                    PB[bidx], AF.Identity, bias=b1), w, sig=True)

            def p2_mm(dc):
                bidx = BKS[dc % 3]
                t_p2[dc] = T(lambda t, dc=dc, bidx=bidx, fmo=fmo, dst_base=dst_base:
                             t.matmul(
                    PB[bidx], FMW[:, fmo + 128:fmo + 256],
                    U2[:, dst_base + dc * 512:dst_base + (dc + 1) * 512],
                    start=True, stop=True, skip_group_check=True),
                    [t_a1[dc]], sig=True)
                t_comb[dc] = V(lambda v, dc=dc, bidx=bidx, b2=bias2, dst_base=dst_base:
                               v.scalar_tensor_tensor(
                    U2[:, dst_base + dc * 512:dst_base + (dc + 1) * 512],
                    PB[bidx], b2,
                    U2[:, A1SL + (dc % 2) * 512:A1SL + (dc % 2 + 1) * 512],
                    ALU.add, ALU.mult), [t_p2[dc]], sig=True)
                pb_free[bidx] = t_comb[dc]

            for dc in range(10):
                if dc < 8:
                    q_mms(dc)
                if 1 <= dc:
                    if dc - 1 < 8:
                        p1_mm(dc - 1)
                if 2 <= dc:
                    if dc - 2 < 8:
                        p2_mm(dc - 2)
            p1_mm(7) if t_p1[7] is None else None
            p2_mm(6) if t_p2[6] is None else None
            p2_mm(7) if t_p2[7] is None else None
            return t_comb

        apply_pend = {}
        for g in range(cfg.n_attn_tiles):
            s_ = g // 4
            sl = g % 2
            base = (g % 4) * CPT
            x_ready = (qx[sl], qx[sl].n)
            xt_ap = XT[:, sl * 4096:(sl + 1) * 4096].rearrange("p (fc t) -> p fc t", fc=8)

            if g == 0:
                t_h = emit_apply("a", 0, xt_ap, U2, HBr, 0, 0, s_,
                                 x_ready, PB[0], 0, PB[1], 1, atn["h_last"],
                                 aff_on_s=True)
            else:
                t_h = apply_pend[g]

            if g == 0:
                dump("h0", U2[:, HBr:HBr + 4096], [t_h[7]])
            tq_comb = proj_featmajor(QP, WQ, t_h)
            tk_comb = proj_featmajor(KP, WK, t_h)
            if g == 0:
                dump("qp0", U2[:, QP:QP + 4096], [tq_comb[7], tq_comb[6]])
                dump("kp0", U2[:, KP:KP + 4096], [tk_comb[7], tk_comb[6]])

            t_vev = [None] * 8
            for st_ in range(4):
                for nh in range(2):
                    bidx = 4 + (st_ * 2 + nh) % 2
                    tv = None
                    for kc in range(8):
                        w = [t_h[kc]]
                        if kc == 0:
                            w.append(pb_free[bidx])
                        tv = T(lambda t, st_=st_, nh=nh, kc=kc, bidx=bidx: t.matmul(
                            PB[bidx],
                            U2[:, HBr + kc * 512 + st_ * 128:HBr + kc * 512 + (st_ + 1) * 128],
                            U1[:, WV + kc * 1024 + nh * 512:WV + kc * 1024 + (nh + 1) * 512],
                            start=(kc == 0), stop=(kc == 7), skip_group_check=True),
                            w, sig=(kc == 7))
                    t_vev[st_ * 2 + nh] = S(lambda s, st_=st_, nh=nh, bidx=bidx: s.copy(
                        U2[:, VTr + st_ * 1024 + nh * 512:VTr + st_ * 1024 + (nh + 1) * 512],
                        PB[bidx]), [tv], sig=True)
                    pb_free[bidx] = t_vev[st_ * 2 + nh]
            t_ktev = [None] * 4
            for batch in range(4):
                ttr = None
                for j in range(8):
                    hh = (batch * 8 + j) // 4
                    pr = (batch * 8 + j) % 4
                    w = [tk_comb[hh], t_const]
                    if j == 0:
                        w.append(pb_free[3])
                    ttr = T(lambda t, hh=hh, pr=pr, j=j: t.transpose(
                        PB[3].bitcast(BF16)[:, j * 128:(j + 1) * 128],
                        U2[:, KP + hh * 512 + pr * 128:KP + hh * 512 + (pr + 1) * 128],
                        CON[:, C_ID:C_ID + 128]), w, sig=(j == 7))
                t_ktev[batch] = S(lambda s, batch=batch: s.copy(
                    U2[:, KTr + batch * 1024:KTr + (batch + 1) * 1024],
                    PB[3].bitcast(BF16)[:, 0:1024]), [ttr], sig=True)
                pb_free[3] = t_ktev[batch]
            t_kt_done = t_ktev[3]
            t_v_done = t_vev[7]
            atn["h_last"] = t_v_done

            emit_wo(atn["g"])
            if g + 1 < cfg.n_attn_tiles:
                load_xtile_attn(g + 1)
            emit_staging(0 if getattr(cfg, 'no_stg_il', False) else 10)
            if g + 1 < cfg.n_attn_tiles:
                sl1 = (g + 1) % 2
                emit_stats("a", g + 1,
                           XT[:, sl1 * 4096:(sl1 + 1) * 4096].rearrange(
                               "p (fc t) -> p fc t", fc=8),
                           (qx[sl1], qx[sl1].n), 1)

            if base == 0:
                for bi in (6, 7):
                    T(lambda t, bi=bi: t.matmul(
                        PB[bi], CON[0:1, C_ZERO:C_ZERO + 128],
                        CON[0:1, C_OROW:C_OROW + 512],
                        start=True, stop=False, skip_group_check=True),
                        [t_const], sig=False)

            a_free = {2: pb_free[2], 3: pb_free[3]}
            o_free = {4: pb_free[4], 5: pb_free[5]}
            t_mask = {}
            rms = {"sq": {}, "msq_tok": {}, "small": {}, "rsb": {},
                   "bc_war": pb_free[0], "tr_war": pb_free[1],
                   "rsb_war": None, "msq_war": pb_free[1]}

            def kv_mms(c):
                n = base + c
                tkv = None
                for h in range(8):
                    w = []
                    if h == 0:
                        w = [t_kt_done, t_v_done]
                        if n % NCHS != 0:
                            w.append(atn["cp_tok"].get(n))
                    tkv = T(lambda t, c=c, h=h: t.matmul(
                        PB[6 + h // 4][:, (h % 4) * 128:(h % 4 + 1) * 128],
                        U2[:, KTr + h * 512 + c * 128:KTr + h * 512 + (c + 1) * 128],
                        U2[:, VTr + c * 1024 + h * 128:VTr + c * 1024 + (h + 1) * 128],
                        start=False, stop=False, skip_group_check=True),
                        w, sig=(h == 7))
                atn["t_kv"][n] = tkv

            def state_copy(n):
                if n % NCHS == 0:
                    return
                par = n % 2
                w = [atn["t_kv"][n - 1], atn["t_o"].get(n - 2)]
                S(lambda s, par=par: s.copy(SBFS[:, par * 1024:par * 1024 + 512], PB[6]),
                  w, sig=True)
                atn["cp_tok"][n] = S(lambda s, par=par: s.copy(
                    SBFS[:, par * 1024 + 512:par * 1024 + 1024], PB[7]), [], sig=True)

            def a_mms(c):
                toks = {}
                for h in range(8):
                    bidx = 2 + h // 4
                    w = [tq_comb[h], tk_comb[h]]
                    if h % 4 == 0:
                        w.append(a_free[bidx])
                    ta = T(lambda t, c=c, h=h, bidx=bidx: t.matmul(
                        PB[bidx][:, (h % 4) * 128:(h % 4 + 1) * 128],
                        U2[:, KP + h * 512 + c * 128:KP + h * 512 + (c + 1) * 128],
                        U2[:, QP + h * 512 + c * 128:QP + h * 512 + (c + 1) * 128],
                        start=True, stop=True, skip_group_check=True),
                        w, sig=(h == 3 or h == 7))
                    if h in (3, 7):
                        toks[bidx] = ta
                return toks

            def mask_op(c, toks):
                n = base + c
                atsl = ATSL + (c % 2) * 1024
                tms = []
                for bi in (2, 3):
                    w = [toks[bi], t_const, atn["t_o"].get(n - 2)]
                    tm = V(lambda v, bi=bi, atsl=atsl: v.tensor_tensor(
                        U2[:, atsl + (bi - 2) * 512:atsl + (bi - 1) * 512],
                        PB[bi], CON[:, C_MSK:C_MSK + 512], ALU.mult),
                        w, sig=True)
                    a_free[bi] = tm
                    tms.append(tm)
                t_mask[n] = tms

            def o_mms(c):
                n = base + c
                if n % NCHS != 0:
                    for h in range(8):
                        bidx = 4 + h // 4
                        w = [tq_comb[h]]
                        if h % 4 == 0:
                            w += [o_free[bidx], atn["cp_tok"][n]]
                        T(lambda t, c=c, h=h, n=n, bidx=bidx: t.matmul(
                            PB[bidx][:, (h % 4) * 128:(h % 4 + 1) * 128],
                            SBFS[:, (n % 2) * 1024 + h * 128:(n % 2) * 1024 + (h + 1) * 128],
                            U2[:, QP + h * 512 + c * 128:QP + h * 512 + (c + 1) * 128],
                            start=(h % 4 == 0), stop=False, skip_group_check=True),
                            w, sig=False)
                to = {}
                st_flag = (n % NCHS == 0)
                for h in range(8):
                    bidx = 4 + h // 4
                    w = [t_mask[n][h // 4]]
                    if st_flag and h % 4 == 0:
                        w.append(o_free[bidx])
                    to[bidx] = T(lambda t, c=c, h=h, bidx=bidx, st_flag=st_flag: t.matmul(
                        PB[bidx][:, (h % 4) * 128:(h % 4 + 1) * 128],
                        U2[:, VTr + c * 1024 + h * 128:VTr + c * 1024 + (h + 1) * 128],
                        U2[:, ATSL + (c % 2) * 1024 + h * 128:ATSL + (c % 2) * 1024 + (h + 1) * 128],
                        start=(st_flag and h % 4 == 0), stop=True, skip_group_check=True),
                        w, sig=(h == 3 or h == 7))
                atn["t_o"][n] = to[5]
                atn["t_o_b"][n] = (to[4], to[5])

            def o_evac(c):
                n = base + c
                onb = ONr + c * 1024
                o2b = O2SL + (c % 2) * 1024
                t4, t5 = atn["t_o_b"][n]
                cp4 = V(lambda v, onb=onb: v.tensor_copy(U2[:, onb:onb + 512], PB[4]),
                        [t4], sig=True)
                cp5 = V(lambda v, onb=onb: v.tensor_copy(
                    U2[:, onb + 512:onb + 1024], PB[5]), [t5], sig=True)
                sq4 = V(lambda v, onb=onb, o2b=o2b: v.tensor_tensor(
                    U2[:, o2b:o2b + 512], U2[:, onb:onb + 512], U2[:, onb:onb + 512],
                    ALU.mult), [cp4, rms["msq_tok"].get(c - 2)], sig=True)
                sq5 = V(lambda v, onb=onb, o2b=o2b: v.tensor_tensor(
                    U2[:, o2b + 512:o2b + 1024], U2[:, onb + 512:onb + 1024],
                    U2[:, onb + 512:onb + 1024], ALU.mult), [cp5], sig=True)
                o_free[4] = cp4
                o_free[5] = cp5
                rms["sq"][c] = (sq4, sq5)

            def rms_msq(c):
                o2b = O2SL + (c % 2) * 1024
                sq4, sq5 = rms["sq"][c]
                mc = 300 + (c % 2) * 8
                tq = None
                for h in range(8):
                    w = [sq4 if h < 4 else sq5]
                    if h == 0:
                        w.append(rms["msq_war"])
                    tq = T(lambda t, h=h, mc=mc, o2b=o2b: t.matmul(
                        PB[1][:, mc + h:mc + h + 1],
                        U2[:, o2b + h * 128:o2b + (h + 1) * 128],
                        CON[:, C_O128:C_O128 + 1],
                        start=True, stop=True, skip_group_check=True),
                        w, sig=(h == 7))
                rms["msq_tok"][c] = tq

            def rms_small(c):
                mc = 300 + (c % 2) * 8
                gc_ = GS + 24 + (c % 2) * 8
                t1 = V(lambda v, mc=mc, gc_=gc_: v.tensor_scalar_add(
                    GATE[:, gc_:gc_ + 8], PB[1][:, mc:mc + 8], EPS),
                    [rms["msq_tok"][c]], sig=True)
                t2 = V(lambda v, gc_=gc_: v.reciprocal(
                    GATE[:, gc_:gc_ + 8], GATE[:, gc_:gc_ + 8]), [t1], sig=True)
                t3 = S(lambda s, gc_=gc_: s.sqrt(
                    GATE[:, gc_:gc_ + 8], GATE[:, gc_:gc_ + 8]), [t2], sig=True)
                rms["small"][c] = t3
                rms["msq_war"] = t1

            def rms_tr(c):
                gc_ = GS + 24 + (c % 2) * 8
                ttr = T(lambda t, gc_=gc_: t.transpose(
                    PB[1][0:8, 384:512], GATE[:, gc_:gc_ + 8], IDFB[:, :]),
                    [rms["small"][c], rms["tr_war"], t_const], sig=True)
                trsb = S(lambda s: s.copy(RW8[0:8, 0:128],
                                          PB[1][0:8, 384:512]),
                         [ttr, rms["rsb_war"]], sig=True)
                rms["rsb"][c] = trsb
                rms["tr_war"] = trsb

            def rms_bcast(c):
                trsb = rms["rsb"][c]
                last_mm = None
                for r in range(2):
                    mm = None
                    for hh in range(4):
                        h = r * 4 + hh
                        w = [trsb]
                        if hh == 0:
                            w.append(rms["bc_war"])
                        mm = T(lambda t, h=h, hh=hh: t.matmul(
                            PB[0][:, hh * 128:(hh + 1) * 128],
                            CONF8[0:8, h * 128:(h + 1) * 128],
                            RW8[0:8, 0:128],
                            start=True, stop=True, skip_group_check=True),
                            w, sig=(hh == 3))
                    ton = V(lambda v, c=c, r=r: v.tensor_tensor(
                        U2[:, ONr + c * 1024 + r * 512:ONr + c * 1024 + (r + 1) * 512],
                        U2[:, ONr + c * 1024 + r * 512:ONr + c * 1024 + (r + 1) * 512],
                        PB[0], ALU.mult), [mm], sig=True)
                    rms["bc_war"] = ton
                    last_mm = mm
                rms["rsb_war"] = last_mm
                atn["t_on_last"] = rms["bc_war"]

            for c in range(CPT):
                kv_mms(c)
                toks = a_mms(c)
                state_copy(base + c + 1)
                mask_op(c, toks)
                o_mms(c)
                o_evac(c)
                if c >= 2:
                    rms_bcast(c - 2)
                if c >= 1:
                    rms_msq(c - 1)
                    rms_small(c - 1)
                    rms_tr(c - 1)
            rms_bcast(CPT - 2)
            rms_msq(CPT - 1)
            rms_small(CPT - 1)
            rms_tr(CPT - 1)
            rms_bcast(CPT - 1)
            if g == 0:
                dump("vt0", U2[:, VTr:VTr + 4096], [t_v_done])
                dump("on0", U2[:, ONr:ONr + 4096], [atn["t_on_last"]])

            pb_free[2] = a_free[2]
            pb_free[3] = a_free[3]
            pb_free[4] = o_free[4]
            pb_free[5] = o_free[5]
            pb_free[0] = rms["bc_war"]
            pb_free[1] = rms["rsb"][CPT - 1]

            if g + 1 < cfg.n_attn_tiles:
                sl1 = (g + 1) % 2
                apply_pend[g + 1] = emit_apply(
                    "a", g + 1,
                    XT[:, sl1 * 4096:(sl1 + 1) * 4096].rearrange(
                        "p (fc t) -> p fc t", fc=8),
                    U2, HBr, 0, 0, (g + 1) // 4, (qx[sl1], qx[sl1].n),
                    PB[0], 0, PB[1], 1, atn["h_last"], aff_on_s=True)

            atn["g"] = g

        emit_wo(atn["g"])
        emit_staging(100)

        t_attn_T = (mk.ec["T"], mk.ec["T"].n)
        t_attn_V = (mk.ec["V"], mk.ec["V"].n)
        t_attn_S = (mk.ec["S"], mk.ec["S"].n)
        stg_done = sst["wr"][-2:] if sst["wr"] else []

        if cfg.n_ffn_tiles > 0:
            T(lambda t: t.ldweights(CON[:, C_ID:C_ID + 128]),
              [t_attn_V, t_attn_S], sig=True)
            V(lambda v: v.memset(GATE[:, GS + 16:GS + 17], 0.0),
              [t_attn_T, t_attn_S], sig=True)
            S(lambda s: s.memzero(SMALL[:, 14:15]),
              [t_attn_T, t_attn_V], sig=True)

        ffn = {"t_h2": {}, "t_gt": {}, "gt_used": {}, "t_dn_bank": {},
               "ws_free": {0: None, 1: None, 2: None, 3: None},
               "xf_tok": {}, "xf_used": {}, "act_ready": {}, "h2_used": {}}

        def load_xtile_ffn(g):
            sl = g % 2
            dst = XT[:, sl * 4096:(sl + 1) * 4096].rearrange("p (fc t) -> p fc t", fc=8)
            w = [atn["xpost"][g] if g < len(atn["xpost"]) else None,
                 ffn["xf_used"].get(g - 2)]
            w += xt_free[sl] if isinstance(xt_free[sl], list) else [xt_free[sl]]
            if getattr(cfg, "xload_y", False):
                Y(lambda y, d=dst, g=g: y.dma_start(
                    out=d, in_=xscr[:, :, g * TILE:(g + 1) * TILE]), w, dma=yx[sl])
            else:
                Q(lambda q, d=dst, g=g: q.dma_start(
                    out=d, in_=xscr[:, :, g * TILE:(g + 1) * TILE]), w, dma=yx[sl])
            ffn["xf_tok"][g] = (yx[sl], yx[sl].n)

        def emit_lnb(g):
            if g >= cfg.n_ffn_tiles:
                return
            sl = g % 2
            b_ = g // 4
            xt_ap = XT[:, sl * 4096:(sl + 1) * 4096].rearrange("p (fc t) -> p fc t", fc=8)
            ffn["t_h2"][g] = emit_apply(
                "f", g, xt_ap, U1, H2r[sl], 32, 48, b_, ffn["xf_tok"][g],
                PB[0], 0, PB[2], 2, ffn["h2_used"].get(g - 2))

        def emit_router(g):
            if g >= cfg.n_ffn_tiles:
                return
            sl = g % 2
            tl = None
            for st_ in range(4):
                for kc in range(8):
                    w = [ffn["t_h2"][g][kc], t_qld]
                    if st_ == 0 and kc == 0:
                        w.append(pb_free[3])
                    tl = T(lambda t, st_=st_, kc=kc, sl=sl: t.matmul(
                        PB[3][:, st_ * 16:(st_ + 1) * 16],
                        U1[:, H2r[sl] + kc * 512 + st_ * 128:H2r[sl] + kc * 512 + (st_ + 1) * 128],
                        RT[:, kc * 16:(kc + 1) * 16],
                        start=(kc == 0), stop=(kc == 7), skip_group_check=True),
                        w, sig=(st_ == 3 and kc == 7))
            lseg = PB[3][:, 0:64].rearrange("p (gg n) -> p gg n", gg=4)
            t1 = V(lambda v: v.reduce_max(GATE[:, GS:GS + 4], lseg, axis=AX),
                   [tl], sig=True)
            eseg = GATE[:, GE:GE + 64].rearrange("p (gg n) -> p gg n", gg=4)
            t2 = V(lambda v: v.tensor_tensor(
                eseg, lseg, GATE[:, GS:GS + 4].unsqueeze(2).broadcast_to([128, 4, 16]),
                ALU.subtract), [t1], sig=True)
            pb_free[3] = t2
            t3 = S(lambda s: s.activation(GATE[:, GE:GE + 64], GATE[:, GE:GE + 64],
                                          AF.Exp), [t2], sig=True)
            t4 = V(lambda v: v.reduce_sum(GATE[:, GS + 4:GS + 8], eseg, axis=AX),
                   [t3], sig=True)
            t5 = V(lambda v: v.reciprocal(GATE[:, GS + 8:GS + 12],
                                          GATE[:, GS + 4:GS + 8]), [t4], sig=True)
            tcur = V(lambda v: v.tensor_copy(GATE[:, GC:GC + 64], GATE[:, GE:GE + 64]),
                     [t3], sig=True)
            cseg = GATE[:, GC:GC + 64].rearrange("p (gg n) -> p gg n", gg=4)
            mseg = GATE[:, GM:GM + 64].rearrange("p (gg n) -> p gg n", gg=4)
            for r in range(4):
                tcur = V(lambda v: v.reduce_max(GATE[:, GS + 12:GS + 16], cseg, axis=AX),
                         [tcur], sig=True)
                if r < 3:
                    tm = V(lambda v: v.tensor_tensor(
                        mseg, cseg,
                        GATE[:, GS + 12:GS + 16].unsqueeze(2).broadcast_to([128, 4, 16]),
                        ALU.is_ge), [tcur], sig=True)
                    tcur = V(lambda v: v.scalar_tensor_tensor(
                        GATE[:, GC:GC + 64], GATE[:, GM:GM + 64], -1e9,
                        GATE[:, GC:GC + 64], ALU.mult, ALU.add), [tm], sig=True)
            tg1 = V(lambda v: v.tensor_tensor(
                mseg, eseg,
                GATE[:, GS + 12:GS + 16].unsqueeze(2).broadcast_to([128, 4, 16]),
                ALU.is_ge), [tcur], sig=True)
            tg2 = V(lambda v: v.tensor_tensor(GATE[:, GM:GM + 64], GATE[:, GM:GM + 64],
                                              GATE[:, GE:GE + 64], ALU.mult),
                    [tg1], sig=True)
            gseg = GATE[:, GE:GE + 64].rearrange("p (gg n) -> p gg n", gg=4)
            tg3 = V(lambda v: v.tensor_tensor(
                gseg, mseg,
                GATE[:, GS + 8:GS + 12].unsqueeze(2).broadcast_to([128, 4, 16]),
                ALU.mult), [tg2, t5], sig=True)
            tg4 = V(lambda v: v.tensor_copy(GATB[:, :], GATE[:, GE:GE + 64]),
                    [tg3], sig=True)
            ffn["tg4"] = (tg4, g)

        def emit_router_gt(g):
            if g >= cfg.n_ffn_tiles:
                return
            sl = g % 2
            tg4, g_chk = ffn["tg4"]
            assert g_chk == g
            ttr = None
            for st_ in range(4):
                w = [tg4, t_const]
                if st_ == 0:
                    w.append(pb_free[5])
                    w.append(ffn["gt_used"].get(g - 2))
                ttr = T(lambda t, st_=st_: t.transpose(
                    PB[5].bitcast(BF16)[0:16, st_ * 128:(st_ + 1) * 128],
                    GATB[:, st_ * 16:(st_ + 1) * 16],
                    CON[:, C_ID:C_ID + 128]), w, sig=(st_ == 3))
            tge = S(lambda s, sl=sl: s.copy(GTT[0:16, sl * 512:(sl + 1) * 512],
                                            PB[5].bitcast(BF16)[0:16, 0:512]),
                    [ttr], sig=True)
            pb_free[5] = tge
            ffn["t_gt"][g] = tge

        def stream_w(x_, g, which, base):
            i = x_ % 3
            src = (wup if which == "up" else wdn)[x_]
            ctr = upc if which == "up" else dnc
            w = [(ctr, base + x_ - 2) if base + x_ - 2 > 0 else None]
            if g == 0:
                stg_idx = 2 * x_ + 1 if which == "up" else 2 * NX + 2 * x_ + 1
                w.append(sst["wr"][stg_idx] if stg_idx < len(sst["wr"]) else None)
                if x_ < 3:
                    w += [t_attn_T, t_attn_V, t_attn_S]
            if which == "up":
                Y(lambda y, d=U1[:, USL[i]:USL[i] + 4096], s=src: y.dma_start(
                    out=d, in_=s[:, :]), w, dma=yws[i])
                return (yws[i], yws[i].n)
            if i < 2:
                q_ = qws[i]
                Q(lambda q, d=U1[:, DSL[i]:DSL[i] + 4096], s=src: q.dma_start(
                    out=d, in_=s[:, :]), w, dma=q_)
                return (q_, q_.n)
            Y(lambda y, d=U1[:, DSL[2]:DSL[2] + 4096], s=src: y.dma_start(
                out=d, in_=s[:, :]), w, dma=yws[3])
            return (yws[3], yws[3].n)

        for g in range(cfg.n_ffn_tiles):
            sl = g % 2
            b_ = g // 4
            if g == 0:
                load_xtile_ffn(0)
                if cfg.n_ffn_tiles > 1:
                    load_xtile_ffn(1)
                emit_lnb(0)
                dump("h2t0", U1[:, H2r[0]:H2r[0] + 4096], [ffn["t_h2"][0][7]])
                emit_router(0)
                emit_router_gt(0)
            up_base, dn_base = upc.n, dnc.n
            lazy = getattr(cfg, "lazy_streams", False)
            if not lazy:
                up_toks = [stream_w(x_, g, "up", up_base) for x_ in range(NX)]
                dn_toks = [stream_w(x_, g, "dn", dn_base) for x_ in range(NX)]
            BUP = (6, 7, 1)
            for x_ in range(NX):
                t_ws = stream_w(x_, g, "up", up_base) if lazy else up_toks[x_]
                tup_last = None
                gel = [None] * ECH
                for ec in range(ECH):
                    bidx = BUP[(x_ * ECH + ec) % 3]
                    tup = None
                    for kc in range(8):
                        w = [ffn["t_h2"][g][kc]]
                        if kc == 0:
                            w += [t_ws, pb_free[bidx]]
                        if kc == 7 and ec == ECH - 1:
                            tup = mk.emit("T", lambda t, x_=x_, ec=ec, kc=kc, bidx=bidx,
                                          i=x_ % 3, sl=sl: t.matmul(
                                PB[bidx],
                                U1[:, USL[i] + kc * 512 + ec * 128:USL[i] + kc * 512 + (ec + 1) * 128],
                                U1[:, H2r[sl] + kc * 512:H2r[sl] + (kc + 1) * 512],
                                start=(kc == 0), stop=(kc == 7), skip_group_check=True),
                                w, inc=(upc, 1))
                        else:
                            tup = T(lambda t, x_=x_, ec=ec, kc=kc, bidx=bidx, i=x_ % 3, sl=sl:
                                    t.matmul(
                                PB[bidx],
                                U1[:, USL[i] + kc * 512 + ec * 128:USL[i] + kc * 512 + (ec + 1) * 128],
                                U1[:, H2r[sl] + kc * 512:H2r[sl] + (kc + 1) * 512],
                                start=(kc == 0), stop=(kc == 7), skip_group_check=True),
                                w, sig=(kc == 7))
                    gel[ec] = S(lambda s, x_=x_, ec=ec, bidx=bidx: s.activation(
                        U2[:, x_ * 2048 + ec * 512:x_ * 2048 + (ec + 1) * 512],
                        PB[bidx], AF.Gelu), [tup], sig=True)
                    pb_free[bidx] = gel[ec]
                    tup_last = tup
                if x_ >= 2:
                    e_idx = x_ - 2
                    tbc = T(lambda t, e_idx=e_idx, sl=sl: t.matmul(
                        PB[4], CON[0:16, C_SEL + e_idx * 128:C_SEL + (e_idx + 1) * 128],
                        GTT[0:16, sl * 512:(sl + 1) * 512],
                        start=True, stop=True, skip_group_check=True),
                        [ffn["t_gt"][g], pb_free[4], t_const], sig=True)
                    ffn["gt_used"][g] = tbc
                    tgb = S(lambda s: s.copy(GBS[:, :], PB[4]),
                            [tbc, ffn["act_ready"].get(x_ - 1)], sig=True)
                    if g == 0 and x_ == 2:
                        dump("gbs0", GBS, [tgb])
                        dump("gtt0", GTT[0:16, 0:512], [ffn["t_gt"][g]])
                    pb_free[4] = tgb
                    tmul = None
                    for ec in range(ECH):
                        tmul = V(lambda v, x_=x_, ec=ec: v.tensor_tensor(
                            U2[:, x_ * 2048 + ec * 512:x_ * 2048 + (ec + 1) * 512],
                            GBS[:, :],
                            U2[:, x_ * 2048 + ec * 512:x_ * 2048 + (ec + 1) * 512],
                            ALU.mult), [gel[ec], tgb], sig=(ec == ECH - 1))
                    ffn["act_ready"][x_] = tmul
                    if g == 0 and x_ == 2:
                        dump("act2", U2[:, 2 * 2048:3 * 2048], [tmul])
                else:
                    ffn["act_ready"][x_] = gel[ECH - 1]
                    if g == 0 and x_ == 0:
                        dump("act0", U2[:, 0:2048], [gel[ECH - 1]])
                if x_ == 3:
                    emit_lnb(g + 1)
                if x_ == 11:
                    emit_router(g + 1)
                if x_ == 15:
                    emit_router_gt(g + 1)
            ffn["h2_used"][g] = (upc, upc.n)
            for x_ in range(NX):
                t_ws = stream_w(x_, g, "dn", dn_base) if lazy else dn_toks[x_]
                tdn = None
                for hc in range(8):
                    for ec in range(ECH):
                        w = []
                        if ec == 0:
                            w = [t_ws, ffn["act_ready"][x_]]
                            if x_ == 0:
                                w.append(pb_free[hc])
                        if hc == 7 and ec == ECH - 1:
                            tdn = mk.emit("T", lambda t, x_=x_, hc=hc, ec=ec, i=x_ % 3:
                                          t.matmul(
                                PB[hc],
                                U1[:, DSL[i] + ec * 1024 + hc * 128:DSL[i] + ec * 1024 + (hc + 1) * 128],
                                U2[:, x_ * 2048 + ec * 512:x_ * 2048 + (ec + 1) * 512],
                                start=(x_ == 0 and ec == 0),
                                stop=(x_ == NX - 1 and ec == ECH - 1),
                                skip_group_check=True), w, inc=(dnc, 1))
                        else:
                            tdn = T(lambda t, x_=x_, hc=hc, ec=ec, i=x_ % 3: t.matmul(
                                PB[hc],
                                U1[:, DSL[i] + ec * 1024 + hc * 128:DSL[i] + ec * 1024 + (hc + 1) * 128],
                                U2[:, x_ * 2048 + ec * 512:x_ * 2048 + (ec + 1) * 512],
                                start=(x_ == 0 and ec == 0),
                                stop=(x_ == NX - 1 and ec == ECH - 1),
                                skip_group_check=True),
                                w, sig=(ec == ECH - 1 and x_ == NX - 1))
                        if x_ == NX - 1 and ec == ECH - 1:
                            ffn["t_dn_bank"][hc] = tdn
            ev_order = [6, 7, 0, 1, 2, 3, 4, 5]
            for k, hc in enumerate(ev_order):
                par = k % 2
                w = [ffn["t_dn_bank"][hc], ffn["xf_tok"][g], t_mod_done,
                     (so[par], so[par].n)]
                tev = V(lambda v, hc=hc, par=par, sl=sl, b_=b_: v.scalar_tensor_tensor(
                    SCRF[:, OST[par]:OST[par] + 512],
                    PB[hc], DERV[:, 48 + hc * 2 + b_:48 + hc * 2 + b_ + 1],
                    XT[:, sl * 4096 + hc * 512:sl * 4096 + (hc + 1) * 512],
                    ALU.mult, ALU.add), w, sig=True)
                pb_free[hc] = tev
                mk.emit("S", lambda s, hc=hc, par=par, g=g: s.dma_start(
                    out=outT[hc * 128:(hc + 1) * 128, g * TILE:(g + 1) * TILE],
                    in_=SCRF[:, OST[par]:OST[par] + 512]), [tev], dma=so[par])
            ffn["xf_used"][g] = pb_free[5]
            if g + 2 < cfg.n_ffn_tiles:
                load_xtile_ffn(g + 2)

        mk.emit("S", lambda s: s.memzero(SMALL[:, 15:16]),
                [(so[0], so[0].n), (so[1], so[1].n)])

        mk.check()
        for c in mk.ctrs:
            c.sem = st.enter_context(nc.semaphore(c.name))
        block = st.enter_context(nc.Block())
        mk.replay(block)
    return nc, dbg_names


def host_prep(inputs):
    c = np.ascontiguousarray
    x = inputs["L0_attn_hidden_states"]
    common = {
        "wqT": c(inputs["L0_attn_Wq"].T), "wkT": c(inputs["L0_attn_Wk"].T),
        "wvT": c(inputs["L0_attn_Wv"].T), "woT": c(inputs["L0_attn_Wo"].T),
        "fm1qT": c(inputs["L0_attn_fmq_w1"].T), "fm2qT": c(inputs["L0_attn_fmq_w2"].T),
        "fm1kT": c(inputs["L0_attn_fmk_w1"].T), "fm2kT": c(inputs["L0_attn_fmk_w2"].T),
        "fmq_b1": c(inputs["L0_attn_fmq_b1"].reshape(DK, 1)),
        "fmq_b2": c(inputs["L0_attn_fmq_b2"].reshape(DK, 1)),
        "fmk_b1": c(inputs["L0_attn_fmk_b1"].reshape(DK, 1)),
        "fmk_b2": c(inputs["L0_attn_fmk_b2"].reshape(DK, 1)),
        "rms_w": c(inputs["L0_attn_rms_w"].reshape(DK, 1)),
        "suT": c(inputs["L0_ffn_shared_up"].transpose(0, 2, 1)),
        "sdT": c(inputs["L0_ffn_shared_down"].transpose(0, 2, 1)),
        "ruT": c(inputs["L0_ffn_routed_up"].transpose(0, 2, 1)),
        "rdT": c(inputs["L0_ffn_routed_down"].transpose(0, 2, 1)),
        "routerT": c(inputs["L0_ffn_router_w"].T),
        "wmod": c(inputs["L0_wmod"]),
    }
    per_core = []
    for cc in range(8):
        d = dict(common)
        d["xT"] = c(x[2 * cc:2 * cc + 2].reshape(2 * TSEQ, HID).T)
        d["cT"] = c(inputs["L0_c"][2 * cc:2 * cc + 2].T)
        per_core.append(d)
    return per_core


def kernel(**inputs):
    from concourse.bass_utils import run_bass_kernel_spmd
    cfg = Cfg()
    nc, _ = build(cfg)
    in_maps = host_prep(inputs)
    res = run_bass_kernel_spmd(nc, in_maps, core_ids=list(range(8)))
    outs = []
    for cc in range(8):
        o = res.results[cc]["outT"]
        outs.append(np.ascontiguousarray(np.asarray(o).T).reshape(2, TSEQ, HID))
    return np.concatenate(outs, axis=0).astype(np.float32)


# version 13
# speedup vs baseline: 1.0409x; 1.0409x over previous
import numpy as np
import ml_dtypes
import concourse.bass as bass
import concourse.mybir as mybir
from contextlib import ExitStack

F32, BF16 = mybir.dt.float32, mybir.dt.bfloat16
AF, ALU = mybir.ActivationFunctionType, mybir.AluOpType
AX = mybir.AxisListType.X
bfd = ml_dtypes.bfloat16

HID, FC, H, DK, CH, TILE = 1024, 8, 8, 128, 128, 512
NSEQ, TSEQ = 2, 2048
CPT = TILE // CH
NAT, NFT = 8, 8
E, ECH, NX = 512, 4, 18
EPS = 1e-5
SCL = float(DK) ** -0.5
NCHS = 16

QP, KP, KTr, VTr, ONr, HBr = 0, 4096, 8192, 12288, 16384, 20480
A1SL, ATSL, O2SL, STG, XSQA = 24576, 25600, 27648, 29696, 33792
U2N = 36864
WQ, WK, WV, WO = 0, 8192, 16384, 24576
USL = [0, 4096, 24576]
DSL = [8192, 12288, 28672]
H2r = [16384, 20480]
U1N = 32768
TA = [0, 512]
TB = [1024, 1536]
RC = [2048, 2560]
OST = [3072, 3584]
OCP = [4096, 4608]
SCRFN = 5120
GE, GC, GM, GS = 0, 64, 128, 192
GATEN = 256
C_ID, C_O1024, C_O128, C_MSK = 0, 128, 256, 384
C_EPS, C_OROW, C_OK1, C_ZERO, C_ONES = 896, 1024, 1536, 1664, 1792
C_SEL = 1920
CONSTN = 4096


class Cfg:
    def __init__(self, n_attn_tiles=NAT, n_ffn_tiles=NFT, dbg=False):
        self.n_attn_tiles = n_attn_tiles
        self.n_ffn_tiles = n_ffn_tiles
        self.dbg = dbg


class Ctr:
    __slots__ = ("name", "n", "sem")
    def __init__(self, name):
        self.name, self.n, self.sem = name, 0, None


class MK:
    def __init__(self):
        self.ops = {e: [] for e in "TVSQY"}
        self.ctrs = []
        self.last_wait = {e: {} for e in "TVSQY"}
        self.ec = {e: self.ctr("ec" + e) for e in "TVSQY"}

    def ctr(self, name):
        c = Ctr(name)
        self.ctrs.append(c)
        return c

    def emit(self, eng, fn, waits=(), sig=False, dma=None, inc=None):
        ws = []
        lw = self.last_wait[eng]
        for t in waits:
            if t is None:
                continue
            c, v = t
            if v <= 0 or lw.get(c, 0) >= v:
                continue
            lw[c] = v
            ws.append((c, v))
        if inc is not None:
            inc[0].n += inc[1]
            inc = (inc[0], inc[1])
        elif dma is not None:
            dma.n += 16
            inc = (dma, 16)
        elif sig:
            c = self.ec[eng]
            c.n += 1
            inc = (c, 1)
        self.ops[eng].append((fn, ws, inc))
        if inc is not None:
            return (inc[0], inc[0].n)
        if dma is not None:
            return (dma, dma.n)
        if sig:
            return (self.ec[eng], self.ec[eng].n)
        return None

    def check(self):
        vals = {id(c): 0 for c in self.ctrs}
        pos = {e: 0 for e in self.ops}
        progress = True
        while progress:
            progress = False
            for e, lst in self.ops.items():
                while pos[e] < len(lst):
                    fn, ws, inc = lst[pos[e]]
                    if any(vals[id(c)] < v for c, v in ws):
                        break
                    if inc is not None:
                        vals[id(inc[0])] += inc[1]
                    pos[e] += 1
                    progress = True
        stuck = {e: p for e, p in pos.items() if p < len(self.ops[e])}
        if stuck:
            lines = []
            for e, p in stuck.items():
                fn, ws, inc = self.ops[e][p]
                blocked = [(c.name, v, vals[id(c)]) for c, v in ws if vals[id(c)] < v]
                lines.append(f"engine {e} op#{p}/{len(self.ops[e])} blocked on {blocked} inc={inc[0].name if inc else None}")
            for c in self.ctrs:
                if c.name in ("ecT", "ecS", "ecV", "upc", "dnc"):
                    lines.append(f"ctr {c.name}: final_n={c.n} sim_val={vals[id(c)]}")
            raise RuntimeError("schedule deadlock:\n" + "\n".join(lines))

    def replay(self, block):
        def runner(e):
            def run(eng):
                for fn, ws, inc in self.ops[e]:
                    for c, v in ws:
                        eng.wait_ge(c.sem, v)
                    ins = fn(eng)
                    if inc is not None:
                        ins.then_inc(inc[0].sem, inc[1])
            return run
        block.gpsimd(runner("Q"))
        block.sync(runner("Y"))
        block.tensor(runner("T"))
        block.vector(runner("V"))
        block.scalar(runner("S"))


def consts_arrays():
    out = np.zeros((128, CONSTN), dtype=bfd)
    out[:, C_ID:C_ID + 128] = np.eye(128, dtype=bfd)
    out[:, C_O1024:C_O1024 + 128] = np.full((128, 128), 1.0 / 1024.0, dtype=bfd)
    out[:, C_O128:C_O128 + 128] = np.full((128, 128), 1.0 / 128.0, dtype=bfd)
    mt = np.tile(np.triu(np.ones((128, 128), np.float32)), (1, 4)).astype(bfd)
    out[:, C_MSK:C_MSK + 512] = mt
    out[0:1, C_EPS:C_EPS + 128] = np.full((1, 128), EPS, np.float32).astype(bfd)
    out[0:1, C_OROW:C_OROW + 512] = np.ones((1, 512), dtype=bfd)
    out[0:1, C_OK1:C_OK1 + 128] = np.ones((1, 128), dtype=bfd)
    out[:, C_ONES:C_ONES + 128] = np.ones((128, 128), dtype=bfd)
    sel = np.zeros((16, 16 * 128), dtype=bfd)
    for e in range(16):
        sel[e, e * 128:(e + 1) * 128] = 1.0
    out[0:16, C_SEL:C_SEL + 2048] = sel
    return out


def build(cfg):
    mk = MK()
    nc = bass.Bass("TRN2", target_bir_lowering=False, debug=False,
                   detect_race_conditions=False)

    def dram_in(name, shape):
        return nc.dram_tensor(name, list(shape), F32, kind="ExternalInput").ap()

    xTh = dram_in("xT", [HID, NSEQ * TSEQ])
    cTh = dram_in("cT", [HID, NSEQ])
    wqTh, wkTh = dram_in("wqT", [HID, HID]), dram_in("wkT", [HID, HID])
    wvTh, woTh = dram_in("wvT", [HID, HID]), dram_in("woT", [HID, HID])
    fm1qh, fm2qh = dram_in("fm1qT", [DK, DK]), dram_in("fm2qT", [DK, DK])
    fm1kh, fm2kh = dram_in("fm1kT", [DK, DK]), dram_in("fm2kT", [DK, DK])
    fbq1h, fbq2h = dram_in("fmq_b1", [DK, 1]), dram_in("fmq_b2", [DK, 1])
    fbk1h, fbk2h = dram_in("fmk_b1", [DK, 1]), dram_in("fmk_b2", [DK, 1])
    rmswh = dram_in("rms_w", [DK, 1])
    suTh, sdTh = dram_in("suT", [2, HID, E]), dram_in("sdT", [2, E, HID])
    ruTh, rdTh = dram_in("ruT", [16, HID, E]), dram_in("rdT", [16, E, HID])
    rwTh = dram_in("routerT", [HID, 16])
    wmodh = dram_in("wmod", [HID, 6 * HID])

    outT = nc.dram_tensor("outT", [HID, NSEQ * TSEQ], F32, kind="ExternalOutput").ap()
    scr_kind = "ExternalOutput" if cfg.dbg else "Internal"
    wup = nc.dram_tensor("wup", [NX, 128, FC * E], BF16, kind="Internal").ap()
    wdn = nc.dram_tensor("wdn", [NX, 128, ECH * HID], BF16, kind="Internal").ap()
    xscr = nc.dram_tensor("xscr", [128, FC, NSEQ * TSEQ], BF16, kind=scr_kind).ap()
    constd = nc.inline_tensor(consts_arrays(), "constd").ap()
    idf2 = nc.inline_tensor(np.eye(2, dtype=np.float32), "idf2").ap()
    idf128 = nc.inline_tensor(np.eye(128, dtype=np.float32), "idf128").ap()
    self4 = np.zeros((4, 512), np.float32)
    for e in range(4):
        self4[e, e * 128:(e + 1) * 128] = 1.0
    self4 = nc.inline_tensor(self4, "self4").ap()
    self8 = np.zeros((8, 1024), dtype=bfd)
    for e in range(8):
        self8[e, e * 128:(e + 1) * 128] = 1.0
    self8 = nc.inline_tensor(self8, "self8").ap()

    with ExitStack() as st:
        U1 = st.enter_context(nc.sbuf_tensor("U1", [128, U1N], BF16))[:]
        U2 = st.enter_context(nc.sbuf_tensor("U2", [128, U2N], BF16))[:]
        XT = st.enter_context(nc.sbuf_tensor("XTb", [128, 2 * FC * TILE], BF16))[:]
        SCRF = st.enter_context(nc.sbuf_tensor("SCRF", [128, SCRFN], F32))[:]
        GATE = st.enter_context(nc.sbuf_tensor("GATE", [128, GATEN], F32))[:]
        CON = st.enter_context(nc.sbuf_tensor("CON", [128, CONSTN], BF16))[:]
        MODF = st.enter_context(nc.sbuf_tensor("MODF", [128, 96], F32))[:]
        DERV = st.enter_context(nc.sbuf_tensor("DERV", [128, 64], F32))[:]
        SMALL = st.enter_context(nc.sbuf_tensor("SMALL", [128, 16], F32))[:]
        CTF = st.enter_context(nc.sbuf_tensor("CTF", [128, 16], F32))[:]
        CTB = st.enter_context(nc.sbuf_tensor("CTB", [128, 16], BF16))[:]
        RT = st.enter_context(nc.sbuf_tensor("RT", [128, 128], BF16))[:]
        FMW = st.enter_context(nc.sbuf_tensor("FMW", [128, 512], BF16))[:]
        MODTMP = st.enter_context(nc.sbuf_tensor("MODTMP", [2, 512], F32))[:]
        SILF = st.enter_context(nc.sbuf_tensor("SILF", [128, 16], F32))[:]
        IDF = st.enter_context(nc.sbuf_tensor("IDF", [2, 2], F32))[:]
        IDFB = st.enter_context(nc.sbuf_tensor("IDFB", [128, 128], F32))[:]
        SBFS = st.enter_context(nc.sbuf_tensor("SBFS", [128, 2048], BF16))[:]
        GTT = st.enter_context(nc.sbuf_tensor("GTT", [16, 1024], BF16))[:]
        GBS = st.enter_context(nc.sbuf_tensor("GBS", [128, 512], BF16))[:]
        GATB = st.enter_context(nc.sbuf_tensor("GATB", [128, 64], BF16))[:]
        STATS = st.enter_context(nc.sbuf_tensor("STATS", [128, 128], F32))[:]
        ROWB = st.enter_context(nc.sbuf_tensor("ROWB", [4, 2560], F32))[:]
        RW8 = st.enter_context(nc.sbuf_tensor("RW8", [8, 128], BF16))[:]
        CONF8 = st.enter_context(nc.sbuf_tensor("CONF8", [8, 1024], BF16))[:]
        SELF4 = st.enter_context(nc.sbuf_tensor("SELF4", [4, 512], F32))[:]
        PB = [st.enter_context(nc.psum_tensor(f"PB{i}", [128, 512], F32))[:]
              for i in range(8)]

        qld = mk.ctr("qld")
        yconst = mk.ctr("yconst")
        qx = [mk.ctr("qx0"), mk.ctr("qx1")]
        yx = [mk.ctr("yx0"), mk.ctr("yx1")]
        yxp = [mk.ctr("yxp0"), mk.ctr("yxp1")]
        wmq = [mk.ctr("wmq0"), mk.ctr("wmq1")]
        qsc = [mk.ctr("qs0"), mk.ctr("qs1")]
        ysm = [mk.ctr("ys0"), mk.ctr("ys1")]
        yws = [mk.ctr(f"yws{i}") for i in range(4)]
        qws = [mk.ctr(f"qws{i}") for i in range(2)]
        upc = mk.ctr("upc")
        dnc = mk.ctr("dnc")
        so = [mk.ctr("so0"), mk.ctr("so1")]
        ydbg = mk.ctr("ydbg")

        def T(fn, waits=(), sig=False):
            return mk.emit("T", fn, waits, sig=sig)
        def V(fn, waits=(), sig=False):
            return mk.emit("V", fn, waits, sig=sig)
        def S(fn, waits=(), sig=False):
            return mk.emit("S", fn, waits, sig=sig)
        def Q(fn, waits=(), dma=None):
            return mk.emit("Q", fn, waits, dma=dma)
        def Y(fn, waits=(), dma=None):
            return mk.emit("Y", fn, waits, dma=dma)

        dbg_names = []
        def dump(name, ap, waits):
            if not cfg.dbg:
                return
            t = nc.dram_tensor("dbg_" + name, list(ap.shape), ap.dtype,
                               kind="ExternalOutput").ap()
            dbg_names.append("dbg_" + name)
            idx = tuple(slice(None) for _ in ap.shape)
            Y(lambda y, o=t, i=ap: y.dma_start(out=o[idx] if o.ndim > 1 else o, in_=i),
              waits, dma=ydbg)

        pb_free = {i: None for i in range(8)}
        xt_free = [None, None]
        lnp = {"aff": [None] * 8, "msq6": None, "msq7": None}

        Y(lambda y: y.dma_start(out=CON[:, :], in_=constd[:, :]), dma=yconst)
        Y(lambda y: y.dma_start(out=IDF[:, :], in_=idf2[:, :]), dma=yconst)
        Y(lambda y: y.dma_start(out=IDFB[:, :], in_=idf128[:, :]), dma=yconst)
        Y(lambda y: y.dma_start(out=SELF4[:, :], in_=self4[:, :]), dma=yconst)
        Y(lambda y: y.dma_start(out=CONF8[:, :], in_=self8[:, :]), dma=yconst)
        t_const = (yconst, yconst.n)

        def cast_load(dst, src):
            return Q(lambda q, d=dst, s=src: q.dma_start(out=d, in_=s), dma=qld)

        cast_load(U1[:, WQ:WQ + 8192].rearrange("p (kc d) -> p kc d", kc=8),
                  wqTh.rearrange("(kc p) d -> p kc d", p=128))
        cast_load(U1[:, WK:WK + 8192].rearrange("p (kc d) -> p kc d", kc=8),
                  wkTh.rearrange("(kc p) d -> p kc d", p=128))
        cast_load(U1[:, WV:WV + 8192].rearrange("p (kc d) -> p kc d", kc=8),
                  wvTh.rearrange("(kc p) d -> p kc d", p=128))
        cast_load(U1[:, WO:WO + 8192].rearrange("p (kc d) -> p kc d", kc=8),
                  woTh.rearrange("(kc p) d -> p kc d", p=128))
        cast_load(FMW[:, 0:128], fm1qh[:, :])
        cast_load(FMW[:, 128:256], fm2qh[:, :])
        cast_load(FMW[:, 256:384], fm1kh[:, :])
        cast_load(FMW[:, 384:512], fm2kh[:, :])
        cast_load(RT[:, 0:128].rearrange("p (kc e) -> p kc e", kc=8),
                  rwTh.rearrange("(kc p) e -> p kc e", p=128))
        for dst, src in ((SMALL[:, 0:1], fbq1h), (SMALL[:, 1:2], fbq2h),
                         (SMALL[:, 2:3], fbk1h), (SMALL[:, 3:4], fbk2h),
                         (SMALL[:, 4:5], rmswh)):
            Q(lambda q, d=dst, s=src: q.dma_start(out=d, in_=s[:, :]), dma=qld)
        Q(lambda q: q.dma_start(out=CTF[:, :].rearrange("p (ic b) -> p ic b", ic=8),
                                in_=cTh.rearrange("(ic p) b -> p ic b", p=128)), dma=qld)
        t_qld = (qld, qld.n)

        def load_xtile_attn(g):
            sl = g % 2
            dst = XT[:, sl * 4096:(sl + 1) * 4096].rearrange("p (fc t) -> p fc t", fc=8)
            src = xTh.rearrange("(fc p) t -> p fc t", p=128)[:, :, g * TILE:(g + 1) * TILE]
            wfree = xt_free[sl] if isinstance(xt_free[sl], list) else [xt_free[sl]]
            Q(lambda q, d=dst, s=src: q.dma_start(out=d, in_=s),
              wfree, dma=qx[sl])

        if cfg.n_attn_tiles > 0:
            load_xtile_attn(0)
        if cfg.n_attn_tiles > 1:
            load_xtile_attn(1)
        need_stats0 = cfg.n_attn_tiles > 0

        t_siluf = S(lambda s: s.activation(SILF[:, :], CTF[:, :], AF.Silu),
                    [t_qld], sig=True)
        t_silu = V(lambda v: v.tensor_copy(CTB[:, :], SILF[:, :]),
                   [t_siluf], sig=True)
        t_fmw = V(lambda v: v.tensor_scalar_mul(FMW[:, 0:128], FMW[:, 0:128], SCL),
                  [t_qld], sig=True)
        t_b1q = V(lambda v: v.tensor_scalar_mul(SMALL[:, 0:1], SMALL[:, 0:1], SCL),
                  [t_qld], sig=True)
        t_wo_scaled = None
        for kc in range(8):
            t_wo_scaled = S(lambda s, kc=kc: s.activation(
                U1[:, WO + kc * 1024:WO + (kc + 1) * 1024],
                U1[:, WO + kc * 1024:WO + (kc + 1) * 1024],
                AF.Identity, scale=SMALL[:, 4:5]), [t_qld], sig=True)

        slab_last = {}
        t_modf = None
        t_ev = None
        ttr_all = None
        for pc in range(12):
            isf = pc in (4, 5, 10, 11)
            par = pc % 2
            slab_id = ("F" if isf else "B") + str(par)
            src_p = wmodh.rearrange("(ic p) j -> p ic j", p=128)[
                :, :, pc * 512:(pc + 1) * 512]
            if isf:
                base = 8192 + par * 8192
                slab = U2[:, base:base + 8192].bitcast(F32).rearrange(
                    "p (ic j) -> p ic j", ic=8)
                slab_mm = U2[:, base:base + 8192].bitcast(F32)
            else:
                base = par * 4096
                slab = U2[:, base:base + 4096].rearrange("p (ic j) -> p ic j", ic=8)
                slab_mm = U2[:, base:base + 4096]
            wq_waits = [slab_last.get(slab_id)]
            Q(lambda q, d=slab, s=src_p: q.dma_start(out=d, in_=s),
              wq_waits, dma=wmq[par])
            t_wm = (wmq[par], wmq[par].n)
            tmm = None
            for ic in range(8):
                w = [t_wm, t_silu, t_siluf]
                if ic == 0 and t_ev is not None:
                    w.append(t_ev)
                tmm = T(lambda t, ic=ic, isf=isf, sm=slab_mm: t.matmul(
                    PB[6][0:2, :],
                    (SILF if isf else CTB)[:, ic * 2:ic * 2 + 2],
                    sm[:, ic * 512:(ic + 1) * 512],
                    start=(ic == 0), stop=(ic == 7), skip_group_check=True),
                    w, sig=(ic == 7))
            slab_last[slab_id] = tmm
            t_ev = S(lambda s: s.copy(MODTMP[0:2, :], PB[6][0:2, :]), [tmm], sig=True)
            for k in range(4):
                w = [t_ev, t_const]
                ttr_all = T(lambda t, k=k, pc=pc: t.transpose(
                    PB[7][:, (pc * 4 + k) * 2:(pc * 4 + k) * 2 + 2],
                    MODTMP[0:2, k * 128:(k + 1) * 128],
                    IDF[0:2, 0:2]), w, sig=(k == 3))
        t_modf = S(lambda s: s.copy(MODF[:, 0:96], PB[7][:, 0:96]),
                   [ttr_all], sig=True)
        t_d = []
        t_d.append(V(lambda v: v.tensor_scalar_add(DERV[:, 0:16], MODF[:, 16:32], 1.0),
                     [t_modf], sig=True))
        t_d.append(V(lambda v: v.tensor_scalar_add(DERV[:, 32:48], MODF[:, 64:80], 1.0),
                     [t_modf], sig=True))
        t_mod_done = None
        for (src0, dst0) in ((32, 16), (80, 48)):
            ta = S(lambda s, src0=src0: s.activation(
                GATE[:, GE:GE + 16], MODF[:, src0:src0 + 16], AF.Abs),
                [t_modf] + ([t_mod_done] if t_mod_done else []), sig=True)
            tb = V(lambda v: v.tensor_scalar_max(GATE[:, GE:GE + 16],
                                                 GATE[:, GE:GE + 16], 0.25),
                   [ta] + t_d, sig=True)
            tc_ = V(lambda v, src0=src0: v.tensor_scalar(
                GATE[:, GC:GC + 16], MODF[:, src0:src0 + 16], 0.0, None, ALU.is_ge),
                [t_modf], sig=True)
            td = V(lambda v: v.tensor_scalar(GATE[:, GC:GC + 16], GATE[:, GC:GC + 16],
                                             2.0, -1.0, ALU.mult, ALU.add),
                   [tc_], sig=True)
            t_mod_done = V(lambda v, dst0=dst0: v.tensor_tensor(
                DERV[:, dst0:dst0 + 16], GATE[:, GE:GE + 16], GATE[:, GC:GC + 16],
                ALU.mult), [tb, td], sig=True)

        dump("modf", MODF, [t_modf])
        dump("derv", DERV, [t_mod_done])

        stage_list = []
        for x in range(NX):
            src = (suTh[x] if x < 2 else ruTh[x - 2]).rearrange("(kc p) e -> p kc e", p=128)
            stage_list.append((src[:, 0:4], wup[x][:, 0:2048], 4))
            stage_list.append((src[:, 4:8], wup[x][:, 2048:4096], 4))
        for x in range(NX):
            src = (sdTh[x] if x < 2 else rdTh[x - 2]).rearrange("(ec p) h -> p ec h", p=128)
            stage_list.append((src[:, 0:2], wdn[x][:, 0:2048], 2))
            stage_list.append((src[:, 2:4], wdn[x][:, 2048:4096], 2))
        sst = {"i": 0, "wr": []}

        def emit_staging(nmax):
            cnt = 0
            while sst["i"] < len(stage_list) and cnt < nmax:
                i = sst["i"]
                src, dst, a = stage_list[i]
                par = i % 2
                slab = U2[:, STG + par * 2048:STG + (par + 1) * 2048]
                w = [sst["wr"][i - 2]] if i >= 2 else []
                Q(lambda q, d=slab, s=src, a=a: q.dma_start(
                    out=d.rearrange("p (a b) -> p a b", a=a), in_=s), w, dma=qsc[par])
                t_rd = (qsc[par], qsc[par].n)
                Y(lambda y, d=dst, s=slab: y.dma_start(out=d, in_=s),
                  [t_rd], dma=ysm[par])
                sst["wr"].append((ysm[par], ysm[par].n))
                sst["i"] += 1
                cnt += 1

        xsqwar = {0: None, 1: None}
        rowb_war = {}
        stat_toks = {}

        def emit_stats(kind, g, xt_ap, x_ready, bank_idx, extra=()):
            pb = PB[bank_idx]
            mcol = (0 if kind == "a" else 64) + g * 4
            rcol = (32 if kind == "a" else 96) + g * 4
            tz = T(lambda t: t.matmul(
                pb[:, 0:8], CON[0:1, C_ZERO:C_ZERO + 128],
                CON[0:1, C_OROW:C_OROW + 8],
                start=True, stop=False, skip_group_check=True),
                [t_const, pb_free[bank_idx]], sig=False)
            t_msqf = [None] * 8
            tmm = None
            for fc in range(8):
                for st_ in range(4):
                    w = [x_ready] if (fc == 0 and st_ == 0) else []
                    tmm = T(lambda t, st_=st_, fc=fc: t.matmul(
                        pb[:, st_:st_ + 1],
                        xt_ap[:, fc, st_ * 128:(st_ + 1) * 128],
                        CON[:, C_O1024:C_O1024 + 1],
                        start=False, stop=False, skip_group_check=True),
                        w, sig=(st_ == 3 and fc == 7))
            t_m_done = tmm
            for fc in range(8):
                sl = fc % 2
                w = [x_ready, xsqwar[sl] if fc < 2 else t_msqf[fc - 2]] + list(extra)
                t_xsq = V(lambda v, fc=fc, sl=sl: v.tensor_tensor(
                    U2[:, XSQA + sl * 512:XSQA + (sl + 1) * 512],
                    xt_ap[:, fc], xt_ap[:, fc], ALU.mult), w, sig=True)
                for st_ in range(4):
                    w = [t_xsq] if st_ == 0 else []
                    tmm = T(lambda t, st_=st_, fc=fc, sl=sl: t.matmul(
                        pb[:, 4 + st_:5 + st_],
                        U2[:, XSQA + sl * 512 + st_ * 128:XSQA + sl * 512 + (st_ + 1) * 128],
                        CON[:, C_O1024:C_O1024 + 1],
                        start=False, stop=False, skip_group_check=True),
                        w, sig=(st_ == 3))
                t_msqf[fc] = tmm
            xsqwar[0], xsqwar[1] = t_msqf[6], t_msqf[7]
            t_mc = V(lambda v, mcol=mcol: v.tensor_copy(STATS[:, mcol:mcol + 4],
                                                        pb[:, 0:4]),
                     [t_m_done, t_msqf[7]], sig=True)
            t_m2 = V(lambda v, mcol=mcol: v.tensor_tensor(
                GATE[:, GS + 16:GS + 20],
                STATS[:, mcol:mcol + 4], STATS[:, mcol:mcol + 4], ALU.mult),
                [t_mc], sig=True)
            t_vv = V(lambda v: v.scalar_tensor_tensor(
                GATE[:, GS + 20:GS + 24], pb[:, 4:8], EPS,
                GATE[:, GS + 16:GS + 20], ALU.add, ALU.subtract),
                [t_m2, t_msqf[7]], sig=True)
            t_rc = V(lambda v: v.reciprocal(GATE[:, GS + 20:GS + 24],
                                            GATE[:, GS + 20:GS + 24]), [t_vv], sig=True)
            pb_free[bank_idx] = t_rc
            t_sq = S(lambda s, rcol=rcol: s.sqrt(STATS[:, rcol:rcol + 4],
                                                 GATE[:, GS + 20:GS + 24]),
                     [t_rc], sig=True)
            rb = (g % 2) * 256 if kind == "a" else 512 + g * 256
            ttm = T(lambda t, mcol=mcol: t.transpose(
                pb[0:4, 8:136], STATS[:, mcol:mcol + 4], IDFB[:, :]),
                [t_mc, t_vv], sig=True)
            tem = S(lambda s, rb=rb: s.copy(ROWB[0:4, rb:rb + 128], pb[0:4, 8:136]),
                    [ttm, rowb_war.get((kind, g % 2))], sig=True)
            ttr = T(lambda t, rcol=rcol: t.transpose(
                pb[0:4, 136:264], STATS[:, rcol:rcol + 4], IDFB[:, :]),
                [t_sq], sig=True)
            ter = S(lambda s, rb=rb: s.copy(ROWB[0:4, rb + 128:rb + 256],
                                            pb[0:4, 136:264]), [ttr], sig=True)
            pb_free[bank_idx] = ter
            stat_toks[(kind, g)] = (tem, ter, rb)
            return tmm, ter

        def emit_apply(kind, g, xt_ap, hbuf, hbase, onep_base, sh_base, b_,
                       x_ready, pb_m, pb_m_idx, pb_v, pb_v_idx, h_war,
                       aff_on_s=False):
            tem, ter, rb = stat_toks[(kind, g)]
            tbm = None
            tbr = None
            for st_ in range(4):
                w = [tem]
                if st_ == 0:
                    w.append(pb_free[pb_m_idx])
                tbm = T(lambda t, st_=st_, rb=rb: t.matmul(
                    pb_m[:, st_ * 128:(st_ + 1) * 128],
                    SELF4[0:4, st_ * 128:(st_ + 1) * 128],
                    ROWB[0:4, rb:rb + 128],
                    start=True, stop=True, skip_group_check=True),
                    w, sig=(st_ == 3))
                w = [ter]
                if st_ == 0:
                    w.append(pb_free[pb_v_idx])
                tbr = T(lambda t, st_=st_, rb=rb: t.matmul(
                    pb_v[:, st_ * 128:(st_ + 1) * 128],
                    SELF4[0:4, st_ * 128:(st_ + 1) * 128],
                    ROWB[0:4, rb + 128:rb + 256],
                    start=True, stop=True, skip_group_check=True),
                    w, sig=(st_ == 3))
            rowb_war[(kind, g % 2)] = tbr
            t_aff = [None] * 8
            t_sub_l = [None] * 8
            t_mul_l = [None] * 8
            for fc in range(8):
                sl = fc % 2
                w = [tbm, x_ready]
                if fc >= 2:
                    w.append(t_aff[fc - 2])
                else:
                    w.append(lnp["aff"][6 + fc])
                t_sub = V(lambda v, fc=fc, sl=sl: v.tensor_tensor(
                    SCRF[:, TA[sl]:TA[sl] + 512], xt_ap[:, fc], pb_m, ALU.subtract),
                    w, sig=True)
                t_sub_l[fc] = t_sub
                t_mul = V(lambda v, fc=fc, sl=sl: v.tensor_tensor(
                    SCRF[:, TB[sl]:TB[sl] + 512], SCRF[:, TA[sl]:TA[sl] + 512],
                    pb_v, ALU.mult), [tbr, t_sub], sig=True)
                t_mul_l[fc] = t_mul
                if aff_on_s:
                    t_aff[fc] = S(lambda s, fc=fc, sl=sl: s.activation(
                        hbuf[:, hbase + fc * 512:hbase + (fc + 1) * 512],
                        SCRF[:, TB[sl]:TB[sl] + 512], AF.Identity,
                        bias=MODF[:, sh_base + fc * 2 + b_:sh_base + fc * 2 + b_ + 1],
                        scale=DERV[:, onep_base + fc * 2 + b_:onep_base + fc * 2 + b_ + 1]),
                        [t_mul, t_mod_done, h_war if fc == 0 else None], sig=True)
                else:
                    t_aff[fc] = V(lambda v, fc=fc, sl=sl: v.tensor_scalar(
                        hbuf[:, hbase + fc * 512:hbase + (fc + 1) * 512],
                        SCRF[:, TB[sl]:TB[sl] + 512],
                        DERV[:, onep_base + fc * 2 + b_:onep_base + fc * 2 + b_ + 1],
                        MODF[:, sh_base + fc * 2 + b_:sh_base + fc * 2 + b_ + 1],
                        ALU.mult, ALU.add),
                        [t_mod_done, h_war if fc == 0 else None], sig=True)
            lnp["aff"] = t_aff
            pb_free[pb_m_idx] = t_sub_l[7]
            pb_free[pb_v_idx] = t_mul_l[7]
            return t_aff

        atn = {"g": None, "t_on_last": None, "cp_tok": {}, "t_kv": {},
               "t_o": {}, "t_o_b": {}, "xpost": [None] * NAT,
               "h_last": None}
        if need_stats0:
            emit_stats("a", 0, XT[:, 0:4096].rearrange("p (fc t) -> p fc t", fc=8),
                       (qx[0], 16), 1)

        def emit_wo(gp):
            if gp is None:
                return
            sl = gp % 2
            s_ = gp // 4
            res_toks = []
            for dc in range(8):
                bidx = 2 + dc % 2
                tmm = None
                for hc in range(8):
                    w = [atn["t_on_last"], t_wo_scaled]
                    if hc == 0:
                        w.append(pb_free[bidx])
                    rhs = U2[:, ONr:ONr + 4096].rearrange(
                        "p (c h t) -> p c h t", c=4, h=8)[:, :, hc, :]
                    tmm = T(lambda t, dc=dc, hc=hc, bidx=bidx, rhs=rhs: t.matmul(
                        PB[bidx],
                        U1[:, WO + hc * 1024 + dc * 128:WO + hc * 1024 + (dc + 1) * 128],
                        rhs, start=(hc == 0), stop=(hc == 7), skip_group_check=True),
                        w, sig=(hc == 7))
                tres = V(lambda v, dc=dc, sl=sl, bidx=bidx, s_=s_: v.scalar_tensor_tensor(
                    XT[:, sl * 4096 + dc * 512:sl * 4096 + (dc + 1) * 512],
                    PB[bidx], DERV[:, 16 + dc * 2 + s_:16 + dc * 2 + s_ + 1],
                    XT[:, sl * 4096 + dc * 512:sl * 4096 + (dc + 1) * 512],
                    ALU.mult, ALU.add), [tmm, t_mod_done], sig=True)
                pb_free[bidx] = tres
                res_toks.append(tres)
                atn.setdefault("wo_toks", {}).setdefault(gp, []).append(tmm)
            Y(lambda y, sl=sl, g=gp: y.dma_start(
                out=xscr[:, :, g * TILE:(g + 1) * TILE],
                in_=XT[:, sl * 4096:(sl + 1) * 4096].rearrange("p (fc t) -> p fc t", fc=8)),
              res_toks, dma=yxp[sl])
            atn["xpost"][gp] = (yxp[sl], yxp[sl].n)
            if cfg.n_ffn_tiles > gp:
                xap = XT[:, sl * 4096:(sl + 1) * 4096].rearrange(
                    "p (fc t) -> p fc t", fc=8)
                tstat, _ = emit_stats("f", gp, xap, res_toks[-1], 1)
                xt_free[sl] = [(yxp[sl], yxp[sl].n), tstat]
            else:
                xt_free[sl] = (yxp[sl], yxp[sl].n)

        def proj_featmajor(dst_base, wbase, t_h):
            isq = dst_base == QP
            fmo = 0 if isq else 256
            bias1 = SMALL[:, 0:1] if isq else SMALL[:, 2:3]
            bias2 = SMALL[:, 1:2] if isq else SMALL[:, 3:4]
            GBKS = [4, 5, 0, 1]
            t_comb = [None] * 8
            t_ev = [None] * 8
            t_a1 = [None] * 8
            tq_last = [None] * 8

            def acc_group(grp):
                for kc in range(8):
                    for i4 in range(4):
                        dc = grp * 4 + i4
                        bidx = GBKS[i4]
                        w = [t_h[kc]]
                        if kc == 0:
                            w.append(pb_free[bidx])
                        tq = T(lambda t, dc=dc, kc=kc, bidx=bidx, wbase=wbase: t.matmul(
                            PB[bidx],
                            U1[:, wbase + kc * 1024 + dc * 128:wbase + kc * 1024 + (dc + 1) * 128],
                            U2[:, HBr + kc * 512:HBr + (kc + 1) * 512],
                            start=(kc == 0), stop=(kc == 7), skip_group_check=True),
                            w, sig=(kc == 7))
                        if kc == 7:
                            tq_last[dc] = tq

            def evac(dc):
                bidx = GBKS[dc % 4]
                t_ev[dc] = S(lambda s, dc=dc, bidx=bidx, dst_base=dst_base: s.copy(
                    U2[:, dst_base + dc * 512:dst_base + (dc + 1) * 512], PB[bidx]),
                    [tq_last[dc]], sig=True)
                pb_free[bidx] = t_ev[dc]

            acc_group(0)
            for dc in range(4):
                evac(dc)
            acc_group(1)
            for dc in range(4, 8):
                evac(dc)
            for dc in range(8):
                bidx = 2 + dc % 2
                w = [t_ev[dc], t_fmw, t_b1q,
                     pb_free[bidx] if dc < 2 else t_comb[dc - 2]]
                tp1 = T(lambda t, dc=dc, bidx=bidx, fmo=fmo, dst_base=dst_base: t.matmul(
                    PB[bidx], FMW[:, fmo:fmo + 128],
                    U2[:, dst_base + dc * 512:dst_base + (dc + 1) * 512],
                    start=True, stop=True, skip_group_check=True), w, sig=True)
                w = [tp1]
                if dc >= 2:
                    w.append(t_comb[dc - 2])
                t_a1[dc] = S(lambda s, dc=dc, bidx=bidx, b1=bias1: s.activation(
                    U2[:, A1SL + (dc % 2) * 512:A1SL + (dc % 2 + 1) * 512],
                    PB[bidx], AF.Identity, bias=b1), w, sig=True)
                tp2 = T(lambda t, dc=dc, bidx=bidx, fmo=fmo, dst_base=dst_base: t.matmul(
                    PB[bidx], FMW[:, fmo + 128:fmo + 256],
                    U2[:, dst_base + dc * 512:dst_base + (dc + 1) * 512],
                    start=True, stop=True, skip_group_check=True),
                    [t_a1[dc]], sig=True)
                t_comb[dc] = V(lambda v, dc=dc, bidx=bidx, b2=bias2, dst_base=dst_base:
                               v.scalar_tensor_tensor(
                    U2[:, dst_base + dc * 512:dst_base + (dc + 1) * 512],
                    PB[bidx], b2,
                    U2[:, A1SL + (dc % 2) * 512:A1SL + (dc % 2 + 1) * 512],
                    ALU.add, ALU.mult), [tp2], sig=True)
                pb_free[bidx] = t_comb[dc]
            return t_comb

        apply_pend = {}
        for g in range(cfg.n_attn_tiles):
            s_ = g // 4
            sl = g % 2
            base = (g % 4) * CPT
            x_ready = (qx[sl], qx[sl].n)
            xt_ap = XT[:, sl * 4096:(sl + 1) * 4096].rearrange("p (fc t) -> p fc t", fc=8)

            if g == 0:
                t_h = emit_apply("a", 0, xt_ap, U2, HBr, 0, 0, s_,
                                 x_ready, PB[2], 2, PB[3], 3, atn["h_last"],
                                 aff_on_s=True)
            else:
                t_h = apply_pend[g]

            if g == 0:
                dump("h0", U2[:, HBr:HBr + 4096], [t_h[7]])
            tq_comb = proj_featmajor(QP, WQ, t_h)
            tk_comb = proj_featmajor(KP, WK, t_h)
            if g == 0:
                dump("qp0", U2[:, QP:QP + 4096], [tq_comb[7], tq_comb[6]])
                dump("kp0", U2[:, KP:KP + 4096], [tk_comb[7], tk_comb[6]])

            t_vev = [None] * 8
            for st_ in range(4):
                for nh in range(2):
                    bidx = 4 + (st_ * 2 + nh) % 2
                    tv = None
                    for kc in range(8):
                        w = [t_h[kc]]
                        if kc == 0:
                            w.append(pb_free[bidx])
                        tv = T(lambda t, st_=st_, nh=nh, kc=kc, bidx=bidx: t.matmul(
                            PB[bidx],
                            U2[:, HBr + kc * 512 + st_ * 128:HBr + kc * 512 + (st_ + 1) * 128],
                            U1[:, WV + kc * 1024 + nh * 512:WV + kc * 1024 + (nh + 1) * 512],
                            start=(kc == 0), stop=(kc == 7), skip_group_check=True),
                            w, sig=(kc == 7))
                    t_vev[st_ * 2 + nh] = S(lambda s, st_=st_, nh=nh, bidx=bidx: s.copy(
                        U2[:, VTr + st_ * 1024 + nh * 512:VTr + st_ * 1024 + (nh + 1) * 512],
                        PB[bidx]), [tv], sig=True)
                    pb_free[bidx] = t_vev[st_ * 2 + nh]
            t_ktev = [None] * 4
            for batch in range(4):
                ttr = None
                for j in range(8):
                    hh = (batch * 8 + j) // 4
                    pr = (batch * 8 + j) % 4
                    w = [tk_comb[hh], t_const]
                    if j == 0:
                        w.append(pb_free[3])
                    ttr = T(lambda t, hh=hh, pr=pr, j=j: t.transpose(
                        PB[3].bitcast(BF16)[:, j * 128:(j + 1) * 128],
                        U2[:, KP + hh * 512 + pr * 128:KP + hh * 512 + (pr + 1) * 128],
                        CON[:, C_ID:C_ID + 128]), w, sig=(j == 7))
                t_ktev[batch] = S(lambda s, batch=batch: s.copy(
                    U2[:, KTr + batch * 1024:KTr + (batch + 1) * 1024],
                    PB[3].bitcast(BF16)[:, 0:1024]), [ttr], sig=True)
                pb_free[3] = t_ktev[batch]
            t_kt_done = t_ktev[3]
            t_v_done = t_vev[7]
            atn["h_last"] = t_v_done

            emit_wo(atn["g"])
            if g + 1 < cfg.n_attn_tiles:
                load_xtile_attn(g + 1)
            emit_staging(0 if getattr(cfg, 'no_stg_il', False) else 10)
            if g + 1 < cfg.n_attn_tiles:
                sl1 = (g + 1) % 2
                emit_stats("a", g + 1,
                           XT[:, sl1 * 4096:(sl1 + 1) * 4096].rearrange(
                               "p (fc t) -> p fc t", fc=8),
                           (qx[sl1], qx[sl1].n), 1)

            if base == 0:
                for bi in (6, 7):
                    T(lambda t, bi=bi: t.matmul(
                        PB[bi], CON[0:1, C_ZERO:C_ZERO + 128],
                        CON[0:1, C_OROW:C_OROW + 512],
                        start=True, stop=False, skip_group_check=True),
                        [t_const], sig=False)

            a_free = {2: pb_free[2], 3: pb_free[3]}
            o_free = {4: pb_free[4], 5: pb_free[5]}
            t_mask = {}
            rms = {"sq": {}, "msq_tok": {}, "small": {}, "rsb": {},
                   "bc_war": pb_free[0], "tr_war": pb_free[1],
                   "rsb_war": None, "msq_war": pb_free[1]}

            def kv_mms(c):
                n = base + c
                tkv = None
                for h in range(8):
                    w = []
                    if h == 0:
                        w = [t_kt_done, t_v_done]
                        if n % NCHS != 0:
                            w.append(atn["cp_tok"].get(n))
                    tkv = T(lambda t, c=c, h=h: t.matmul(
                        PB[6 + h // 4][:, (h % 4) * 128:(h % 4 + 1) * 128],
                        U2[:, KTr + h * 512 + c * 128:KTr + h * 512 + (c + 1) * 128],
                        U2[:, VTr + c * 1024 + h * 128:VTr + c * 1024 + (h + 1) * 128],
                        start=False, stop=False, skip_group_check=True),
                        w, sig=(h == 7))
                atn["t_kv"][n] = tkv

            def state_copy(n):
                if n % NCHS == 0:
                    return
                par = n % 2
                w = [atn["t_kv"][n - 1], atn["t_o"].get(n - 2)]
                S(lambda s, par=par: s.copy(SBFS[:, par * 1024:par * 1024 + 512], PB[6]),
                  w, sig=True)
                atn["cp_tok"][n] = S(lambda s, par=par: s.copy(
                    SBFS[:, par * 1024 + 512:par * 1024 + 1024], PB[7]), [], sig=True)

            def a_mms(c):
                toks = {}
                for h in range(8):
                    bidx = 2 + h // 4
                    w = [tq_comb[h], tk_comb[h]]
                    if h % 4 == 0:
                        w.append(a_free[bidx])
                    ta = T(lambda t, c=c, h=h, bidx=bidx: t.matmul(
                        PB[bidx][:, (h % 4) * 128:(h % 4 + 1) * 128],
                        U2[:, KP + h * 512 + c * 128:KP + h * 512 + (c + 1) * 128],
                        U2[:, QP + h * 512 + c * 128:QP + h * 512 + (c + 1) * 128],
                        start=True, stop=True, skip_group_check=True),
                        w, sig=(h == 3 or h == 7))
                    if h in (3, 7):
                        toks[bidx] = ta
                return toks

            def mask_op(c, toks):
                n = base + c
                atsl = ATSL + (c % 2) * 1024
                tms = []
                for bi in (2, 3):
                    w = [toks[bi], t_const, atn["t_o"].get(n - 2)]
                    tm = V(lambda v, bi=bi, atsl=atsl: v.tensor_tensor(
                        U2[:, atsl + (bi - 2) * 512:atsl + (bi - 1) * 512],
                        PB[bi], CON[:, C_MSK:C_MSK + 512], ALU.mult),
                        w, sig=True)
                    a_free[bi] = tm
                    tms.append(tm)
                t_mask[n] = tms

            def o_mms(c):
                n = base + c
                if n % NCHS != 0:
                    for h in range(8):
                        bidx = 4 + h // 4
                        w = [tq_comb[h]]
                        if h % 4 == 0:
                            w += [o_free[bidx], atn["cp_tok"][n]]
                        T(lambda t, c=c, h=h, n=n, bidx=bidx: t.matmul(
                            PB[bidx][:, (h % 4) * 128:(h % 4 + 1) * 128],
                            SBFS[:, (n % 2) * 1024 + h * 128:(n % 2) * 1024 + (h + 1) * 128],
                            U2[:, QP + h * 512 + c * 128:QP + h * 512 + (c + 1) * 128],
                            start=(h % 4 == 0), stop=False, skip_group_check=True),
                            w, sig=False)
                to = {}
                st_flag = (n % NCHS == 0)
                for h in range(8):
                    bidx = 4 + h // 4
                    w = [t_mask[n][h // 4]]
                    if st_flag and h % 4 == 0:
                        w.append(o_free[bidx])
                    to[bidx] = T(lambda t, c=c, h=h, bidx=bidx, st_flag=st_flag: t.matmul(
                        PB[bidx][:, (h % 4) * 128:(h % 4 + 1) * 128],
                        U2[:, VTr + c * 1024 + h * 128:VTr + c * 1024 + (h + 1) * 128],
                        U2[:, ATSL + (c % 2) * 1024 + h * 128:ATSL + (c % 2) * 1024 + (h + 1) * 128],
                        start=(st_flag and h % 4 == 0), stop=True, skip_group_check=True),
                        w, sig=(h == 3 or h == 7))
                atn["t_o"][n] = to[5]
                atn["t_o_b"][n] = (to[4], to[5])

            def o_evac(c):
                n = base + c
                onb = ONr + c * 1024
                o2b = O2SL + (c % 2) * 1024
                t4, t5 = atn["t_o_b"][n]
                cp4 = V(lambda v, onb=onb: v.tensor_copy(U2[:, onb:onb + 512], PB[4]),
                        [t4], sig=True)
                cp5 = V(lambda v, onb=onb: v.tensor_copy(
                    U2[:, onb + 512:onb + 1024], PB[5]), [t5], sig=True)
                sq4 = V(lambda v, onb=onb, o2b=o2b: v.tensor_tensor(
                    U2[:, o2b:o2b + 512], U2[:, onb:onb + 512], U2[:, onb:onb + 512],
                    ALU.mult), [cp4, rms["msq_tok"].get(c - 2)], sig=True)
                sq5 = V(lambda v, onb=onb, o2b=o2b: v.tensor_tensor(
                    U2[:, o2b + 512:o2b + 1024], U2[:, onb + 512:onb + 1024],
                    U2[:, onb + 512:onb + 1024], ALU.mult), [cp5], sig=True)
                o_free[4] = cp4
                o_free[5] = cp5
                rms["sq"][c] = (sq4, sq5)

            def rms_msq(c):
                o2b = O2SL + (c % 2) * 1024
                sq4, sq5 = rms["sq"][c]
                mc = 300 + (c % 2) * 8
                tq = None
                for h in range(8):
                    w = [sq4 if h < 4 else sq5]
                    if h == 0:
                        w.append(rms["msq_war"])
                    tq = T(lambda t, h=h, mc=mc, o2b=o2b: t.matmul(
                        PB[1][:, mc + h:mc + h + 1],
                        U2[:, o2b + h * 128:o2b + (h + 1) * 128],
                        CON[:, C_O128:C_O128 + 1],
                        start=True, stop=True, skip_group_check=True),
                        w, sig=(h == 7))
                rms["msq_tok"][c] = tq

            def rms_small(c):
                mc = 300 + (c % 2) * 8
                gc_ = GS + 24 + (c % 2) * 8
                t1 = V(lambda v, mc=mc, gc_=gc_: v.tensor_scalar_add(
                    GATE[:, gc_:gc_ + 8], PB[1][:, mc:mc + 8], EPS),
                    [rms["msq_tok"][c]], sig=True)
                t2 = V(lambda v, gc_=gc_: v.reciprocal(
                    GATE[:, gc_:gc_ + 8], GATE[:, gc_:gc_ + 8]), [t1], sig=True)
                t3 = S(lambda s, gc_=gc_: s.sqrt(
                    GATE[:, gc_:gc_ + 8], GATE[:, gc_:gc_ + 8]), [t2], sig=True)
                rms["small"][c] = t3
                rms["msq_war"] = t1

            def rms_tr(c):
                gc_ = GS + 24 + (c % 2) * 8
                ttr = T(lambda t, gc_=gc_: t.transpose(
                    PB[1][0:8, 384:512], GATE[:, gc_:gc_ + 8], IDFB[:, :]),
                    [rms["small"][c], rms["tr_war"], t_const], sig=True)
                trsb = S(lambda s: s.copy(RW8[0:8, 0:128],
                                          PB[1][0:8, 384:512]),
                         [ttr, rms["rsb_war"]], sig=True)
                rms["rsb"][c] = trsb
                rms["tr_war"] = trsb

            def rms_bcast(c):
                trsb = rms["rsb"][c]
                last_mm = None
                for r in range(2):
                    mm = None
                    for hh in range(4):
                        h = r * 4 + hh
                        w = [trsb]
                        if hh == 0:
                            w.append(rms["bc_war"])
                        mm = T(lambda t, h=h, hh=hh: t.matmul(
                            PB[0][:, hh * 128:(hh + 1) * 128],
                            CONF8[0:8, h * 128:(h + 1) * 128],
                            RW8[0:8, 0:128],
                            start=True, stop=True, skip_group_check=True),
                            w, sig=(hh == 3))
                    ton = V(lambda v, c=c, r=r: v.tensor_tensor(
                        U2[:, ONr + c * 1024 + r * 512:ONr + c * 1024 + (r + 1) * 512],
                        U2[:, ONr + c * 1024 + r * 512:ONr + c * 1024 + (r + 1) * 512],
                        PB[0], ALU.mult), [mm], sig=True)
                    rms["bc_war"] = ton
                    last_mm = mm
                rms["rsb_war"] = last_mm
                atn["t_on_last"] = rms["bc_war"]

            for c in range(CPT):
                kv_mms(c)
                toks = a_mms(c)
                state_copy(base + c + 1)
                mask_op(c, toks)
                if c >= 2:
                    rms_tr(c - 2)
                o_mms(c)
                if c >= 2:
                    rms_bcast(c - 2)
                o_evac(c)
                if c >= 1:
                    rms_msq(c - 1)
                    rms_small(c - 1)
            rms_msq(CPT - 1)
            rms_small(CPT - 1)
            rms_tr(CPT - 2)
            rms_bcast(CPT - 2)
            rms_tr(CPT - 1)

            pb_free[2] = a_free[2]
            pb_free[3] = a_free[3]
            pb_free[4] = o_free[4]
            pb_free[5] = o_free[5]

            if g + 1 < cfg.n_attn_tiles:
                sl1 = (g + 1) % 2
                apply_pend[g + 1] = emit_apply(
                    "a", g + 1,
                    XT[:, sl1 * 4096:(sl1 + 1) * 4096].rearrange(
                        "p (fc t) -> p fc t", fc=8),
                    U2, HBr, 0, 0, (g + 1) // 4, (qx[sl1], qx[sl1].n),
                    PB[2], 2, PB[3], 3, atn["h_last"], aff_on_s=True)

            rms_bcast(CPT - 1)
            pb_free[0] = rms["bc_war"]
            pb_free[1] = rms["rsb"][CPT - 1]
            if g == 0:
                dump("vt0", U2[:, VTr:VTr + 4096], [t_v_done])
                dump("on0", U2[:, ONr:ONr + 4096], [atn["t_on_last"]])

            atn["g"] = g

        emit_wo(atn["g"])
        emit_staging(100)

        t_attn_T = (mk.ec["T"], mk.ec["T"].n)
        t_attn_V = (mk.ec["V"], mk.ec["V"].n)
        t_attn_S = (mk.ec["S"], mk.ec["S"].n)
        stg_done = sst["wr"][-2:] if sst["wr"] else []

        if cfg.n_ffn_tiles > 0:
            T(lambda t: t.ldweights(CON[:, C_ID:C_ID + 128]),
              [t_attn_V, t_attn_S], sig=True)
            V(lambda v: v.memset(GATE[:, GS + 16:GS + 17], 0.0),
              [t_attn_T, t_attn_S], sig=True)
            S(lambda s: s.memzero(SMALL[:, 14:15]),
              [t_attn_T, t_attn_V], sig=True)

        ffn = {"t_h2": {}, "t_gt": {}, "gt_used": {}, "t_dn_bank": {},
               "ws_free": {0: None, 1: None, 2: None, 3: None},
               "xf_tok": {}, "xf_used": {}, "act_ready": {}, "h2_used": {}}

        def load_xtile_ffn(g):
            sl = g % 2
            dst = XT[:, sl * 4096:(sl + 1) * 4096].rearrange("p (fc t) -> p fc t", fc=8)
            w = [atn["xpost"][g] if g < len(atn["xpost"]) else None,
                 ffn["xf_used"].get(g - 2)]
            w += xt_free[sl] if isinstance(xt_free[sl], list) else [xt_free[sl]]
            if getattr(cfg, "xload_y", False):
                Y(lambda y, d=dst, g=g: y.dma_start(
                    out=d, in_=xscr[:, :, g * TILE:(g + 1) * TILE]), w, dma=yx[sl])
            else:
                Q(lambda q, d=dst, g=g: q.dma_start(
                    out=d, in_=xscr[:, :, g * TILE:(g + 1) * TILE]), w, dma=yx[sl])
            ffn["xf_tok"][g] = (yx[sl], yx[sl].n)

        def emit_lnb(g):
            if g >= cfg.n_ffn_tiles:
                return
            sl = g % 2
            b_ = g // 4
            xt_ap = XT[:, sl * 4096:(sl + 1) * 4096].rearrange("p (fc t) -> p fc t", fc=8)
            ffn["t_h2"][g] = emit_apply(
                "f", g, xt_ap, U1, H2r[sl], 32, 48, b_, ffn["xf_tok"][g],
                PB[0], 0, PB[2], 2, ffn["h2_used"].get(g - 2))

        def emit_router(g):
            if g >= cfg.n_ffn_tiles:
                return
            sl = g % 2
            tl = None
            for st_ in range(4):
                for kc in range(8):
                    w = [ffn["t_h2"][g][kc], t_qld]
                    if st_ == 0 and kc == 0:
                        w.append(pb_free[3])
                    tl = T(lambda t, st_=st_, kc=kc, sl=sl: t.matmul(
                        PB[3][:, st_ * 16:(st_ + 1) * 16],
                        U1[:, H2r[sl] + kc * 512 + st_ * 128:H2r[sl] + kc * 512 + (st_ + 1) * 128],
                        RT[:, kc * 16:(kc + 1) * 16],
                        start=(kc == 0), stop=(kc == 7), skip_group_check=True),
                        w, sig=(st_ == 3 and kc == 7))
            lseg = PB[3][:, 0:64].rearrange("p (gg n) -> p gg n", gg=4)
            t1 = V(lambda v: v.reduce_max(GATE[:, GS:GS + 4], lseg, axis=AX),
                   [tl], sig=True)
            eseg = GATE[:, GE:GE + 64].rearrange("p (gg n) -> p gg n", gg=4)
            t2 = V(lambda v: v.tensor_tensor(
                eseg, lseg, GATE[:, GS:GS + 4].unsqueeze(2).broadcast_to([128, 4, 16]),
                ALU.subtract), [t1], sig=True)
            pb_free[3] = t2
            t3 = S(lambda s: s.activation(GATE[:, GE:GE + 64], GATE[:, GE:GE + 64],
                                          AF.Exp), [t2], sig=True)
            t4 = V(lambda v: v.reduce_sum(GATE[:, GS + 4:GS + 8], eseg, axis=AX),
                   [t3], sig=True)
            t5 = V(lambda v: v.reciprocal(GATE[:, GS + 8:GS + 12],
                                          GATE[:, GS + 4:GS + 8]), [t4], sig=True)
            tcur = V(lambda v: v.tensor_copy(GATE[:, GC:GC + 64], GATE[:, GE:GE + 64]),
                     [t3], sig=True)
            cseg = GATE[:, GC:GC + 64].rearrange("p (gg n) -> p gg n", gg=4)
            mseg = GATE[:, GM:GM + 64].rearrange("p (gg n) -> p gg n", gg=4)
            for r in range(4):
                tcur = V(lambda v: v.reduce_max(GATE[:, GS + 12:GS + 16], cseg, axis=AX),
                         [tcur], sig=True)
                if r < 3:
                    tm = V(lambda v: v.tensor_tensor(
                        mseg, cseg,
                        GATE[:, GS + 12:GS + 16].unsqueeze(2).broadcast_to([128, 4, 16]),
                        ALU.is_ge), [tcur], sig=True)
                    tcur = V(lambda v: v.scalar_tensor_tensor(
                        GATE[:, GC:GC + 64], GATE[:, GM:GM + 64], -1e9,
                        GATE[:, GC:GC + 64], ALU.mult, ALU.add), [tm], sig=True)
            tg1 = V(lambda v: v.tensor_tensor(
                mseg, eseg,
                GATE[:, GS + 12:GS + 16].unsqueeze(2).broadcast_to([128, 4, 16]),
                ALU.is_ge), [tcur], sig=True)
            tg2 = V(lambda v: v.tensor_tensor(GATE[:, GM:GM + 64], GATE[:, GM:GM + 64],
                                              GATE[:, GE:GE + 64], ALU.mult),
                    [tg1], sig=True)
            gseg = GATE[:, GE:GE + 64].rearrange("p (gg n) -> p gg n", gg=4)
            tg3 = V(lambda v: v.tensor_tensor(
                gseg, mseg,
                GATE[:, GS + 8:GS + 12].unsqueeze(2).broadcast_to([128, 4, 16]),
                ALU.mult), [tg2, t5], sig=True)
            tg4 = V(lambda v: v.tensor_copy(GATB[:, :], GATE[:, GE:GE + 64]),
                    [tg3], sig=True)
            ffn["tg4"] = (tg4, g)

        def emit_router_gt(g):
            if g >= cfg.n_ffn_tiles:
                return
            sl = g % 2
            tg4, g_chk = ffn["tg4"]
            assert g_chk == g
            ttr = None
            for st_ in range(4):
                w = [tg4, t_const]
                if st_ == 0:
                    w.append(pb_free[5])
                    w.append(ffn["gt_used"].get(g - 2))
                ttr = T(lambda t, st_=st_: t.transpose(
                    PB[5].bitcast(BF16)[0:16, st_ * 128:(st_ + 1) * 128],
                    GATB[:, st_ * 16:(st_ + 1) * 16],
                    CON[:, C_ID:C_ID + 128]), w, sig=(st_ == 3))
            tge = S(lambda s, sl=sl: s.copy(GTT[0:16, sl * 512:(sl + 1) * 512],
                                            PB[5].bitcast(BF16)[0:16, 0:512]),
                    [ttr], sig=True)
            pb_free[5] = tge
            ffn["t_gt"][g] = tge

        def stream_w(x_, g, which, base):
            i = x_ % 3
            src = (wup if which == "up" else wdn)[x_]
            ctr = upc if which == "up" else dnc
            w = [(ctr, base + x_ - 2) if base + x_ - 2 > 0 else None]
            if g == 0:
                stg_idx = 2 * x_ + 1 if which == "up" else 2 * NX + 2 * x_ + 1
                w.append(sst["wr"][stg_idx] if stg_idx < len(sst["wr"]) else None)
                if x_ < 3:
                    w += [t_attn_T, t_attn_V, t_attn_S]
            if which == "up":
                Y(lambda y, d=U1[:, USL[i]:USL[i] + 4096], s=src: y.dma_start(
                    out=d, in_=s[:, :]), w, dma=yws[i])
                return (yws[i], yws[i].n)
            if i < 2:
                q_ = qws[i]
                Q(lambda q, d=U1[:, DSL[i]:DSL[i] + 4096], s=src: q.dma_start(
                    out=d, in_=s[:, :]), w, dma=q_)
                return (q_, q_.n)
            Y(lambda y, d=U1[:, DSL[2]:DSL[2] + 4096], s=src: y.dma_start(
                out=d, in_=s[:, :]), w, dma=yws[3])
            return (yws[3], yws[3].n)

        for g in range(cfg.n_ffn_tiles):
            sl = g % 2
            b_ = g // 4
            if g == 0:
                load_xtile_ffn(0)
                if cfg.n_ffn_tiles > 1:
                    load_xtile_ffn(1)
                emit_lnb(0)
                dump("h2t0", U1[:, H2r[0]:H2r[0] + 4096], [ffn["t_h2"][0][7]])
                emit_router(0)
                emit_router_gt(0)
            up_base, dn_base = upc.n, dnc.n
            lazy = getattr(cfg, "lazy_streams", False)
            if not lazy:
                up_toks = [stream_w(x_, g, "up", up_base) for x_ in range(NX)]
                dn_toks = [stream_w(x_, g, "dn", dn_base) for x_ in range(NX)]
            BUP = (6, 7, 1)
            for x_ in range(NX):
                t_ws = stream_w(x_, g, "up", up_base) if lazy else up_toks[x_]
                tup_last = None
                gel = [None] * ECH
                for ec in range(ECH):
                    bidx = BUP[(x_ * ECH + ec) % 3]
                    tup = None
                    for kc in range(8):
                        w = [ffn["t_h2"][g][kc]]
                        if kc == 0:
                            w += [t_ws, pb_free[bidx]]
                        if kc == 7 and ec == ECH - 1:
                            tup = mk.emit("T", lambda t, x_=x_, ec=ec, kc=kc, bidx=bidx,
                                          i=x_ % 3, sl=sl: t.matmul(
                                PB[bidx],
                                U1[:, USL[i] + kc * 512 + ec * 128:USL[i] + kc * 512 + (ec + 1) * 128],
                                U1[:, H2r[sl] + kc * 512:H2r[sl] + (kc + 1) * 512],
                                start=(kc == 0), stop=(kc == 7), skip_group_check=True),
                                w, inc=(upc, 1))
                        else:
                            tup = T(lambda t, x_=x_, ec=ec, kc=kc, bidx=bidx, i=x_ % 3, sl=sl:
                                    t.matmul(
                                PB[bidx],
                                U1[:, USL[i] + kc * 512 + ec * 128:USL[i] + kc * 512 + (ec + 1) * 128],
                                U1[:, H2r[sl] + kc * 512:H2r[sl] + (kc + 1) * 512],
                                start=(kc == 0), stop=(kc == 7), skip_group_check=True),
                                w, sig=(kc == 7))
                    gel[ec] = S(lambda s, x_=x_, ec=ec, bidx=bidx: s.activation(
                        U2[:, x_ * 2048 + ec * 512:x_ * 2048 + (ec + 1) * 512],
                        PB[bidx], AF.Gelu), [tup], sig=True)
                    pb_free[bidx] = gel[ec]
                    tup_last = tup
                if x_ >= 2:
                    e_idx = x_ - 2
                    tbc = T(lambda t, e_idx=e_idx, sl=sl: t.matmul(
                        PB[4], CON[0:16, C_SEL + e_idx * 128:C_SEL + (e_idx + 1) * 128],
                        GTT[0:16, sl * 512:(sl + 1) * 512],
                        start=True, stop=True, skip_group_check=True),
                        [ffn["t_gt"][g], pb_free[4], t_const], sig=True)
                    ffn["gt_used"][g] = tbc
                    tgb = S(lambda s: s.copy(GBS[:, :], PB[4]),
                            [tbc, ffn["act_ready"].get(x_ - 1)], sig=True)
                    if g == 0 and x_ == 2:
                        dump("gbs0", GBS, [tgb])
                        dump("gtt0", GTT[0:16, 0:512], [ffn["t_gt"][g]])
                    pb_free[4] = tgb
                    tmul = None
                    for ec in range(ECH):
                        tmul = V(lambda v, x_=x_, ec=ec: v.tensor_tensor(
                            U2[:, x_ * 2048 + ec * 512:x_ * 2048 + (ec + 1) * 512],
                            GBS[:, :],
                            U2[:, x_ * 2048 + ec * 512:x_ * 2048 + (ec + 1) * 512],
                            ALU.mult), [gel[ec], tgb], sig=(ec == ECH - 1))
                    ffn["act_ready"][x_] = tmul
                    if g == 0 and x_ == 2:
                        dump("act2", U2[:, 2 * 2048:3 * 2048], [tmul])
                else:
                    ffn["act_ready"][x_] = gel[ECH - 1]
                    if g == 0 and x_ == 0:
                        dump("act0", U2[:, 0:2048], [gel[ECH - 1]])
                if x_ == 3:
                    emit_lnb(g + 1)
                if x_ == 11:
                    emit_router(g + 1)
                if x_ == 15:
                    emit_router_gt(g + 1)
            ffn["h2_used"][g] = (upc, upc.n)
            for x_ in range(NX):
                t_ws = stream_w(x_, g, "dn", dn_base) if lazy else dn_toks[x_]
                tdn = None
                for hc in range(8):
                    for ec in range(ECH):
                        w = []
                        if ec == 0:
                            w = [t_ws, ffn["act_ready"][x_]]
                            if x_ == 0:
                                w.append(pb_free[hc])
                        if hc == 7 and ec == ECH - 1:
                            tdn = mk.emit("T", lambda t, x_=x_, hc=hc, ec=ec, i=x_ % 3:
                                          t.matmul(
                                PB[hc],
                                U1[:, DSL[i] + ec * 1024 + hc * 128:DSL[i] + ec * 1024 + (hc + 1) * 128],
                                U2[:, x_ * 2048 + ec * 512:x_ * 2048 + (ec + 1) * 512],
                                start=(x_ == 0 and ec == 0),
                                stop=(x_ == NX - 1 and ec == ECH - 1),
                                skip_group_check=True), w, inc=(dnc, 1))
                        else:
                            tdn = T(lambda t, x_=x_, hc=hc, ec=ec, i=x_ % 3: t.matmul(
                                PB[hc],
                                U1[:, DSL[i] + ec * 1024 + hc * 128:DSL[i] + ec * 1024 + (hc + 1) * 128],
                                U2[:, x_ * 2048 + ec * 512:x_ * 2048 + (ec + 1) * 512],
                                start=(x_ == 0 and ec == 0),
                                stop=(x_ == NX - 1 and ec == ECH - 1),
                                skip_group_check=True),
                                w, sig=(ec == ECH - 1 and x_ == NX - 1))
                        if x_ == NX - 1 and ec == ECH - 1:
                            ffn["t_dn_bank"][hc] = tdn
            ev_order = [6, 7, 0, 1, 2, 3, 4, 5]
            for k, hc in enumerate(ev_order):
                par = k % 2
                w = [ffn["t_dn_bank"][hc], ffn["xf_tok"][g], t_mod_done,
                     (so[par], so[par].n)]
                tev = V(lambda v, hc=hc, par=par, sl=sl, b_=b_: v.scalar_tensor_tensor(
                    SCRF[:, OST[par]:OST[par] + 512],
                    PB[hc], DERV[:, 48 + hc * 2 + b_:48 + hc * 2 + b_ + 1],
                    XT[:, sl * 4096 + hc * 512:sl * 4096 + (hc + 1) * 512],
                    ALU.mult, ALU.add), w, sig=True)
                pb_free[hc] = tev
                mk.emit("S", lambda s, hc=hc, par=par, g=g: s.dma_start(
                    out=outT[hc * 128:(hc + 1) * 128, g * TILE:(g + 1) * TILE],
                    in_=SCRF[:, OST[par]:OST[par] + 512]), [tev], dma=so[par])
            ffn["xf_used"][g] = pb_free[5]
            if g + 2 < cfg.n_ffn_tiles:
                load_xtile_ffn(g + 2)

        mk.emit("S", lambda s: s.memzero(SMALL[:, 15:16]),
                [(so[0], so[0].n), (so[1], so[1].n)])

        mk.check()
        for c in mk.ctrs:
            c.sem = st.enter_context(nc.semaphore(c.name))
        block = st.enter_context(nc.Block())
        mk.replay(block)
    return nc, dbg_names


def host_prep(inputs):
    c = np.ascontiguousarray
    x = inputs["L0_attn_hidden_states"]
    common = {
        "wqT": c(inputs["L0_attn_Wq"].T), "wkT": c(inputs["L0_attn_Wk"].T),
        "wvT": c(inputs["L0_attn_Wv"].T), "woT": c(inputs["L0_attn_Wo"].T),
        "fm1qT": c(inputs["L0_attn_fmq_w1"].T), "fm2qT": c(inputs["L0_attn_fmq_w2"].T),
        "fm1kT": c(inputs["L0_attn_fmk_w1"].T), "fm2kT": c(inputs["L0_attn_fmk_w2"].T),
        "fmq_b1": c(inputs["L0_attn_fmq_b1"].reshape(DK, 1)),
        "fmq_b2": c(inputs["L0_attn_fmq_b2"].reshape(DK, 1)),
        "fmk_b1": c(inputs["L0_attn_fmk_b1"].reshape(DK, 1)),
        "fmk_b2": c(inputs["L0_attn_fmk_b2"].reshape(DK, 1)),
        "rms_w": c(inputs["L0_attn_rms_w"].reshape(DK, 1)),
        "suT": c(inputs["L0_ffn_shared_up"].transpose(0, 2, 1)),
        "sdT": c(inputs["L0_ffn_shared_down"].transpose(0, 2, 1)),
        "ruT": c(inputs["L0_ffn_routed_up"].transpose(0, 2, 1)),
        "rdT": c(inputs["L0_ffn_routed_down"].transpose(0, 2, 1)),
        "routerT": c(inputs["L0_ffn_router_w"].T),
        "wmod": c(inputs["L0_wmod"]),
    }
    per_core = []
    for cc in range(8):
        d = dict(common)
        d["xT"] = c(x[2 * cc:2 * cc + 2].reshape(2 * TSEQ, HID).T)
        d["cT"] = c(inputs["L0_c"][2 * cc:2 * cc + 2].T)
        per_core.append(d)
    return per_core


def kernel(**inputs):
    from concourse.bass_utils import run_bass_kernel_spmd
    cfg = Cfg()
    nc, _ = build(cfg)
    in_maps = host_prep(inputs)
    res = run_bass_kernel_spmd(nc, in_maps, core_ids=list(range(8)))
    outs = []
    for cc in range(8):
        o = res.results[cc]["outT"]
        outs.append(np.ascontiguousarray(np.asarray(o).T).reshape(2, TSEQ, HID))
    return np.concatenate(outs, axis=0).astype(np.float32)


# version 14
# speedup vs baseline: 1.0452x; 1.0042x over previous
import numpy as np
import ml_dtypes
import concourse.bass as bass
import concourse.mybir as mybir
from contextlib import ExitStack

F32, BF16 = mybir.dt.float32, mybir.dt.bfloat16
AF, ALU = mybir.ActivationFunctionType, mybir.AluOpType
AX = mybir.AxisListType.X
bfd = ml_dtypes.bfloat16

HID, FC, H, DK, CH, TILE = 1024, 8, 8, 128, 128, 512
NSEQ, TSEQ = 2, 2048
CPT = TILE // CH
NAT, NFT = 8, 8
E, ECH, NX = 512, 4, 18
EPS = 1e-5
SCL = float(DK) ** -0.5
NCHS = 16

QP, KP, KTr, VTr, ONr, HBr = 0, 4096, 8192, 12288, 16384, 20480
A1SL, ATSL, O2SL, STG, XSQA = 24576, 25600, 27648, 29696, 33792
U2N = 36864
WQ, WK, WV, WO = 0, 8192, 16384, 24576
USL = [0, 4096, 24576]
DSL = [8192, 12288, 28672]
H2r = [16384, 20480]
U1N = 32768
TA = [0, 512]
TB = [1024, 1536]
RC = [2048, 2560]
OST = [3072, 3584]
OCP = [4096, 4608]
SCRFN = 5120
GE, GC, GM, GS = 0, 64, 128, 192
GATEN = 256
C_ID, C_O1024, C_O128, C_MSK = 0, 128, 256, 384
C_EPS, C_OROW, C_OK1, C_ZERO, C_ONES = 896, 1024, 1536, 1664, 1792
C_SEL = 1920
CONSTN = 4096


class Cfg:
    def __init__(self, n_attn_tiles=NAT, n_ffn_tiles=NFT, dbg=False):
        self.n_attn_tiles = n_attn_tiles
        self.n_ffn_tiles = n_ffn_tiles
        self.dbg = dbg


class Ctr:
    __slots__ = ("name", "n", "sem")
    def __init__(self, name):
        self.name, self.n, self.sem = name, 0, None


class MK:
    def __init__(self):
        self.ops = {e: [] for e in "TVSQY"}
        self.ctrs = []
        self.last_wait = {e: {} for e in "TVSQY"}
        self.ec = {e: self.ctr("ec" + e) for e in "TVSQY"}

    def ctr(self, name):
        c = Ctr(name)
        self.ctrs.append(c)
        return c

    def emit(self, eng, fn, waits=(), sig=False, dma=None, inc=None):
        ws = []
        lw = self.last_wait[eng]
        for t in waits:
            if t is None:
                continue
            c, v = t
            if v <= 0 or lw.get(c, 0) >= v:
                continue
            lw[c] = v
            ws.append((c, v))
        if inc is not None:
            inc[0].n += inc[1]
            inc = (inc[0], inc[1])
        elif dma is not None:
            dma.n += 16
            inc = (dma, 16)
        elif sig:
            c = self.ec[eng]
            c.n += 1
            inc = (c, 1)
        self.ops[eng].append((fn, ws, inc))
        if inc is not None:
            return (inc[0], inc[0].n)
        if dma is not None:
            return (dma, dma.n)
        if sig:
            return (self.ec[eng], self.ec[eng].n)
        return None

    def check(self):
        vals = {id(c): 0 for c in self.ctrs}
        pos = {e: 0 for e in self.ops}
        progress = True
        while progress:
            progress = False
            for e, lst in self.ops.items():
                while pos[e] < len(lst):
                    fn, ws, inc = lst[pos[e]]
                    if any(vals[id(c)] < v for c, v in ws):
                        break
                    if inc is not None:
                        vals[id(inc[0])] += inc[1]
                    pos[e] += 1
                    progress = True
        stuck = {e: p for e, p in pos.items() if p < len(self.ops[e])}
        if stuck:
            lines = []
            for e, p in stuck.items():
                fn, ws, inc = self.ops[e][p]
                blocked = [(c.name, v, vals[id(c)]) for c, v in ws if vals[id(c)] < v]
                lines.append(f"engine {e} op#{p}/{len(self.ops[e])} blocked on {blocked} inc={inc[0].name if inc else None}")
            for c in self.ctrs:
                if c.name in ("ecT", "ecS", "ecV", "upc", "dnc"):
                    lines.append(f"ctr {c.name}: final_n={c.n} sim_val={vals[id(c)]}")
            raise RuntimeError("schedule deadlock:\n" + "\n".join(lines))

    def replay(self, block):
        def runner(e):
            def run(eng):
                for fn, ws, inc in self.ops[e]:
                    for c, v in ws:
                        eng.wait_ge(c.sem, v)
                    ins = fn(eng)
                    if inc is not None:
                        ins.then_inc(inc[0].sem, inc[1])
            return run
        block.gpsimd(runner("Q"))
        block.sync(runner("Y"))
        block.tensor(runner("T"))
        block.vector(runner("V"))
        block.scalar(runner("S"))


def consts_arrays():
    out = np.zeros((128, CONSTN), dtype=bfd)
    out[:, C_ID:C_ID + 128] = np.eye(128, dtype=bfd)
    out[:, C_O1024:C_O1024 + 128] = np.full((128, 128), 1.0 / 1024.0, dtype=bfd)
    out[:, C_O128:C_O128 + 128] = np.full((128, 128), 1.0 / 128.0, dtype=bfd)
    mt = np.tile(np.triu(np.ones((128, 128), np.float32)), (1, 4)).astype(bfd)
    out[:, C_MSK:C_MSK + 512] = mt
    out[0:1, C_EPS:C_EPS + 128] = np.full((1, 128), EPS, np.float32).astype(bfd)
    out[0:1, C_OROW:C_OROW + 512] = np.ones((1, 512), dtype=bfd)
    out[0:1, C_OK1:C_OK1 + 128] = np.ones((1, 128), dtype=bfd)
    out[:, C_ONES:C_ONES + 128] = np.ones((128, 128), dtype=bfd)
    sel = np.zeros((16, 16 * 128), dtype=bfd)
    for e in range(16):
        sel[e, e * 128:(e + 1) * 128] = 1.0
    out[0:16, C_SEL:C_SEL + 2048] = sel
    return out


def build(cfg):
    mk = MK()
    nc = bass.Bass("TRN2", target_bir_lowering=False, debug=False,
                   detect_race_conditions=False)

    def dram_in(name, shape):
        return nc.dram_tensor(name, list(shape), F32, kind="ExternalInput").ap()

    xTh = dram_in("xT", [HID, NSEQ * TSEQ])
    cTh = dram_in("cT", [HID, NSEQ])
    wqTh, wkTh = dram_in("wqT", [HID, HID]), dram_in("wkT", [HID, HID])
    wvTh, woTh = dram_in("wvT", [HID, HID]), dram_in("woT", [HID, HID])
    fm1qh, fm2qh = dram_in("fm1qT", [DK, DK]), dram_in("fm2qT", [DK, DK])
    fm1kh, fm2kh = dram_in("fm1kT", [DK, DK]), dram_in("fm2kT", [DK, DK])
    fbq1h, fbq2h = dram_in("fmq_b1", [DK, 1]), dram_in("fmq_b2", [DK, 1])
    fbk1h, fbk2h = dram_in("fmk_b1", [DK, 1]), dram_in("fmk_b2", [DK, 1])
    rmswh = dram_in("rms_w", [DK, 1])
    suTh, sdTh = dram_in("suT", [2, HID, E]), dram_in("sdT", [2, E, HID])
    ruTh, rdTh = dram_in("ruT", [16, HID, E]), dram_in("rdT", [16, E, HID])
    rwTh = dram_in("routerT", [HID, 16])
    wmodh = dram_in("wmod", [HID, 6 * HID])

    outT = nc.dram_tensor("outT", [HID, NSEQ * TSEQ], F32, kind="ExternalOutput").ap()
    scr_kind = "ExternalOutput" if cfg.dbg else "Internal"
    wup = nc.dram_tensor("wup", [NX, 128, FC * E], BF16, kind="Internal").ap()
    wdn = nc.dram_tensor("wdn", [NX, 128, ECH * HID], BF16, kind="Internal").ap()
    xscr = nc.dram_tensor("xscr", [128, FC, NSEQ * TSEQ], BF16, kind=scr_kind).ap()
    constd = nc.inline_tensor(consts_arrays(), "constd").ap()
    idf2 = nc.inline_tensor(np.eye(2, dtype=np.float32), "idf2").ap()
    idf128 = nc.inline_tensor(np.eye(128, dtype=np.float32), "idf128").ap()
    self4 = np.zeros((4, 512), np.float32)
    for e in range(4):
        self4[e, e * 128:(e + 1) * 128] = 1.0
    self4 = nc.inline_tensor(self4, "self4").ap()
    self8 = np.zeros((8, 1024), dtype=bfd)
    for e in range(8):
        self8[e, e * 128:(e + 1) * 128] = 1.0
    self8 = nc.inline_tensor(self8, "self8").ap()

    with ExitStack() as st:
        U1 = st.enter_context(nc.sbuf_tensor("U1", [128, U1N], BF16))[:]
        U2 = st.enter_context(nc.sbuf_tensor("U2", [128, U2N], BF16))[:]
        XT = st.enter_context(nc.sbuf_tensor("XTb", [128, 2 * FC * TILE], BF16))[:]
        SCRF = st.enter_context(nc.sbuf_tensor("SCRF", [128, SCRFN], F32))[:]
        GATE = st.enter_context(nc.sbuf_tensor("GATE", [128, GATEN], F32))[:]
        CON = st.enter_context(nc.sbuf_tensor("CON", [128, CONSTN], BF16))[:]
        MODF = st.enter_context(nc.sbuf_tensor("MODF", [128, 96], F32))[:]
        DERV = st.enter_context(nc.sbuf_tensor("DERV", [128, 64], F32))[:]
        SMALL = st.enter_context(nc.sbuf_tensor("SMALL", [128, 16], F32))[:]
        CTF = st.enter_context(nc.sbuf_tensor("CTF", [128, 16], F32))[:]
        CTB = st.enter_context(nc.sbuf_tensor("CTB", [128, 16], BF16))[:]
        RT = st.enter_context(nc.sbuf_tensor("RT", [128, 128], BF16))[:]
        FMW = st.enter_context(nc.sbuf_tensor("FMW", [128, 512], BF16))[:]
        MODTMP = st.enter_context(nc.sbuf_tensor("MODTMP", [2, 512], F32))[:]
        SILF = st.enter_context(nc.sbuf_tensor("SILF", [128, 16], F32))[:]
        IDF = st.enter_context(nc.sbuf_tensor("IDF", [2, 2], F32))[:]
        IDFB = st.enter_context(nc.sbuf_tensor("IDFB", [128, 128], F32))[:]
        SBFS = st.enter_context(nc.sbuf_tensor("SBFS", [128, 2048], BF16))[:]
        GTT = st.enter_context(nc.sbuf_tensor("GTT", [16, 1024], BF16))[:]
        GBS = st.enter_context(nc.sbuf_tensor("GBS", [128, 512], BF16))[:]
        GATB = st.enter_context(nc.sbuf_tensor("GATB", [128, 64], BF16))[:]
        STATS = st.enter_context(nc.sbuf_tensor("STATS", [128, 128], F32))[:]
        ROWB = st.enter_context(nc.sbuf_tensor("ROWB", [4, 2560], F32))[:]
        RW8 = st.enter_context(nc.sbuf_tensor("RW8", [8, 128], BF16))[:]
        CONF8 = st.enter_context(nc.sbuf_tensor("CONF8", [8, 1024], BF16))[:]
        SELF4 = st.enter_context(nc.sbuf_tensor("SELF4", [4, 512], F32))[:]
        PB = [st.enter_context(nc.psum_tensor(f"PB{i}", [128, 512], F32))[:]
              for i in range(8)]

        qld = mk.ctr("qld")
        yconst = mk.ctr("yconst")
        qx = [mk.ctr("qx0"), mk.ctr("qx1")]
        yx = [mk.ctr("yx0"), mk.ctr("yx1")]
        yxp = [mk.ctr("yxp0"), mk.ctr("yxp1")]
        wmq = [mk.ctr("wmq0"), mk.ctr("wmq1")]
        qsc = [mk.ctr("qs0"), mk.ctr("qs1")]
        ysm = [mk.ctr("ys0"), mk.ctr("ys1")]
        yws = [mk.ctr(f"yws{i}") for i in range(4)]
        qws = [mk.ctr(f"qws{i}") for i in range(2)]
        upc = mk.ctr("upc")
        dnc = mk.ctr("dnc")
        so = [mk.ctr("so0"), mk.ctr("so1")]
        ydbg = mk.ctr("ydbg")

        def T(fn, waits=(), sig=False):
            return mk.emit("T", fn, waits, sig=sig)
        def V(fn, waits=(), sig=False):
            return mk.emit("V", fn, waits, sig=sig)
        def S(fn, waits=(), sig=False):
            return mk.emit("S", fn, waits, sig=sig)
        def Q(fn, waits=(), dma=None):
            return mk.emit("Q", fn, waits, dma=dma)
        def Y(fn, waits=(), dma=None):
            return mk.emit("Y", fn, waits, dma=dma)

        dbg_names = []
        def dump(name, ap, waits):
            if not cfg.dbg:
                return
            t = nc.dram_tensor("dbg_" + name, list(ap.shape), ap.dtype,
                               kind="ExternalOutput").ap()
            dbg_names.append("dbg_" + name)
            idx = tuple(slice(None) for _ in ap.shape)
            Y(lambda y, o=t, i=ap: y.dma_start(out=o[idx] if o.ndim > 1 else o, in_=i),
              waits, dma=ydbg)

        pb_free = {i: None for i in range(8)}
        xt_free = [None, None]
        lnp = {"aff": [None] * 8, "msq6": None, "msq7": None}

        Y(lambda y: y.dma_start(out=CON[:, :], in_=constd[:, :]), dma=yconst)
        Y(lambda y: y.dma_start(out=IDF[:, :], in_=idf2[:, :]), dma=yconst)
        Y(lambda y: y.dma_start(out=IDFB[:, :], in_=idf128[:, :]), dma=yconst)
        Y(lambda y: y.dma_start(out=SELF4[:, :], in_=self4[:, :]), dma=yconst)
        Y(lambda y: y.dma_start(out=CONF8[:, :], in_=self8[:, :]), dma=yconst)
        t_const = (yconst, yconst.n)

        def cast_load(dst, src):
            return Q(lambda q, d=dst, s=src: q.dma_start(out=d, in_=s), dma=qld)

        cast_load(U1[:, WQ:WQ + 8192].rearrange("p (kc d) -> p kc d", kc=8),
                  wqTh.rearrange("(kc p) d -> p kc d", p=128))
        cast_load(U1[:, WK:WK + 8192].rearrange("p (kc d) -> p kc d", kc=8),
                  wkTh.rearrange("(kc p) d -> p kc d", p=128))
        cast_load(U1[:, WV:WV + 8192].rearrange("p (kc d) -> p kc d", kc=8),
                  wvTh.rearrange("(kc p) d -> p kc d", p=128))
        cast_load(U1[:, WO:WO + 8192].rearrange("p (kc d) -> p kc d", kc=8),
                  woTh.rearrange("(kc p) d -> p kc d", p=128))
        cast_load(FMW[:, 0:128], fm1qh[:, :])
        cast_load(FMW[:, 128:256], fm2qh[:, :])
        cast_load(FMW[:, 256:384], fm1kh[:, :])
        cast_load(FMW[:, 384:512], fm2kh[:, :])
        cast_load(RT[:, 0:128].rearrange("p (kc e) -> p kc e", kc=8),
                  rwTh.rearrange("(kc p) e -> p kc e", p=128))
        for dst, src in ((SMALL[:, 0:1], fbq1h), (SMALL[:, 1:2], fbq2h),
                         (SMALL[:, 2:3], fbk1h), (SMALL[:, 3:4], fbk2h),
                         (SMALL[:, 4:5], rmswh)):
            Q(lambda q, d=dst, s=src: q.dma_start(out=d, in_=s[:, :]), dma=qld)
        Q(lambda q: q.dma_start(out=CTF[:, :].rearrange("p (ic b) -> p ic b", ic=8),
                                in_=cTh.rearrange("(ic p) b -> p ic b", p=128)), dma=qld)
        t_qld = (qld, qld.n)

        def load_xtile_attn(g):
            sl = g % 2
            dst = XT[:, sl * 4096:(sl + 1) * 4096].rearrange("p (fc t) -> p fc t", fc=8)
            src = xTh.rearrange("(fc p) t -> p fc t", p=128)[:, :, g * TILE:(g + 1) * TILE]
            wfree = xt_free[sl] if isinstance(xt_free[sl], list) else [xt_free[sl]]
            Q(lambda q, d=dst, s=src: q.dma_start(out=d, in_=s),
              wfree, dma=qx[sl])

        if cfg.n_attn_tiles > 0:
            load_xtile_attn(0)
        if cfg.n_attn_tiles > 1:
            load_xtile_attn(1)
        need_stats0 = cfg.n_attn_tiles > 0

        t_siluf = S(lambda s: s.activation(SILF[:, :], CTF[:, :], AF.Silu),
                    [t_qld], sig=True)
        t_silu = V(lambda v: v.tensor_copy(CTB[:, :], SILF[:, :]),
                   [t_siluf], sig=True)
        t_fmw = V(lambda v: v.tensor_scalar_mul(FMW[:, 0:128], FMW[:, 0:128], SCL),
                  [t_qld], sig=True)
        t_b1q = V(lambda v: v.tensor_scalar_mul(SMALL[:, 0:1], SMALL[:, 0:1], SCL),
                  [t_qld], sig=True)
        t_wo_scaled = None
        for kc in range(8):
            t_wo_scaled = S(lambda s, kc=kc: s.activation(
                U1[:, WO + kc * 1024:WO + (kc + 1) * 1024],
                U1[:, WO + kc * 1024:WO + (kc + 1) * 1024],
                AF.Identity, scale=SMALL[:, 4:5]), [t_qld], sig=True)

        slab_last = {}
        t_modf = None
        t_ev = None
        ttr_all = None
        for pc in range(12):
            isf = pc in (4, 5, 10, 11)
            par = pc % 2
            slab_id = ("F" if isf else "B") + str(par)
            src_p = wmodh.rearrange("(ic p) j -> p ic j", p=128)[
                :, :, pc * 512:(pc + 1) * 512]
            if isf:
                base = 8192 + par * 8192
                slab = U2[:, base:base + 8192].bitcast(F32).rearrange(
                    "p (ic j) -> p ic j", ic=8)
                slab_mm = U2[:, base:base + 8192].bitcast(F32)
            else:
                base = par * 4096
                slab = U2[:, base:base + 4096].rearrange("p (ic j) -> p ic j", ic=8)
                slab_mm = U2[:, base:base + 4096]
            wq_waits = [slab_last.get(slab_id)]
            Q(lambda q, d=slab, s=src_p: q.dma_start(out=d, in_=s),
              wq_waits, dma=wmq[par])
            t_wm = (wmq[par], wmq[par].n)
            tmm = None
            for ic in range(8):
                w = [t_wm, t_silu, t_siluf]
                if ic == 0 and t_ev is not None:
                    w.append(t_ev)
                tmm = T(lambda t, ic=ic, isf=isf, sm=slab_mm: t.matmul(
                    PB[6][0:2, :],
                    (SILF if isf else CTB)[:, ic * 2:ic * 2 + 2],
                    sm[:, ic * 512:(ic + 1) * 512],
                    start=(ic == 0), stop=(ic == 7), skip_group_check=True),
                    w, sig=(ic == 7))
            slab_last[slab_id] = tmm
            t_ev = S(lambda s: s.copy(MODTMP[0:2, :], PB[6][0:2, :]), [tmm], sig=True)
            for k in range(4):
                w = [t_ev, t_const]
                ttr_all = T(lambda t, k=k, pc=pc: t.transpose(
                    PB[7][:, (pc * 4 + k) * 2:(pc * 4 + k) * 2 + 2],
                    MODTMP[0:2, k * 128:(k + 1) * 128],
                    IDF[0:2, 0:2]), w, sig=(k == 3))
        t_modf = S(lambda s: s.copy(MODF[:, 0:96], PB[7][:, 0:96]),
                   [ttr_all], sig=True)
        t_d = []
        t_d.append(V(lambda v: v.tensor_scalar_add(DERV[:, 0:16], MODF[:, 16:32], 1.0),
                     [t_modf], sig=True))
        t_d.append(V(lambda v: v.tensor_scalar_add(DERV[:, 32:48], MODF[:, 64:80], 1.0),
                     [t_modf], sig=True))
        t_mod_done = None
        for (src0, dst0) in ((32, 16), (80, 48)):
            ta = S(lambda s, src0=src0: s.activation(
                GATE[:, GE:GE + 16], MODF[:, src0:src0 + 16], AF.Abs),
                [t_modf] + ([t_mod_done] if t_mod_done else []), sig=True)
            tb = V(lambda v: v.tensor_scalar_max(GATE[:, GE:GE + 16],
                                                 GATE[:, GE:GE + 16], 0.25),
                   [ta] + t_d, sig=True)
            tc_ = V(lambda v, src0=src0: v.tensor_scalar(
                GATE[:, GC:GC + 16], MODF[:, src0:src0 + 16], 0.0, None, ALU.is_ge),
                [t_modf], sig=True)
            td = V(lambda v: v.tensor_scalar(GATE[:, GC:GC + 16], GATE[:, GC:GC + 16],
                                             2.0, -1.0, ALU.mult, ALU.add),
                   [tc_], sig=True)
            t_mod_done = V(lambda v, dst0=dst0: v.tensor_tensor(
                DERV[:, dst0:dst0 + 16], GATE[:, GE:GE + 16], GATE[:, GC:GC + 16],
                ALU.mult), [tb, td], sig=True)

        dump("modf", MODF, [t_modf])
        dump("derv", DERV, [t_mod_done])

        stage_list = []
        for x in range(NX):
            src = (suTh[x] if x < 2 else ruTh[x - 2]).rearrange("(kc p) e -> p kc e", p=128)
            stage_list.append((src[:, 0:4], wup[x][:, 0:2048], 4))
            stage_list.append((src[:, 4:8], wup[x][:, 2048:4096], 4))
        for x in range(NX):
            src = (sdTh[x] if x < 2 else rdTh[x - 2]).rearrange("(ec p) h -> p ec h", p=128)
            stage_list.append((src[:, 0:2], wdn[x][:, 0:2048], 2))
            stage_list.append((src[:, 2:4], wdn[x][:, 2048:4096], 2))
        sst = {"i": 0, "wr": []}

        def emit_staging(nmax):
            cnt = 0
            while sst["i"] < len(stage_list) and cnt < nmax:
                i = sst["i"]
                src, dst, a = stage_list[i]
                par = i % 2
                slab = U2[:, STG + par * 2048:STG + (par + 1) * 2048]
                w = [sst["wr"][i - 2]] if i >= 2 else []
                Q(lambda q, d=slab, s=src, a=a: q.dma_start(
                    out=d.rearrange("p (a b) -> p a b", a=a), in_=s), w, dma=qsc[par])
                t_rd = (qsc[par], qsc[par].n)
                Y(lambda y, d=dst, s=slab: y.dma_start(out=d, in_=s),
                  [t_rd], dma=ysm[par])
                sst["wr"].append((ysm[par], ysm[par].n))
                sst["i"] += 1
                cnt += 1

        xsqwar = {0: None, 1: None}
        rowb_war = {}
        stat_toks = {}

        def emit_stats(kind, g, xt_ap, x_ready, bank_idx, extra=()):
            pb = PB[bank_idx]
            mcol = (0 if kind == "a" else 64) + g * 4
            rcol = (32 if kind == "a" else 96) + g * 4
            tz = T(lambda t: t.matmul(
                pb[:, 0:8], CON[0:1, C_ZERO:C_ZERO + 128],
                CON[0:1, C_OROW:C_OROW + 8],
                start=True, stop=False, skip_group_check=True),
                [t_const, pb_free[bank_idx]], sig=False)
            t_msqf = [None] * 8
            tmm = None
            for fc in range(8):
                for st_ in range(4):
                    w = [x_ready] if (fc == 0 and st_ == 0) else []
                    tmm = T(lambda t, st_=st_, fc=fc: t.matmul(
                        pb[:, st_:st_ + 1],
                        xt_ap[:, fc, st_ * 128:(st_ + 1) * 128],
                        CON[:, C_O1024:C_O1024 + 1],
                        start=False, stop=False, skip_group_check=True),
                        w, sig=(st_ == 3 and fc == 7))
            t_m_done = tmm
            for fc in range(8):
                sl = fc % 2
                w = [x_ready, xsqwar[sl] if fc < 2 else t_msqf[fc - 2]] + list(extra)
                t_xsq = V(lambda v, fc=fc, sl=sl: v.tensor_tensor(
                    U2[:, XSQA + sl * 512:XSQA + (sl + 1) * 512],
                    xt_ap[:, fc], xt_ap[:, fc], ALU.mult), w, sig=True)
                for st_ in range(4):
                    w = [t_xsq] if st_ == 0 else []
                    tmm = T(lambda t, st_=st_, fc=fc, sl=sl: t.matmul(
                        pb[:, 4 + st_:5 + st_],
                        U2[:, XSQA + sl * 512 + st_ * 128:XSQA + sl * 512 + (st_ + 1) * 128],
                        CON[:, C_O1024:C_O1024 + 1],
                        start=False, stop=False, skip_group_check=True),
                        w, sig=(st_ == 3))
                t_msqf[fc] = tmm
            xsqwar[0], xsqwar[1] = t_msqf[6], t_msqf[7]
            t_mc = V(lambda v, mcol=mcol: v.tensor_copy(STATS[:, mcol:mcol + 4],
                                                        pb[:, 0:4]),
                     [t_m_done, t_msqf[7]], sig=True)
            t_m2 = V(lambda v, mcol=mcol: v.tensor_tensor(
                GATE[:, GS + 16:GS + 20],
                STATS[:, mcol:mcol + 4], STATS[:, mcol:mcol + 4], ALU.mult),
                [t_mc], sig=True)
            t_vv = V(lambda v: v.scalar_tensor_tensor(
                GATE[:, GS + 20:GS + 24], pb[:, 4:8], EPS,
                GATE[:, GS + 16:GS + 20], ALU.add, ALU.subtract),
                [t_m2, t_msqf[7]], sig=True)
            t_rc = V(lambda v: v.reciprocal(GATE[:, GS + 20:GS + 24],
                                            GATE[:, GS + 20:GS + 24]), [t_vv], sig=True)
            pb_free[bank_idx] = t_rc
            t_sq = S(lambda s, rcol=rcol: s.sqrt(STATS[:, rcol:rcol + 4],
                                                 GATE[:, GS + 20:GS + 24]),
                     [t_rc], sig=True)
            rb = (g % 2) * 256 if kind == "a" else 512 + g * 256
            ttm = T(lambda t, mcol=mcol: t.transpose(
                pb[0:4, 8:136], STATS[:, mcol:mcol + 4], IDFB[:, :]),
                [t_mc, t_vv], sig=True)
            tem = S(lambda s, rb=rb: s.copy(ROWB[0:4, rb:rb + 128], pb[0:4, 8:136]),
                    [ttm, rowb_war.get((kind, g % 2))], sig=True)
            ttr = T(lambda t, rcol=rcol: t.transpose(
                pb[0:4, 136:264], STATS[:, rcol:rcol + 4], IDFB[:, :]),
                [t_sq], sig=True)
            ter = S(lambda s, rb=rb: s.copy(ROWB[0:4, rb + 128:rb + 256],
                                            pb[0:4, 136:264]), [ttr], sig=True)
            pb_free[bank_idx] = ter
            stat_toks[(kind, g)] = (tem, ter, rb)
            return tmm, ter

        def emit_apply(kind, g, xt_ap, hbuf, hbase, onep_base, sh_base, b_,
                       x_ready, pb_m, pb_m_idx, pb_v, pb_v_idx, h_war,
                       aff_on_s=False):
            tem, ter, rb = stat_toks[(kind, g)]
            tbm = None
            tbr = None
            for st_ in range(4):
                w = [tem]
                if st_ == 0:
                    w.append(pb_free[pb_m_idx])
                tbm = T(lambda t, st_=st_, rb=rb: t.matmul(
                    pb_m[:, st_ * 128:(st_ + 1) * 128],
                    SELF4[0:4, st_ * 128:(st_ + 1) * 128],
                    ROWB[0:4, rb:rb + 128],
                    start=True, stop=True, skip_group_check=True),
                    w, sig=(st_ == 3))
                w = [ter]
                if st_ == 0:
                    w.append(pb_free[pb_v_idx])
                tbr = T(lambda t, st_=st_, rb=rb: t.matmul(
                    pb_v[:, st_ * 128:(st_ + 1) * 128],
                    SELF4[0:4, st_ * 128:(st_ + 1) * 128],
                    ROWB[0:4, rb + 128:rb + 256],
                    start=True, stop=True, skip_group_check=True),
                    w, sig=(st_ == 3))
            rowb_war[(kind, g % 2)] = tbr
            t_aff = [None] * 8
            t_sub_l = [None] * 8
            t_mul_l = [None] * 8
            for fc in range(8):
                sl = fc % 2
                w = [tbm, x_ready]
                if fc >= 2:
                    w.append(t_aff[fc - 2])
                else:
                    w.append(lnp["aff"][6 + fc])
                t_sub = V(lambda v, fc=fc, sl=sl: v.tensor_tensor(
                    SCRF[:, TA[sl]:TA[sl] + 512], xt_ap[:, fc], pb_m, ALU.subtract),
                    w, sig=True)
                t_sub_l[fc] = t_sub
                t_mul = V(lambda v, fc=fc, sl=sl: v.tensor_tensor(
                    SCRF[:, TB[sl]:TB[sl] + 512], SCRF[:, TA[sl]:TA[sl] + 512],
                    pb_v, ALU.mult), [tbr, t_sub], sig=True)
                t_mul_l[fc] = t_mul
                if aff_on_s:
                    t_aff[fc] = S(lambda s, fc=fc, sl=sl: s.activation(
                        hbuf[:, hbase + fc * 512:hbase + (fc + 1) * 512],
                        SCRF[:, TB[sl]:TB[sl] + 512], AF.Identity,
                        bias=MODF[:, sh_base + fc * 2 + b_:sh_base + fc * 2 + b_ + 1],
                        scale=DERV[:, onep_base + fc * 2 + b_:onep_base + fc * 2 + b_ + 1]),
                        [t_mul, t_mod_done, h_war if fc == 0 else None], sig=True)
                else:
                    t_aff[fc] = V(lambda v, fc=fc, sl=sl: v.tensor_scalar(
                        hbuf[:, hbase + fc * 512:hbase + (fc + 1) * 512],
                        SCRF[:, TB[sl]:TB[sl] + 512],
                        DERV[:, onep_base + fc * 2 + b_:onep_base + fc * 2 + b_ + 1],
                        MODF[:, sh_base + fc * 2 + b_:sh_base + fc * 2 + b_ + 1],
                        ALU.mult, ALU.add),
                        [t_mod_done, h_war if fc == 0 else None], sig=True)
            lnp["aff"] = t_aff
            pb_free[pb_m_idx] = t_sub_l[7]
            pb_free[pb_v_idx] = t_mul_l[7]
            return t_aff

        atn = {"g": None, "t_on_last": None, "cp_tok": {}, "t_kv": {},
               "t_o": {}, "t_o_b": {}, "xpost": [None] * NAT,
               "h_last": None}
        if need_stats0:
            emit_stats("a", 0, XT[:, 0:4096].rearrange("p (fc t) -> p fc t", fc=8),
                       (qx[0], 16), 1)

        def emit_wo_dc(gp, dc, res_toks):
            if gp is None:
                return
            sl = gp % 2
            s_ = gp // 4
            bidx = dc % 2
            tmm = None
            for hc in range(8):
                w = [atn["t_on_last"], t_wo_scaled]
                if hc == 0:
                    w.append(pb_free[bidx])
                rhs = U2[:, ONr:ONr + 4096].rearrange(
                    "p (c h t) -> p c h t", c=4, h=8)[:, :, hc, :]
                tmm = T(lambda t, dc=dc, hc=hc, bidx=bidx, rhs=rhs: t.matmul(
                    PB[bidx],
                    U1[:, WO + hc * 1024 + dc * 128:WO + hc * 1024 + (dc + 1) * 128],
                    rhs, start=(hc == 0), stop=(hc == 7), skip_group_check=True),
                    w, sig=(hc == 7))
            tres = V(lambda v, dc=dc, sl=sl, bidx=bidx, s_=s_: v.scalar_tensor_tensor(
                XT[:, sl * 4096 + dc * 512:sl * 4096 + (dc + 1) * 512],
                PB[bidx], DERV[:, 16 + dc * 2 + s_:16 + dc * 2 + s_ + 1],
                XT[:, sl * 4096 + dc * 512:sl * 4096 + (dc + 1) * 512],
                ALU.mult, ALU.add), [tmm, t_mod_done], sig=True)
            pb_free[bidx] = tres
            res_toks.append(tres)

        def emit_wo_fin(gp, res_toks):
            if gp is None:
                return
            sl = gp % 2
            Y(lambda y, sl=sl, g=gp: y.dma_start(
                out=xscr[:, :, g * TILE:(g + 1) * TILE],
                in_=XT[:, sl * 4096:(sl + 1) * 4096].rearrange("p (fc t) -> p fc t", fc=8)),
              res_toks, dma=yxp[sl])
            atn["xpost"][gp] = (yxp[sl], yxp[sl].n)
            if cfg.n_ffn_tiles > gp:
                xap = XT[:, sl * 4096:(sl + 1) * 4096].rearrange(
                    "p (fc t) -> p fc t", fc=8)
                tstat, _ = emit_stats("f", gp, xap, res_toks[-1], 1)
                xt_free[sl] = [(yxp[sl], yxp[sl].n), tstat]
            else:
                xt_free[sl] = (yxp[sl], yxp[sl].n)

        GBKS = [4, 5, 0, 1]

        def proj_acc(dst_base, wbase, t_h):
            t_ev = [None] * 8
            tq_last = [None] * 8

            def acc_group(grp):
                for kc in range(8):
                    for i4 in range(4):
                        dc = grp * 4 + i4
                        bidx = GBKS[i4]
                        w = [t_h[kc]]
                        if kc == 0:
                            w.append(pb_free[bidx])
                        tq = T(lambda t, dc=dc, kc=kc, bidx=bidx, wbase=wbase: t.matmul(
                            PB[bidx],
                            U1[:, wbase + kc * 1024 + dc * 128:wbase + kc * 1024 + (dc + 1) * 128],
                            U2[:, HBr + kc * 512:HBr + (kc + 1) * 512],
                            start=(kc == 0), stop=(kc == 7), skip_group_check=True),
                            w, sig=(kc == 7))
                        if kc == 7:
                            tq_last[dc] = tq

            def evac(dc):
                bidx = GBKS[dc % 4]
                t_ev[dc] = S(lambda s, dc=dc, bidx=bidx, dst_base=dst_base: s.copy(
                    U2[:, dst_base + dc * 512:dst_base + (dc + 1) * 512], PB[bidx]),
                    [tq_last[dc]], sig=True)
                pb_free[bidx] = t_ev[dc]

            acc_group(0)
            for dc in range(4):
                evac(dc)
            acc_group(1)
            for dc in range(4, 8):
                evac(dc)
            return t_ev

        fmst = {"last2": None, "last3": None}

        def fm_chain(dst_base, t_ev, t_comb, t_a1, dc):
            isq = dst_base == QP
            fmo = 0 if isq else 256
            bias1 = SMALL[:, 0:1] if isq else SMALL[:, 2:3]
            bias2 = SMALL[:, 1:2] if isq else SMALL[:, 3:4]
            if True:
                bidx = 2 + dc % 2
                w = [t_ev[dc], t_fmw, t_b1q, pb_free[bidx]]
                tp1 = T(lambda t, dc=dc, bidx=bidx, fmo=fmo, dst_base=dst_base: t.matmul(
                    PB[bidx], FMW[:, fmo:fmo + 128],
                    U2[:, dst_base + dc * 512:dst_base + (dc + 1) * 512],
                    start=True, stop=True, skip_group_check=True), w, sig=True)
                w = [tp1, fmst["a1w" + str(dc % 2)] if ("a1w" + str(dc % 2)) in fmst else None]
                t_a1[dc] = S(lambda s, dc=dc, bidx=bidx, b1=bias1: s.activation(
                    U2[:, A1SL + (dc % 2) * 512:A1SL + (dc % 2 + 1) * 512],
                    PB[bidx], AF.Identity, bias=b1), w, sig=True)
                tp2 = T(lambda t, dc=dc, bidx=bidx, fmo=fmo, dst_base=dst_base: t.matmul(
                    PB[bidx], FMW[:, fmo + 128:fmo + 256],
                    U2[:, dst_base + dc * 512:dst_base + (dc + 1) * 512],
                    start=True, stop=True, skip_group_check=True),
                    [t_a1[dc]], sig=True)
                t_comb[dc] = V(lambda v, dc=dc, bidx=bidx, b2=bias2, dst_base=dst_base:
                               v.scalar_tensor_tensor(
                    U2[:, dst_base + dc * 512:dst_base + (dc + 1) * 512],
                    PB[bidx], b2,
                    U2[:, A1SL + (dc % 2) * 512:A1SL + (dc % 2 + 1) * 512],
                    ALU.add, ALU.mult), [tp2], sig=True)
                pb_free[bidx] = t_comb[dc]
                fmst["a1w" + str(dc % 2)] = t_comb[dc]

        apply_pend = {}
        for g in range(cfg.n_attn_tiles):
            s_ = g // 4
            sl = g % 2
            base = (g % 4) * CPT
            x_ready = (qx[sl], qx[sl].n)
            xt_ap = XT[:, sl * 4096:(sl + 1) * 4096].rearrange("p (fc t) -> p fc t", fc=8)

            if g == 0:
                t_h = emit_apply("a", 0, xt_ap, U2, HBr, 0, 0, s_,
                                 x_ready, PB[2], 2, PB[3], 3, atn["h_last"],
                                 aff_on_s=True)
            else:
                t_h = apply_pend[g]

            if g == 0:
                dump("h0", U2[:, HBr:HBr + 4096], [t_h[7]])
            q_ev = proj_acc(QP, WQ, t_h)
            k_ev = proj_acc(KP, WK, t_h)
            tq_comb = [None] * 8
            tk_comb = [None] * 8
            tq_a1 = [None] * 8
            tk_a1 = [None] * 8
            t_vev = [None] * 8
            t_ktev = [None] * 4

            def vproj_block(i):
                st_, nh = i // 2, i % 2
                bidx = 4 + i % 2
                tv = None
                for kc in range(8):
                    w = [t_h[kc]]
                    if kc == 0:
                        w.append(pb_free[bidx])
                    tv = T(lambda t, st_=st_, nh=nh, kc=kc, bidx=bidx: t.matmul(
                        PB[bidx],
                        U2[:, HBr + kc * 512 + st_ * 128:HBr + kc * 512 + (st_ + 1) * 128],
                        U1[:, WV + kc * 1024 + nh * 512:WV + kc * 1024 + (nh + 1) * 512],
                        start=(kc == 0), stop=(kc == 7), skip_group_check=True),
                        w, sig=(kc == 7))
                t_vev[i] = S(lambda s, st_=st_, nh=nh, bidx=bidx: s.copy(
                    U2[:, VTr + st_ * 1024 + nh * 512:VTr + st_ * 1024 + (nh + 1) * 512],
                    PB[bidx]), [tv], sig=True)
                pb_free[bidx] = t_vev[i]

            def kt_batch(batch):
                ttr = None
                for j in range(8):
                    hh = (batch * 8 + j) // 4
                    pr = (batch * 8 + j) % 4
                    w = [tk_comb[hh], t_const]
                    if j == 0:
                        w.append(pb_free[3])
                    ttr = T(lambda t, hh=hh, pr=pr, j=j: t.transpose(
                        PB[3].bitcast(BF16)[:, j * 128:(j + 1) * 128],
                        U2[:, KP + hh * 512 + pr * 128:KP + hh * 512 + (pr + 1) * 128],
                        CON[:, C_ID:C_ID + 128]), w, sig=(j == 7))
                t_ktev[batch] = S(lambda s, batch=batch: s.copy(
                    U2[:, KTr + batch * 1024:KTr + (batch + 1) * 1024],
                    PB[3].bitcast(BF16)[:, 0:1024]), [ttr], sig=True)
                pb_free[3] = t_ktev[batch]

            for dc in range(8):
                fm_chain(QP, q_ev, tq_comb, tq_a1, dc)
                vproj_block(dc)
            wo_res = []
            for dc in range(8):
                fm_chain(KP, k_ev, tk_comb, tk_a1, dc)
                if dc % 2 == 1:
                    kt_batch(dc // 2)
                emit_wo_dc(atn["g"], dc, wo_res)
            emit_wo_fin(atn["g"], wo_res)
            t_kt_done = t_ktev[3]
            t_v_done = t_vev[7]
            atn["h_last"] = t_v_done

            if g + 1 < cfg.n_attn_tiles:
                load_xtile_attn(g + 1)
            emit_staging(0 if getattr(cfg, 'no_stg_il', False) else 10)
            if g + 1 < cfg.n_attn_tiles:
                sl1 = (g + 1) % 2
                emit_stats("a", g + 1,
                           XT[:, sl1 * 4096:(sl1 + 1) * 4096].rearrange(
                               "p (fc t) -> p fc t", fc=8),
                           (qx[sl1], qx[sl1].n), 1)

            if base == 0:
                for bi in (6, 7):
                    T(lambda t, bi=bi: t.matmul(
                        PB[bi], CON[0:1, C_ZERO:C_ZERO + 128],
                        CON[0:1, C_OROW:C_OROW + 512],
                        start=True, stop=False, skip_group_check=True),
                        [t_const], sig=False)

            a_free = {2: pb_free[2], 3: pb_free[3]}
            o_free = {4: pb_free[4], 5: pb_free[5]}
            t_mask = {}
            rms = {"sq": {}, "msq_tok": {}, "small": {}, "rsb": {},
                   "bc_war": pb_free[0], "tr_war": pb_free[1],
                   "rsb_war": None, "msq_war": pb_free[1]}

            def kv_mms(c):
                n = base + c
                tkv = None
                for h in range(8):
                    w = []
                    if h == 0:
                        w = [t_kt_done, t_v_done]
                        if n % NCHS != 0:
                            w.append(atn["cp_tok"].get(n))
                    tkv = T(lambda t, c=c, h=h: t.matmul(
                        PB[6 + h // 4][:, (h % 4) * 128:(h % 4 + 1) * 128],
                        U2[:, KTr + h * 512 + c * 128:KTr + h * 512 + (c + 1) * 128],
                        U2[:, VTr + c * 1024 + h * 128:VTr + c * 1024 + (h + 1) * 128],
                        start=False, stop=False, skip_group_check=True),
                        w, sig=(h == 7))
                atn["t_kv"][n] = tkv

            def state_copy(n):
                if n % NCHS == 0:
                    return
                par = n % 2
                w = [atn["t_kv"][n - 1], atn["t_o"].get(n - 2)]
                S(lambda s, par=par: s.copy(SBFS[:, par * 1024:par * 1024 + 512], PB[6]),
                  w, sig=True)
                atn["cp_tok"][n] = S(lambda s, par=par: s.copy(
                    SBFS[:, par * 1024 + 512:par * 1024 + 1024], PB[7]), [], sig=True)

            def a_mms(c):
                toks = {}
                for h in range(8):
                    bidx = 2 + h // 4
                    w = [tq_comb[h], tk_comb[h]]
                    if h % 4 == 0:
                        w.append(a_free[bidx])
                    ta = T(lambda t, c=c, h=h, bidx=bidx: t.matmul(
                        PB[bidx][:, (h % 4) * 128:(h % 4 + 1) * 128],
                        U2[:, KP + h * 512 + c * 128:KP + h * 512 + (c + 1) * 128],
                        U2[:, QP + h * 512 + c * 128:QP + h * 512 + (c + 1) * 128],
                        start=True, stop=True, skip_group_check=True),
                        w, sig=(h == 3 or h == 7))
                    if h in (3, 7):
                        toks[bidx] = ta
                return toks

            def mask_op(c, toks):
                n = base + c
                atsl = ATSL + (c % 2) * 1024
                tms = []
                for bi in (2, 3):
                    w = [toks[bi], t_const, atn["t_o"].get(n - 2)]
                    tm = V(lambda v, bi=bi, atsl=atsl: v.tensor_tensor(
                        U2[:, atsl + (bi - 2) * 512:atsl + (bi - 1) * 512],
                        PB[bi], CON[:, C_MSK:C_MSK + 512], ALU.mult),
                        w, sig=True)
                    a_free[bi] = tm
                    tms.append(tm)
                t_mask[n] = tms

            def o_mms(c):
                n = base + c
                if n % NCHS != 0:
                    for h in range(8):
                        bidx = 4 + h // 4
                        w = [tq_comb[h]]
                        if h % 4 == 0:
                            w += [o_free[bidx], atn["cp_tok"][n]]
                        T(lambda t, c=c, h=h, n=n, bidx=bidx: t.matmul(
                            PB[bidx][:, (h % 4) * 128:(h % 4 + 1) * 128],
                            SBFS[:, (n % 2) * 1024 + h * 128:(n % 2) * 1024 + (h + 1) * 128],
                            U2[:, QP + h * 512 + c * 128:QP + h * 512 + (c + 1) * 128],
                            start=(h % 4 == 0), stop=False, skip_group_check=True),
                            w, sig=False)
                to = {}
                st_flag = (n % NCHS == 0)
                for h in range(8):
                    bidx = 4 + h // 4
                    w = [t_mask[n][h // 4]]
                    if st_flag and h % 4 == 0:
                        w.append(o_free[bidx])
                    to[bidx] = T(lambda t, c=c, h=h, bidx=bidx, st_flag=st_flag: t.matmul(
                        PB[bidx][:, (h % 4) * 128:(h % 4 + 1) * 128],
                        U2[:, VTr + c * 1024 + h * 128:VTr + c * 1024 + (h + 1) * 128],
                        U2[:, ATSL + (c % 2) * 1024 + h * 128:ATSL + (c % 2) * 1024 + (h + 1) * 128],
                        start=(st_flag and h % 4 == 0), stop=True, skip_group_check=True),
                        w, sig=(h == 3 or h == 7))
                atn["t_o"][n] = to[5]
                atn["t_o_b"][n] = (to[4], to[5])

            def o_evac(c):
                n = base + c
                onb = ONr + c * 1024
                o2b = O2SL + (c % 2) * 1024
                t4, t5 = atn["t_o_b"][n]
                cp4 = V(lambda v, onb=onb: v.tensor_copy(U2[:, onb:onb + 512], PB[4]),
                        [t4], sig=True)
                cp5 = V(lambda v, onb=onb: v.tensor_copy(
                    U2[:, onb + 512:onb + 1024], PB[5]), [t5], sig=True)
                sq4 = V(lambda v, onb=onb, o2b=o2b: v.tensor_tensor(
                    U2[:, o2b:o2b + 512], U2[:, onb:onb + 512], U2[:, onb:onb + 512],
                    ALU.mult), [cp4, rms["msq_tok"].get(c - 2)], sig=True)
                sq5 = V(lambda v, onb=onb, o2b=o2b: v.tensor_tensor(
                    U2[:, o2b + 512:o2b + 1024], U2[:, onb + 512:onb + 1024],
                    U2[:, onb + 512:onb + 1024], ALU.mult), [cp5], sig=True)
                o_free[4] = cp4
                o_free[5] = cp5
                rms["sq"][c] = (sq4, sq5)

            def rms_msq(c):
                o2b = O2SL + (c % 2) * 1024
                sq4, sq5 = rms["sq"][c]
                mc = 300 + (c % 2) * 8
                tq = None
                for h in range(8):
                    w = [sq4 if h < 4 else sq5]
                    if h == 0:
                        w.append(rms["msq_war"])
                    tq = T(lambda t, h=h, mc=mc, o2b=o2b: t.matmul(
                        PB[1][:, mc + h:mc + h + 1],
                        U2[:, o2b + h * 128:o2b + (h + 1) * 128],
                        CON[:, C_O128:C_O128 + 1],
                        start=True, stop=True, skip_group_check=True),
                        w, sig=(h == 7))
                rms["msq_tok"][c] = tq

            def rms_small(c):
                mc = 300 + (c % 2) * 8
                gc_ = GS + 24 + (c % 2) * 8
                t1 = V(lambda v, mc=mc, gc_=gc_: v.tensor_scalar_add(
                    GATE[:, gc_:gc_ + 8], PB[1][:, mc:mc + 8], EPS),
                    [rms["msq_tok"][c]], sig=True)
                t2 = V(lambda v, gc_=gc_: v.reciprocal(
                    GATE[:, gc_:gc_ + 8], GATE[:, gc_:gc_ + 8]), [t1], sig=True)
                t3 = S(lambda s, gc_=gc_: s.sqrt(
                    GATE[:, gc_:gc_ + 8], GATE[:, gc_:gc_ + 8]), [t2], sig=True)
                rms["small"][c] = t3
                rms["msq_war"] = t1

            def rms_tr(c):
                gc_ = GS + 24 + (c % 2) * 8
                ttr = T(lambda t, gc_=gc_: t.transpose(
                    PB[1][0:8, 384:512], GATE[:, gc_:gc_ + 8], IDFB[:, :]),
                    [rms["small"][c], rms["tr_war"], t_const], sig=True)
                trsb = S(lambda s: s.copy(RW8[0:8, 0:128],
                                          PB[1][0:8, 384:512]),
                         [ttr, rms["rsb_war"]], sig=True)
                rms["rsb"][c] = trsb
                rms["tr_war"] = trsb

            def rms_bcast(c):
                trsb = rms["rsb"][c]
                last_mm = None
                for r in range(2):
                    mm = None
                    for hh in range(4):
                        h = r * 4 + hh
                        w = [trsb]
                        if hh == 0:
                            w.append(rms["bc_war"])
                        mm = T(lambda t, h=h, hh=hh: t.matmul(
                            PB[0][:, hh * 128:(hh + 1) * 128],
                            CONF8[0:8, h * 128:(h + 1) * 128],
                            RW8[0:8, 0:128],
                            start=True, stop=True, skip_group_check=True),
                            w, sig=(hh == 3))
                    ton = V(lambda v, c=c, r=r: v.tensor_tensor(
                        U2[:, ONr + c * 1024 + r * 512:ONr + c * 1024 + (r + 1) * 512],
                        U2[:, ONr + c * 1024 + r * 512:ONr + c * 1024 + (r + 1) * 512],
                        PB[0], ALU.mult), [mm], sig=True)
                    rms["bc_war"] = ton
                    last_mm = mm
                rms["rsb_war"] = last_mm
                atn["t_on_last"] = rms["bc_war"]

            for c in range(CPT):
                kv_mms(c)
                toks = a_mms(c)
                state_copy(base + c + 1)
                mask_op(c, toks)
                if c >= 2:
                    rms_tr(c - 2)
                o_mms(c)
                if c >= 2:
                    rms_bcast(c - 2)
                o_evac(c)
                if c >= 1:
                    rms_msq(c - 1)
                    rms_small(c - 1)
            rms_msq(CPT - 1)
            rms_small(CPT - 1)
            rms_tr(CPT - 2)
            rms_bcast(CPT - 2)
            rms_tr(CPT - 1)

            pb_free[2] = a_free[2]
            pb_free[3] = a_free[3]
            pb_free[4] = o_free[4]
            pb_free[5] = o_free[5]

            if g + 1 < cfg.n_attn_tiles:
                sl1 = (g + 1) % 2
                apply_pend[g + 1] = emit_apply(
                    "a", g + 1,
                    XT[:, sl1 * 4096:(sl1 + 1) * 4096].rearrange(
                        "p (fc t) -> p fc t", fc=8),
                    U2, HBr, 0, 0, (g + 1) // 4, (qx[sl1], qx[sl1].n),
                    PB[2], 2, PB[3], 3, atn["h_last"], aff_on_s=True)

            rms_bcast(CPT - 1)
            pb_free[0] = rms["bc_war"]
            pb_free[1] = rms["rsb"][CPT - 1]
            if g == 0:
                dump("vt0", U2[:, VTr:VTr + 4096], [t_v_done])
                dump("on0", U2[:, ONr:ONr + 4096], [atn["t_on_last"]])

            atn["g"] = g

        wo_res = []
        for dc in range(8):
            emit_wo_dc(atn["g"], dc, wo_res)
        emit_wo_fin(atn["g"], wo_res)
        emit_staging(100)

        t_attn_T = (mk.ec["T"], mk.ec["T"].n)
        t_attn_V = (mk.ec["V"], mk.ec["V"].n)
        t_attn_S = (mk.ec["S"], mk.ec["S"].n)
        stg_done = sst["wr"][-2:] if sst["wr"] else []

        if cfg.n_ffn_tiles > 0:
            T(lambda t: t.ldweights(CON[:, C_ID:C_ID + 128]),
              [t_attn_V, t_attn_S], sig=True)
            V(lambda v: v.memset(GATE[:, GS + 16:GS + 17], 0.0),
              [t_attn_T, t_attn_S], sig=True)
            S(lambda s: s.memzero(SMALL[:, 14:15]),
              [t_attn_T, t_attn_V], sig=True)

        ffn = {"t_h2": {}, "t_gt": {}, "gt_used": {}, "t_dn_bank": {},
               "ws_free": {0: None, 1: None, 2: None, 3: None},
               "xf_tok": {}, "xf_used": {}, "act_ready": {}, "h2_used": {}}

        def load_xtile_ffn(g):
            sl = g % 2
            dst = XT[:, sl * 4096:(sl + 1) * 4096].rearrange("p (fc t) -> p fc t", fc=8)
            w = [atn["xpost"][g] if g < len(atn["xpost"]) else None,
                 ffn["xf_used"].get(g - 2)]
            w += xt_free[sl] if isinstance(xt_free[sl], list) else [xt_free[sl]]
            if getattr(cfg, "xload_y", False):
                Y(lambda y, d=dst, g=g: y.dma_start(
                    out=d, in_=xscr[:, :, g * TILE:(g + 1) * TILE]), w, dma=yx[sl])
            else:
                Q(lambda q, d=dst, g=g: q.dma_start(
                    out=d, in_=xscr[:, :, g * TILE:(g + 1) * TILE]), w, dma=yx[sl])
            ffn["xf_tok"][g] = (yx[sl], yx[sl].n)

        def emit_lnb(g):
            if g >= cfg.n_ffn_tiles:
                return
            sl = g % 2
            b_ = g // 4
            xt_ap = XT[:, sl * 4096:(sl + 1) * 4096].rearrange("p (fc t) -> p fc t", fc=8)
            ffn["t_h2"][g] = emit_apply(
                "f", g, xt_ap, U1, H2r[sl], 32, 48, b_, ffn["xf_tok"][g],
                PB[0], 0, PB[2], 2, ffn["h2_used"].get(g - 2))

        def emit_router(g):
            if g >= cfg.n_ffn_tiles:
                return
            sl = g % 2
            tl = None
            for st_ in range(4):
                for kc in range(8):
                    w = [ffn["t_h2"][g][kc], t_qld]
                    if st_ == 0 and kc == 0:
                        w.append(pb_free[3])
                    tl = T(lambda t, st_=st_, kc=kc, sl=sl: t.matmul(
                        PB[3][:, st_ * 16:(st_ + 1) * 16],
                        U1[:, H2r[sl] + kc * 512 + st_ * 128:H2r[sl] + kc * 512 + (st_ + 1) * 128],
                        RT[:, kc * 16:(kc + 1) * 16],
                        start=(kc == 0), stop=(kc == 7), skip_group_check=True),
                        w, sig=(st_ == 3 and kc == 7))
            lseg = PB[3][:, 0:64].rearrange("p (gg n) -> p gg n", gg=4)
            t1 = V(lambda v: v.reduce_max(GATE[:, GS:GS + 4], lseg, axis=AX),
                   [tl], sig=True)
            eseg = GATE[:, GE:GE + 64].rearrange("p (gg n) -> p gg n", gg=4)
            t2 = V(lambda v: v.tensor_tensor(
                eseg, lseg, GATE[:, GS:GS + 4].unsqueeze(2).broadcast_to([128, 4, 16]),
                ALU.subtract), [t1], sig=True)
            pb_free[3] = t2
            t3 = S(lambda s: s.activation(GATE[:, GE:GE + 64], GATE[:, GE:GE + 64],
                                          AF.Exp), [t2], sig=True)
            t4 = V(lambda v: v.reduce_sum(GATE[:, GS + 4:GS + 8], eseg, axis=AX),
                   [t3], sig=True)
            t5 = V(lambda v: v.reciprocal(GATE[:, GS + 8:GS + 12],
                                          GATE[:, GS + 4:GS + 8]), [t4], sig=True)
            tcur = V(lambda v: v.tensor_copy(GATE[:, GC:GC + 64], GATE[:, GE:GE + 64]),
                     [t3], sig=True)
            cseg = GATE[:, GC:GC + 64].rearrange("p (gg n) -> p gg n", gg=4)
            mseg = GATE[:, GM:GM + 64].rearrange("p (gg n) -> p gg n", gg=4)
            for r in range(4):
                tcur = V(lambda v: v.reduce_max(GATE[:, GS + 12:GS + 16], cseg, axis=AX),
                         [tcur], sig=True)
                if r < 3:
                    tm = V(lambda v: v.tensor_tensor(
                        mseg, cseg,
                        GATE[:, GS + 12:GS + 16].unsqueeze(2).broadcast_to([128, 4, 16]),
                        ALU.is_ge), [tcur], sig=True)
                    tcur = V(lambda v: v.scalar_tensor_tensor(
                        GATE[:, GC:GC + 64], GATE[:, GM:GM + 64], -1e9,
                        GATE[:, GC:GC + 64], ALU.mult, ALU.add), [tm], sig=True)
            tg1 = V(lambda v: v.tensor_tensor(
                mseg, eseg,
                GATE[:, GS + 12:GS + 16].unsqueeze(2).broadcast_to([128, 4, 16]),
                ALU.is_ge), [tcur], sig=True)
            tg2 = V(lambda v: v.tensor_tensor(GATE[:, GM:GM + 64], GATE[:, GM:GM + 64],
                                              GATE[:, GE:GE + 64], ALU.mult),
                    [tg1], sig=True)
            gseg = GATE[:, GE:GE + 64].rearrange("p (gg n) -> p gg n", gg=4)
            tg3 = V(lambda v: v.tensor_tensor(
                gseg, mseg,
                GATE[:, GS + 8:GS + 12].unsqueeze(2).broadcast_to([128, 4, 16]),
                ALU.mult), [tg2, t5], sig=True)
            tg4 = V(lambda v: v.tensor_copy(GATB[:, :], GATE[:, GE:GE + 64]),
                    [tg3], sig=True)
            ffn["tg4"] = (tg4, g)

        def emit_router_gt(g):
            if g >= cfg.n_ffn_tiles:
                return
            sl = g % 2
            tg4, g_chk = ffn["tg4"]
            assert g_chk == g
            ttr = None
            for st_ in range(4):
                w = [tg4, t_const]
                if st_ == 0:
                    w.append(pb_free[5])
                    w.append(ffn["gt_used"].get(g - 2))
                ttr = T(lambda t, st_=st_: t.transpose(
                    PB[5].bitcast(BF16)[0:16, st_ * 128:(st_ + 1) * 128],
                    GATB[:, st_ * 16:(st_ + 1) * 16],
                    CON[:, C_ID:C_ID + 128]), w, sig=(st_ == 3))
            tge = S(lambda s, sl=sl: s.copy(GTT[0:16, sl * 512:(sl + 1) * 512],
                                            PB[5].bitcast(BF16)[0:16, 0:512]),
                    [ttr], sig=True)
            pb_free[5] = tge
            ffn["t_gt"][g] = tge

        def stream_w(x_, g, which, base):
            i = x_ % 3
            src = (wup if which == "up" else wdn)[x_]
            ctr = upc if which == "up" else dnc
            w = [(ctr, base + x_ - 2) if base + x_ - 2 > 0 else None]
            if g == 0:
                stg_idx = 2 * x_ + 1 if which == "up" else 2 * NX + 2 * x_ + 1
                w.append(sst["wr"][stg_idx] if stg_idx < len(sst["wr"]) else None)
                if x_ < 3:
                    w += [t_attn_T, t_attn_V, t_attn_S]
            if which == "up":
                Y(lambda y, d=U1[:, USL[i]:USL[i] + 4096], s=src: y.dma_start(
                    out=d, in_=s[:, :]), w, dma=yws[i])
                return (yws[i], yws[i].n)
            if i < 2:
                q_ = qws[i]
                Q(lambda q, d=U1[:, DSL[i]:DSL[i] + 4096], s=src: q.dma_start(
                    out=d, in_=s[:, :]), w, dma=q_)
                return (q_, q_.n)
            Y(lambda y, d=U1[:, DSL[2]:DSL[2] + 4096], s=src: y.dma_start(
                out=d, in_=s[:, :]), w, dma=yws[3])
            return (yws[3], yws[3].n)

        for g in range(cfg.n_ffn_tiles):
            sl = g % 2
            b_ = g // 4
            if g == 0:
                load_xtile_ffn(0)
                if cfg.n_ffn_tiles > 1:
                    load_xtile_ffn(1)
                emit_lnb(0)
                dump("h2t0", U1[:, H2r[0]:H2r[0] + 4096], [ffn["t_h2"][0][7]])
                emit_router(0)
                emit_router_gt(0)
            up_base, dn_base = upc.n, dnc.n
            lazy = getattr(cfg, "lazy_streams", False)
            if not lazy:
                up_toks = [stream_w(x_, g, "up", up_base) for x_ in range(NX)]
                dn_toks = [stream_w(x_, g, "dn", dn_base) for x_ in range(NX)]
            BUP = (6, 7, 1)
            for x_ in range(NX):
                t_ws = stream_w(x_, g, "up", up_base) if lazy else up_toks[x_]
                tup_last = None
                gel = [None] * ECH
                for ec in range(ECH):
                    bidx = BUP[(x_ * ECH + ec) % 3]
                    tup = None
                    for kc in range(8):
                        w = [ffn["t_h2"][g][kc]]
                        if kc == 0:
                            w += [t_ws, pb_free[bidx]]
                        if kc == 7 and ec == ECH - 1:
                            tup = mk.emit("T", lambda t, x_=x_, ec=ec, kc=kc, bidx=bidx,
                                          i=x_ % 3, sl=sl: t.matmul(
                                PB[bidx],
                                U1[:, USL[i] + kc * 512 + ec * 128:USL[i] + kc * 512 + (ec + 1) * 128],
                                U1[:, H2r[sl] + kc * 512:H2r[sl] + (kc + 1) * 512],
                                start=(kc == 0), stop=(kc == 7), skip_group_check=True),
                                w, inc=(upc, 1))
                        else:
                            tup = T(lambda t, x_=x_, ec=ec, kc=kc, bidx=bidx, i=x_ % 3, sl=sl:
                                    t.matmul(
                                PB[bidx],
                                U1[:, USL[i] + kc * 512 + ec * 128:USL[i] + kc * 512 + (ec + 1) * 128],
                                U1[:, H2r[sl] + kc * 512:H2r[sl] + (kc + 1) * 512],
                                start=(kc == 0), stop=(kc == 7), skip_group_check=True),
                                w, sig=(kc == 7))
                    gel[ec] = S(lambda s, x_=x_, ec=ec, bidx=bidx: s.activation(
                        U2[:, x_ * 2048 + ec * 512:x_ * 2048 + (ec + 1) * 512],
                        PB[bidx], AF.Gelu), [tup], sig=True)
                    pb_free[bidx] = gel[ec]
                    tup_last = tup
                if x_ >= 2:
                    e_idx = x_ - 2
                    tbc = T(lambda t, e_idx=e_idx, sl=sl: t.matmul(
                        PB[4], CON[0:16, C_SEL + e_idx * 128:C_SEL + (e_idx + 1) * 128],
                        GTT[0:16, sl * 512:(sl + 1) * 512],
                        start=True, stop=True, skip_group_check=True),
                        [ffn["t_gt"][g], pb_free[4], t_const], sig=True)
                    ffn["gt_used"][g] = tbc
                    tgb = S(lambda s: s.copy(GBS[:, :], PB[4]),
                            [tbc, ffn["act_ready"].get(x_ - 1)], sig=True)
                    if g == 0 and x_ == 2:
                        dump("gbs0", GBS, [tgb])
                        dump("gtt0", GTT[0:16, 0:512], [ffn["t_gt"][g]])
                    pb_free[4] = tgb
                    tmul = None
                    for ec in range(ECH):
                        tmul = V(lambda v, x_=x_, ec=ec: v.tensor_tensor(
                            U2[:, x_ * 2048 + ec * 512:x_ * 2048 + (ec + 1) * 512],
                            GBS[:, :],
                            U2[:, x_ * 2048 + ec * 512:x_ * 2048 + (ec + 1) * 512],
                            ALU.mult), [gel[ec], tgb], sig=(ec == ECH - 1))
                    ffn["act_ready"][x_] = tmul
                    if g == 0 and x_ == 2:
                        dump("act2", U2[:, 2 * 2048:3 * 2048], [tmul])
                else:
                    ffn["act_ready"][x_] = gel[ECH - 1]
                    if g == 0 and x_ == 0:
                        dump("act0", U2[:, 0:2048], [gel[ECH - 1]])
                if x_ == 3:
                    emit_lnb(g + 1)
                if x_ == 11:
                    emit_router(g + 1)
                if x_ == 15:
                    emit_router_gt(g + 1)
            ffn["h2_used"][g] = (upc, upc.n)
            for x_ in range(NX):
                t_ws = stream_w(x_, g, "dn", dn_base) if lazy else dn_toks[x_]
                tdn = None
                for hc in range(8):
                    for ec in range(ECH):
                        w = []
                        if ec == 0:
                            w = [t_ws, ffn["act_ready"][x_]]
                            if x_ == 0:
                                w.append(pb_free[hc])
                        if hc == 7 and ec == ECH - 1:
                            tdn = mk.emit("T", lambda t, x_=x_, hc=hc, ec=ec, i=x_ % 3:
                                          t.matmul(
                                PB[hc],
                                U1[:, DSL[i] + ec * 1024 + hc * 128:DSL[i] + ec * 1024 + (hc + 1) * 128],
                                U2[:, x_ * 2048 + ec * 512:x_ * 2048 + (ec + 1) * 512],
                                start=(x_ == 0 and ec == 0),
                                stop=(x_ == NX - 1 and ec == ECH - 1),
                                skip_group_check=True), w, inc=(dnc, 1))
                        else:
                            tdn = T(lambda t, x_=x_, hc=hc, ec=ec, i=x_ % 3: t.matmul(
                                PB[hc],
                                U1[:, DSL[i] + ec * 1024 + hc * 128:DSL[i] + ec * 1024 + (hc + 1) * 128],
                                U2[:, x_ * 2048 + ec * 512:x_ * 2048 + (ec + 1) * 512],
                                start=(x_ == 0 and ec == 0),
                                stop=(x_ == NX - 1 and ec == ECH - 1),
                                skip_group_check=True),
                                w, sig=(ec == ECH - 1 and x_ == NX - 1))
                        if x_ == NX - 1 and ec == ECH - 1:
                            ffn["t_dn_bank"][hc] = tdn
            ev_order = [6, 7, 0, 1, 2, 3, 4, 5]
            for k, hc in enumerate(ev_order):
                par = k % 2
                w = [ffn["t_dn_bank"][hc], ffn["xf_tok"][g], t_mod_done,
                     (so[par], so[par].n)]
                tev = V(lambda v, hc=hc, par=par, sl=sl, b_=b_: v.scalar_tensor_tensor(
                    SCRF[:, OST[par]:OST[par] + 512],
                    PB[hc], DERV[:, 48 + hc * 2 + b_:48 + hc * 2 + b_ + 1],
                    XT[:, sl * 4096 + hc * 512:sl * 4096 + (hc + 1) * 512],
                    ALU.mult, ALU.add), w, sig=True)
                pb_free[hc] = tev
                mk.emit("S", lambda s, hc=hc, par=par, g=g: s.dma_start(
                    out=outT[hc * 128:(hc + 1) * 128, g * TILE:(g + 1) * TILE],
                    in_=SCRF[:, OST[par]:OST[par] + 512]), [tev], dma=so[par])
            ffn["xf_used"][g] = pb_free[5]
            if g + 2 < cfg.n_ffn_tiles:
                load_xtile_ffn(g + 2)

        mk.emit("S", lambda s: s.memzero(SMALL[:, 15:16]),
                [(so[0], so[0].n), (so[1], so[1].n)])

        mk.check()
        for c in mk.ctrs:
            c.sem = st.enter_context(nc.semaphore(c.name))
        block = st.enter_context(nc.Block())
        mk.replay(block)
    return nc, dbg_names


def host_prep(inputs):
    c = np.ascontiguousarray
    x = inputs["L0_attn_hidden_states"]
    common = {
        "wqT": c(inputs["L0_attn_Wq"].T), "wkT": c(inputs["L0_attn_Wk"].T),
        "wvT": c(inputs["L0_attn_Wv"].T), "woT": c(inputs["L0_attn_Wo"].T),
        "fm1qT": c(inputs["L0_attn_fmq_w1"].T), "fm2qT": c(inputs["L0_attn_fmq_w2"].T),
        "fm1kT": c(inputs["L0_attn_fmk_w1"].T), "fm2kT": c(inputs["L0_attn_fmk_w2"].T),
        "fmq_b1": c(inputs["L0_attn_fmq_b1"].reshape(DK, 1)),
        "fmq_b2": c(inputs["L0_attn_fmq_b2"].reshape(DK, 1)),
        "fmk_b1": c(inputs["L0_attn_fmk_b1"].reshape(DK, 1)),
        "fmk_b2": c(inputs["L0_attn_fmk_b2"].reshape(DK, 1)),
        "rms_w": c(inputs["L0_attn_rms_w"].reshape(DK, 1)),
        "suT": c(inputs["L0_ffn_shared_up"].transpose(0, 2, 1)),
        "sdT": c(inputs["L0_ffn_shared_down"].transpose(0, 2, 1)),
        "ruT": c(inputs["L0_ffn_routed_up"].transpose(0, 2, 1)),
        "rdT": c(inputs["L0_ffn_routed_down"].transpose(0, 2, 1)),
        "routerT": c(inputs["L0_ffn_router_w"].T),
        "wmod": c(inputs["L0_wmod"]),
    }
    per_core = []
    for cc in range(8):
        d = dict(common)
        d["xT"] = c(x[2 * cc:2 * cc + 2].reshape(2 * TSEQ, HID).T)
        d["cT"] = c(inputs["L0_c"][2 * cc:2 * cc + 2].T)
        per_core.append(d)
    return per_core


def kernel(**inputs):
    from concourse.bass_utils import run_bass_kernel_spmd
    cfg = Cfg()
    nc, _ = build(cfg)
    in_maps = host_prep(inputs)
    res = run_bass_kernel_spmd(nc, in_maps, core_ids=list(range(8)))
    outs = []
    for cc in range(8):
        o = res.results[cc]["outT"]
        outs.append(np.ascontiguousarray(np.asarray(o).T).reshape(2, TSEQ, HID))
    return np.concatenate(outs, axis=0).astype(np.float32)


# version 15
# speedup vs baseline: 1.0455x; 1.0003x over previous
import numpy as np
import ml_dtypes
import concourse.bass as bass
import concourse.mybir as mybir
from contextlib import ExitStack

F32, BF16 = mybir.dt.float32, mybir.dt.bfloat16
AF, ALU = mybir.ActivationFunctionType, mybir.AluOpType
AX = mybir.AxisListType.X
bfd = ml_dtypes.bfloat16

HID, FC, H, DK, CH, TILE = 1024, 8, 8, 128, 128, 512
NSEQ, TSEQ = 2, 2048
CPT = TILE // CH
NAT, NFT = 8, 8
E, ECH, NX = 512, 4, 18
EPS = 1e-5
SCL = float(DK) ** -0.5
NCHS = 16

QP, KP, KTr, VTr, ONr, HBr = 0, 4096, 8192, 12288, 16384, 20480
A1SL, ATSL, O2SL, STG, XSQA = 24576, 25600, 27648, 29696, 33792
U2N = 36864
WQ, WK, WV, WO = 0, 8192, 16384, 24576
USL = [0, 4096, 24576]
DSL = [8192, 12288, 28672]
H2r = [16384, 20480]
U1N = 32768
TA = [0, 512]
TB = [1024, 1536]
RC = [2048, 2560]
OST = [3072, 3584]
OCP = [4096, 4608]
SCRFN = 5120
GE, GC, GM, GS = 0, 64, 128, 192
GATEN = 256
C_ID, C_O1024, C_O128, C_MSK = 0, 128, 256, 384
C_EPS, C_OROW, C_OK1, C_ZERO, C_ONES = 896, 1024, 1536, 1664, 1792
C_SEL = 1920
CONSTN = 4096


class Cfg:
    def __init__(self, n_attn_tiles=NAT, n_ffn_tiles=NFT, dbg=False):
        self.n_attn_tiles = n_attn_tiles
        self.n_ffn_tiles = n_ffn_tiles
        self.dbg = dbg


class Ctr:
    __slots__ = ("name", "n", "sem")
    def __init__(self, name):
        self.name, self.n, self.sem = name, 0, None


class MK:
    def __init__(self):
        self.ops = {e: [] for e in "TVSQY"}
        self.ctrs = []
        self.last_wait = {e: {} for e in "TVSQY"}
        self.ec = {e: self.ctr("ec" + e) for e in "TVSQY"}

    def ctr(self, name):
        c = Ctr(name)
        self.ctrs.append(c)
        return c

    def emit(self, eng, fn, waits=(), sig=False, dma=None, inc=None):
        ws = []
        lw = self.last_wait[eng]
        for t in waits:
            if t is None:
                continue
            c, v = t
            if v <= 0 or lw.get(c, 0) >= v:
                continue
            lw[c] = v
            ws.append((c, v))
        if inc is not None:
            inc[0].n += inc[1]
            inc = (inc[0], inc[1])
        elif dma is not None:
            dma.n += 16
            inc = (dma, 16)
        elif sig:
            c = self.ec[eng]
            c.n += 1
            inc = (c, 1)
        self.ops[eng].append((fn, ws, inc))
        if inc is not None:
            return (inc[0], inc[0].n)
        if dma is not None:
            return (dma, dma.n)
        if sig:
            return (self.ec[eng], self.ec[eng].n)
        return None

    def check(self):
        vals = {id(c): 0 for c in self.ctrs}
        pos = {e: 0 for e in self.ops}
        progress = True
        while progress:
            progress = False
            for e, lst in self.ops.items():
                while pos[e] < len(lst):
                    fn, ws, inc = lst[pos[e]]
                    if any(vals[id(c)] < v for c, v in ws):
                        break
                    if inc is not None:
                        vals[id(inc[0])] += inc[1]
                    pos[e] += 1
                    progress = True
        stuck = {e: p for e, p in pos.items() if p < len(self.ops[e])}
        if stuck:
            lines = []
            for e, p in stuck.items():
                fn, ws, inc = self.ops[e][p]
                blocked = [(c.name, v, vals[id(c)]) for c, v in ws if vals[id(c)] < v]
                lines.append(f"engine {e} op#{p}/{len(self.ops[e])} blocked on {blocked} inc={inc[0].name if inc else None}")
            for c in self.ctrs:
                if c.name in ("ecT", "ecS", "ecV", "upc", "dnc"):
                    lines.append(f"ctr {c.name}: final_n={c.n} sim_val={vals[id(c)]}")
            raise RuntimeError("schedule deadlock:\n" + "\n".join(lines))

    def replay(self, block):
        def runner(e):
            def run(eng):
                for fn, ws, inc in self.ops[e]:
                    for c, v in ws:
                        eng.wait_ge(c.sem, v)
                    ins = fn(eng)
                    if inc is not None:
                        ins.then_inc(inc[0].sem, inc[1])
            return run
        block.gpsimd(runner("Q"))
        block.sync(runner("Y"))
        block.tensor(runner("T"))
        block.vector(runner("V"))
        block.scalar(runner("S"))


def consts_arrays():
    out = np.zeros((128, CONSTN), dtype=bfd)
    out[:, C_ID:C_ID + 128] = np.eye(128, dtype=bfd)
    out[:, C_O1024:C_O1024 + 128] = np.full((128, 128), 1.0 / 1024.0, dtype=bfd)
    out[:, C_O128:C_O128 + 128] = np.full((128, 128), 1.0 / 128.0, dtype=bfd)
    mt = np.tile(np.triu(np.ones((128, 128), np.float32)), (1, 4)).astype(bfd)
    out[:, C_MSK:C_MSK + 512] = mt
    out[0:1, C_EPS:C_EPS + 128] = np.full((1, 128), EPS, np.float32).astype(bfd)
    out[0:1, C_OROW:C_OROW + 512] = np.ones((1, 512), dtype=bfd)
    out[0:1, C_OK1:C_OK1 + 128] = np.ones((1, 128), dtype=bfd)
    out[:, C_ONES:C_ONES + 128] = np.ones((128, 128), dtype=bfd)
    sel = np.zeros((16, 16 * 128), dtype=bfd)
    for e in range(16):
        sel[e, e * 128:(e + 1) * 128] = 1.0
    out[0:16, C_SEL:C_SEL + 2048] = sel
    return out


def build(cfg):
    mk = MK()
    nc = bass.Bass("TRN2", target_bir_lowering=False, debug=False,
                   detect_race_conditions=False)

    def dram_in(name, shape):
        return nc.dram_tensor(name, list(shape), F32, kind="ExternalInput").ap()

    xTh = dram_in("xT", [HID, NSEQ * TSEQ])
    cTh = dram_in("cT", [HID, NSEQ])
    wqTh, wkTh = dram_in("wqT", [HID, HID]), dram_in("wkT", [HID, HID])
    wvTh, woTh = dram_in("wvT", [HID, HID]), dram_in("woT", [HID, HID])
    fm1qh, fm2qh = dram_in("fm1qT", [DK, DK]), dram_in("fm2qT", [DK, DK])
    fm1kh, fm2kh = dram_in("fm1kT", [DK, DK]), dram_in("fm2kT", [DK, DK])
    fbq1h, fbq2h = dram_in("fmq_b1", [DK, 1]), dram_in("fmq_b2", [DK, 1])
    fbk1h, fbk2h = dram_in("fmk_b1", [DK, 1]), dram_in("fmk_b2", [DK, 1])
    rmswh = dram_in("rms_w", [DK, 1])
    suTh, sdTh = dram_in("suT", [2, HID, E]), dram_in("sdT", [2, E, HID])
    ruTh, rdTh = dram_in("ruT", [16, HID, E]), dram_in("rdT", [16, E, HID])
    rwTh = dram_in("routerT", [HID, 16])
    wmodh = dram_in("wmod", [HID, 6 * HID])

    outT = nc.dram_tensor("outT", [HID, NSEQ * TSEQ], F32, kind="ExternalOutput").ap()
    scr_kind = "ExternalOutput" if cfg.dbg else "Internal"
    wup = nc.dram_tensor("wup", [NX, 128, FC * E], BF16, kind="Internal").ap()
    wdn = nc.dram_tensor("wdn", [NX, 128, ECH * HID], BF16, kind="Internal").ap()
    xscr = nc.dram_tensor("xscr", [128, FC, NSEQ * TSEQ], BF16, kind=scr_kind).ap()
    constd = nc.inline_tensor(consts_arrays(), "constd").ap()
    idf2 = nc.inline_tensor(np.eye(2, dtype=np.float32), "idf2").ap()
    idf128 = nc.inline_tensor(np.eye(128, dtype=np.float32), "idf128").ap()
    self4 = np.zeros((4, 512), np.float32)
    for e in range(4):
        self4[e, e * 128:(e + 1) * 128] = 1.0
    self4 = nc.inline_tensor(self4, "self4").ap()
    self8 = np.zeros((8, 1024), dtype=bfd)
    for e in range(8):
        self8[e, e * 128:(e + 1) * 128] = 1.0
    self8 = nc.inline_tensor(self8, "self8").ap()

    with ExitStack() as st:
        U1 = st.enter_context(nc.sbuf_tensor("U1", [128, U1N], BF16))[:]
        U2 = st.enter_context(nc.sbuf_tensor("U2", [128, U2N], BF16))[:]
        XT = st.enter_context(nc.sbuf_tensor("XTb", [128, 2 * FC * TILE], BF16))[:]
        SCRF = st.enter_context(nc.sbuf_tensor("SCRF", [128, SCRFN], F32))[:]
        GATE = st.enter_context(nc.sbuf_tensor("GATE", [128, GATEN], F32))[:]
        CON = st.enter_context(nc.sbuf_tensor("CON", [128, CONSTN], BF16))[:]
        MODF = st.enter_context(nc.sbuf_tensor("MODF", [128, 96], F32))[:]
        DERV = st.enter_context(nc.sbuf_tensor("DERV", [128, 64], F32))[:]
        SMALL = st.enter_context(nc.sbuf_tensor("SMALL", [128, 16], F32))[:]
        CTF = st.enter_context(nc.sbuf_tensor("CTF", [128, 16], F32))[:]
        CTB = st.enter_context(nc.sbuf_tensor("CTB", [128, 16], BF16))[:]
        RT = st.enter_context(nc.sbuf_tensor("RT", [128, 128], BF16))[:]
        FMW = st.enter_context(nc.sbuf_tensor("FMW", [128, 512], BF16))[:]
        MODTMP = st.enter_context(nc.sbuf_tensor("MODTMP", [2, 512], F32))[:]
        SILF = st.enter_context(nc.sbuf_tensor("SILF", [128, 16], F32))[:]
        IDF = st.enter_context(nc.sbuf_tensor("IDF", [2, 2], F32))[:]
        IDFB = st.enter_context(nc.sbuf_tensor("IDFB", [128, 128], F32))[:]
        SBFS = st.enter_context(nc.sbuf_tensor("SBFS", [128, 2048], BF16))[:]
        GTT = st.enter_context(nc.sbuf_tensor("GTT", [16, 1024], BF16))[:]
        GBS = st.enter_context(nc.sbuf_tensor("GBS", [128, 512], BF16))[:]
        GATB = st.enter_context(nc.sbuf_tensor("GATB", [128, 64], BF16))[:]
        STATS = st.enter_context(nc.sbuf_tensor("STATS", [128, 128], F32))[:]
        ROWB = st.enter_context(nc.sbuf_tensor("ROWB", [4, 2560], F32))[:]
        RW8 = st.enter_context(nc.sbuf_tensor("RW8", [8, 128], BF16))[:]
        CONF8 = st.enter_context(nc.sbuf_tensor("CONF8", [8, 1024], BF16))[:]
        SELF4 = st.enter_context(nc.sbuf_tensor("SELF4", [4, 512], F32))[:]
        PB = [st.enter_context(nc.psum_tensor(f"PB{i}", [128, 512], F32))[:]
              for i in range(8)]

        qld = mk.ctr("qld")
        yconst = mk.ctr("yconst")
        qx = [mk.ctr("qx0"), mk.ctr("qx1")]
        yx = [mk.ctr("yx0"), mk.ctr("yx1")]
        yxp = [mk.ctr("yxp0"), mk.ctr("yxp1")]
        wmq = [mk.ctr("wmq0"), mk.ctr("wmq1")]
        qsc = [mk.ctr("qs0"), mk.ctr("qs1")]
        ysm = [mk.ctr("ys0"), mk.ctr("ys1")]
        yws = [mk.ctr(f"yws{i}") for i in range(4)]
        qws = [mk.ctr(f"qws{i}") for i in range(2)]
        upc = mk.ctr("upc")
        dnc = mk.ctr("dnc")
        so = [mk.ctr("so0"), mk.ctr("so1")]
        ydbg = mk.ctr("ydbg")

        def T(fn, waits=(), sig=False):
            return mk.emit("T", fn, waits, sig=sig)
        def V(fn, waits=(), sig=False):
            return mk.emit("V", fn, waits, sig=sig)
        def S(fn, waits=(), sig=False):
            return mk.emit("S", fn, waits, sig=sig)
        def Q(fn, waits=(), dma=None):
            return mk.emit("Q", fn, waits, dma=dma)
        def Y(fn, waits=(), dma=None):
            return mk.emit("Y", fn, waits, dma=dma)

        dbg_names = []
        def dump(name, ap, waits):
            if not cfg.dbg:
                return
            t = nc.dram_tensor("dbg_" + name, list(ap.shape), ap.dtype,
                               kind="ExternalOutput").ap()
            dbg_names.append("dbg_" + name)
            idx = tuple(slice(None) for _ in ap.shape)
            Y(lambda y, o=t, i=ap: y.dma_start(out=o[idx] if o.ndim > 1 else o, in_=i),
              waits, dma=ydbg)

        pb_free = {i: None for i in range(8)}
        xt_free = [None, None]
        lnp = {"aff": [None] * 8, "msq6": None, "msq7": None}

        Y(lambda y: y.dma_start(out=CON[:, :], in_=constd[:, :]), dma=yconst)
        Y(lambda y: y.dma_start(out=IDF[:, :], in_=idf2[:, :]), dma=yconst)
        Y(lambda y: y.dma_start(out=IDFB[:, :], in_=idf128[:, :]), dma=yconst)
        Y(lambda y: y.dma_start(out=SELF4[:, :], in_=self4[:, :]), dma=yconst)
        Y(lambda y: y.dma_start(out=CONF8[:, :], in_=self8[:, :]), dma=yconst)
        t_const = (yconst, yconst.n)

        def cast_load(dst, src):
            return Q(lambda q, d=dst, s=src: q.dma_start(out=d, in_=s), dma=qld)

        cast_load(U1[:, WQ:WQ + 8192].rearrange("p (kc d) -> p kc d", kc=8),
                  wqTh.rearrange("(kc p) d -> p kc d", p=128))
        cast_load(U1[:, WK:WK + 8192].rearrange("p (kc d) -> p kc d", kc=8),
                  wkTh.rearrange("(kc p) d -> p kc d", p=128))
        cast_load(U1[:, WV:WV + 8192].rearrange("p (kc d) -> p kc d", kc=8),
                  wvTh.rearrange("(kc p) d -> p kc d", p=128))
        cast_load(U1[:, WO:WO + 8192].rearrange("p (kc d) -> p kc d", kc=8),
                  woTh.rearrange("(kc p) d -> p kc d", p=128))
        cast_load(FMW[:, 0:128], fm1qh[:, :])
        cast_load(FMW[:, 128:256], fm2qh[:, :])
        cast_load(FMW[:, 256:384], fm1kh[:, :])
        cast_load(FMW[:, 384:512], fm2kh[:, :])
        cast_load(RT[:, 0:128].rearrange("p (kc e) -> p kc e", kc=8),
                  rwTh.rearrange("(kc p) e -> p kc e", p=128))
        for dst, src in ((SMALL[:, 0:1], fbq1h), (SMALL[:, 1:2], fbq2h),
                         (SMALL[:, 2:3], fbk1h), (SMALL[:, 3:4], fbk2h),
                         (SMALL[:, 4:5], rmswh)):
            Q(lambda q, d=dst, s=src: q.dma_start(out=d, in_=s[:, :]), dma=qld)
        Q(lambda q: q.dma_start(out=CTF[:, :].rearrange("p (ic b) -> p ic b", ic=8),
                                in_=cTh.rearrange("(ic p) b -> p ic b", p=128)), dma=qld)
        t_qld = (qld, qld.n)

        def load_xtile_attn(g):
            sl = g % 2
            dst = XT[:, sl * 4096:(sl + 1) * 4096].rearrange("p (fc t) -> p fc t", fc=8)
            src = xTh.rearrange("(fc p) t -> p fc t", p=128)[:, :, g * TILE:(g + 1) * TILE]
            wfree = xt_free[sl] if isinstance(xt_free[sl], list) else [xt_free[sl]]
            Q(lambda q, d=dst, s=src: q.dma_start(out=d, in_=s),
              wfree, dma=qx[sl])

        if cfg.n_attn_tiles > 0:
            load_xtile_attn(0)
        if cfg.n_attn_tiles > 1:
            load_xtile_attn(1)
        need_stats0 = cfg.n_attn_tiles > 0

        t_siluf = S(lambda s: s.activation(SILF[:, :], CTF[:, :], AF.Silu),
                    [t_qld], sig=True)
        t_silu = V(lambda v: v.tensor_copy(CTB[:, :], SILF[:, :]),
                   [t_siluf], sig=True)
        t_fmw = V(lambda v: v.tensor_scalar_mul(FMW[:, 0:128], FMW[:, 0:128], SCL),
                  [t_qld], sig=True)
        t_b1q = V(lambda v: v.tensor_scalar_mul(SMALL[:, 0:1], SMALL[:, 0:1], SCL),
                  [t_qld], sig=True)
        t_wo_scaled = None
        for kc in range(8):
            t_wo_scaled = S(lambda s, kc=kc: s.activation(
                U1[:, WO + kc * 1024:WO + (kc + 1) * 1024],
                U1[:, WO + kc * 1024:WO + (kc + 1) * 1024],
                AF.Identity, scale=SMALL[:, 4:5]), [t_qld], sig=True)

        slab_last = {}
        t_modf = None
        t_ev = None
        ttr_all = None
        for pc in range(12):
            isf = pc in (4, 5, 10, 11)
            par = pc % 2
            slab_id = ("F" if isf else "B") + str(par)
            src_p = wmodh.rearrange("(ic p) j -> p ic j", p=128)[
                :, :, pc * 512:(pc + 1) * 512]
            if isf:
                base = 8192 + par * 8192
                slab = U2[:, base:base + 8192].bitcast(F32).rearrange(
                    "p (ic j) -> p ic j", ic=8)
                slab_mm = U2[:, base:base + 8192].bitcast(F32)
            else:
                base = par * 4096
                slab = U2[:, base:base + 4096].rearrange("p (ic j) -> p ic j", ic=8)
                slab_mm = U2[:, base:base + 4096]
            wq_waits = [slab_last.get(slab_id)]
            Q(lambda q, d=slab, s=src_p: q.dma_start(out=d, in_=s),
              wq_waits, dma=wmq[par])
            t_wm = (wmq[par], wmq[par].n)
            tmm = None
            for ic in range(8):
                w = [t_wm, t_silu, t_siluf]
                if ic == 0 and t_ev is not None:
                    w.append(t_ev)
                tmm = T(lambda t, ic=ic, isf=isf, sm=slab_mm: t.matmul(
                    PB[6][0:2, :],
                    (SILF if isf else CTB)[:, ic * 2:ic * 2 + 2],
                    sm[:, ic * 512:(ic + 1) * 512],
                    start=(ic == 0), stop=(ic == 7), skip_group_check=True),
                    w, sig=(ic == 7))
            slab_last[slab_id] = tmm
            t_ev = S(lambda s: s.copy(MODTMP[0:2, :], PB[6][0:2, :]), [tmm], sig=True)
            for k in range(4):
                w = [t_ev, t_const]
                ttr_all = T(lambda t, k=k, pc=pc: t.transpose(
                    PB[7][:, (pc * 4 + k) * 2:(pc * 4 + k) * 2 + 2],
                    MODTMP[0:2, k * 128:(k + 1) * 128],
                    IDF[0:2, 0:2]), w, sig=(k == 3))
        t_modf = S(lambda s: s.copy(MODF[:, 0:96], PB[7][:, 0:96]),
                   [ttr_all], sig=True)
        t_d = []
        t_d.append(V(lambda v: v.tensor_scalar_add(DERV[:, 0:16], MODF[:, 16:32], 1.0),
                     [t_modf], sig=True))
        t_d.append(V(lambda v: v.tensor_scalar_add(DERV[:, 32:48], MODF[:, 64:80], 1.0),
                     [t_modf], sig=True))
        t_mod_done = None
        for (src0, dst0) in ((32, 16), (80, 48)):
            ta = S(lambda s, src0=src0: s.activation(
                GATE[:, GE:GE + 16], MODF[:, src0:src0 + 16], AF.Abs),
                [t_modf] + ([t_mod_done] if t_mod_done else []), sig=True)
            tb = V(lambda v: v.tensor_scalar_max(GATE[:, GE:GE + 16],
                                                 GATE[:, GE:GE + 16], 0.25),
                   [ta] + t_d, sig=True)
            tc_ = V(lambda v, src0=src0: v.tensor_scalar(
                GATE[:, GC:GC + 16], MODF[:, src0:src0 + 16], 0.0, None, ALU.is_ge),
                [t_modf], sig=True)
            td = V(lambda v: v.tensor_scalar(GATE[:, GC:GC + 16], GATE[:, GC:GC + 16],
                                             2.0, -1.0, ALU.mult, ALU.add),
                   [tc_], sig=True)
            t_mod_done = V(lambda v, dst0=dst0: v.tensor_tensor(
                DERV[:, dst0:dst0 + 16], GATE[:, GE:GE + 16], GATE[:, GC:GC + 16],
                ALU.mult), [tb, td], sig=True)

        dump("modf", MODF, [t_modf])
        dump("derv", DERV, [t_mod_done])

        stage_list = []
        for x in range(NX):
            src = (suTh[x] if x < 2 else ruTh[x - 2]).rearrange("(kc p) e -> p kc e", p=128)
            stage_list.append((src[:, 0:4], wup[x][:, 0:2048], 4))
            stage_list.append((src[:, 4:8], wup[x][:, 2048:4096], 4))
        for x in range(NX):
            src = (sdTh[x] if x < 2 else rdTh[x - 2]).rearrange("(ec p) h -> p ec h", p=128)
            stage_list.append((src[:, 0:2], wdn[x][:, 0:2048], 2))
            stage_list.append((src[:, 2:4], wdn[x][:, 2048:4096], 2))
        sst = {"i": 0, "wr": []}

        def emit_staging(nmax):
            cnt = 0
            while sst["i"] < len(stage_list) and cnt < nmax:
                i = sst["i"]
                src, dst, a = stage_list[i]
                par = i % 2
                slab = U2[:, STG + par * 2048:STG + (par + 1) * 2048]
                w = [sst["wr"][i - 2]] if i >= 2 else []
                Q(lambda q, d=slab, s=src, a=a: q.dma_start(
                    out=d.rearrange("p (a b) -> p a b", a=a), in_=s), w, dma=qsc[par])
                t_rd = (qsc[par], qsc[par].n)
                Y(lambda y, d=dst, s=slab: y.dma_start(out=d, in_=s),
                  [t_rd], dma=ysm[par])
                sst["wr"].append((ysm[par], ysm[par].n))
                sst["i"] += 1
                cnt += 1

        xsqwar = {0: None, 1: None}
        rowb_war = {}
        stat_toks = {}

        def emit_stats(kind, g, xt_ap, x_ready, bank_idx, extra=()):
            pb = PB[bank_idx]
            mcol = (0 if kind == "a" else 64) + g * 4
            rcol = (32 if kind == "a" else 96) + g * 4
            tz = T(lambda t: t.matmul(
                pb[:, 0:8], CON[0:1, C_ZERO:C_ZERO + 128],
                CON[0:1, C_OROW:C_OROW + 8],
                start=True, stop=False, skip_group_check=True),
                [t_const, pb_free[bank_idx]], sig=False)
            t_msqf = [None] * 8
            tmm = None
            for fc in range(8):
                for st_ in range(4):
                    w = [x_ready] if (fc == 0 and st_ == 0) else []
                    tmm = T(lambda t, st_=st_, fc=fc: t.matmul(
                        pb[:, st_:st_ + 1],
                        xt_ap[:, fc, st_ * 128:(st_ + 1) * 128],
                        CON[:, C_O1024:C_O1024 + 1],
                        start=False, stop=False, skip_group_check=True),
                        w, sig=(st_ == 3 and fc == 7))
            t_m_done = tmm
            for fc in range(8):
                sl = fc % 2
                w = [x_ready, xsqwar[sl] if fc < 2 else t_msqf[fc - 2]] + list(extra)
                t_xsq = V(lambda v, fc=fc, sl=sl: v.tensor_tensor(
                    U2[:, XSQA + sl * 512:XSQA + (sl + 1) * 512],
                    xt_ap[:, fc], xt_ap[:, fc], ALU.mult), w, sig=True)
                for st_ in range(4):
                    w = [t_xsq] if st_ == 0 else []
                    tmm = T(lambda t, st_=st_, fc=fc, sl=sl: t.matmul(
                        pb[:, 4 + st_:5 + st_],
                        U2[:, XSQA + sl * 512 + st_ * 128:XSQA + sl * 512 + (st_ + 1) * 128],
                        CON[:, C_O1024:C_O1024 + 1],
                        start=False, stop=False, skip_group_check=True),
                        w, sig=(st_ == 3))
                t_msqf[fc] = tmm
            xsqwar[0], xsqwar[1] = t_msqf[6], t_msqf[7]
            t_mc = V(lambda v, mcol=mcol: v.tensor_copy(STATS[:, mcol:mcol + 4],
                                                        pb[:, 0:4]),
                     [t_m_done, t_msqf[7]], sig=True)
            t_m2 = V(lambda v, mcol=mcol: v.tensor_tensor(
                GATE[:, GS + 16:GS + 20],
                STATS[:, mcol:mcol + 4], STATS[:, mcol:mcol + 4], ALU.mult),
                [t_mc], sig=True)
            t_vv = V(lambda v: v.scalar_tensor_tensor(
                GATE[:, GS + 20:GS + 24], pb[:, 4:8], EPS,
                GATE[:, GS + 16:GS + 20], ALU.add, ALU.subtract),
                [t_m2, t_msqf[7]], sig=True)
            t_rc = V(lambda v: v.reciprocal(GATE[:, GS + 20:GS + 24],
                                            GATE[:, GS + 20:GS + 24]), [t_vv], sig=True)
            pb_free[bank_idx] = t_rc
            t_sq = S(lambda s, rcol=rcol: s.sqrt(STATS[:, rcol:rcol + 4],
                                                 GATE[:, GS + 20:GS + 24]),
                     [t_rc], sig=True)
            rb = (g % 2) * 256 if kind == "a" else 512 + g * 256
            ttm = T(lambda t, mcol=mcol: t.transpose(
                pb[0:4, 8:136], STATS[:, mcol:mcol + 4], IDFB[:, :]),
                [t_mc, t_vv], sig=True)
            tem = S(lambda s, rb=rb: s.copy(ROWB[0:4, rb:rb + 128], pb[0:4, 8:136]),
                    [ttm, rowb_war.get((kind, g % 2))], sig=True)
            ttr = T(lambda t, rcol=rcol: t.transpose(
                pb[0:4, 136:264], STATS[:, rcol:rcol + 4], IDFB[:, :]),
                [t_sq], sig=True)
            ter = S(lambda s, rb=rb: s.copy(ROWB[0:4, rb + 128:rb + 256],
                                            pb[0:4, 136:264]), [ttr], sig=True)
            pb_free[bank_idx] = ter
            stat_toks[(kind, g)] = (tem, ter, rb)
            return tmm, ter

        def emit_apply(kind, g, xt_ap, hbuf, hbase, onep_base, sh_base, b_,
                       x_ready, pb_m, pb_m_idx, pb_v, pb_v_idx, h_war,
                       aff_on_s=False):
            tem, ter, rb = stat_toks[(kind, g)]
            tbm = None
            tbr = None
            for st_ in range(4):
                w = [tem]
                if st_ == 0:
                    w.append(pb_free[pb_m_idx])
                tbm = T(lambda t, st_=st_, rb=rb: t.matmul(
                    pb_m[:, st_ * 128:(st_ + 1) * 128],
                    SELF4[0:4, st_ * 128:(st_ + 1) * 128],
                    ROWB[0:4, rb:rb + 128],
                    start=True, stop=True, skip_group_check=True),
                    w, sig=(st_ == 3))
                w = [ter]
                if st_ == 0:
                    w.append(pb_free[pb_v_idx])
                tbr = T(lambda t, st_=st_, rb=rb: t.matmul(
                    pb_v[:, st_ * 128:(st_ + 1) * 128],
                    SELF4[0:4, st_ * 128:(st_ + 1) * 128],
                    ROWB[0:4, rb + 128:rb + 256],
                    start=True, stop=True, skip_group_check=True),
                    w, sig=(st_ == 3))
            rowb_war[(kind, g % 2)] = tbr
            t_aff = [None] * 8
            t_sub_l = [None] * 8
            t_mul_l = [None] * 8
            for fc in range(8):
                sl = fc % 2
                w = [tbm, x_ready]
                if fc >= 2:
                    w.append(t_aff[fc - 2])
                else:
                    w.append(lnp["aff"][6 + fc])
                t_sub = V(lambda v, fc=fc, sl=sl: v.tensor_tensor(
                    SCRF[:, TA[sl]:TA[sl] + 512], xt_ap[:, fc], pb_m, ALU.subtract),
                    w, sig=True)
                t_sub_l[fc] = t_sub
                t_mul = V(lambda v, fc=fc, sl=sl: v.tensor_tensor(
                    SCRF[:, TB[sl]:TB[sl] + 512], SCRF[:, TA[sl]:TA[sl] + 512],
                    pb_v, ALU.mult), [tbr, t_sub], sig=True)
                t_mul_l[fc] = t_mul
                if aff_on_s:
                    t_aff[fc] = S(lambda s, fc=fc, sl=sl: s.activation(
                        hbuf[:, hbase + fc * 512:hbase + (fc + 1) * 512],
                        SCRF[:, TB[sl]:TB[sl] + 512], AF.Identity,
                        bias=MODF[:, sh_base + fc * 2 + b_:sh_base + fc * 2 + b_ + 1],
                        scale=DERV[:, onep_base + fc * 2 + b_:onep_base + fc * 2 + b_ + 1]),
                        [t_mul, t_mod_done, h_war if fc == 0 else None], sig=True)
                else:
                    t_aff[fc] = V(lambda v, fc=fc, sl=sl: v.tensor_scalar(
                        hbuf[:, hbase + fc * 512:hbase + (fc + 1) * 512],
                        SCRF[:, TB[sl]:TB[sl] + 512],
                        DERV[:, onep_base + fc * 2 + b_:onep_base + fc * 2 + b_ + 1],
                        MODF[:, sh_base + fc * 2 + b_:sh_base + fc * 2 + b_ + 1],
                        ALU.mult, ALU.add),
                        [t_mod_done, h_war if fc == 0 else None], sig=True)
            lnp["aff"] = t_aff
            pb_free[pb_m_idx] = t_sub_l[7]
            pb_free[pb_v_idx] = t_mul_l[7]
            return t_aff

        atn = {"g": None, "t_on_last": None, "cp_tok": {}, "t_kv": {},
               "t_o": {}, "t_o_b": {}, "xpost": [None] * NAT,
               "h_last": None}
        if need_stats0:
            emit_stats("a", 0, XT[:, 0:4096].rearrange("p (fc t) -> p fc t", fc=8),
                       (qx[0], 16), 1)

        def emit_wo_dc(gp, dc, res_toks):
            if gp is None:
                return
            sl = gp % 2
            s_ = gp // 4
            bidx = dc % 2
            tmm = None
            for hc in range(8):
                w = [atn["t_on_last"], t_wo_scaled]
                if hc == 0:
                    w.append(pb_free[bidx])
                rhs = U2[:, ONr:ONr + 4096].rearrange(
                    "p (c h t) -> p c h t", c=4, h=8)[:, :, hc, :]
                tmm = T(lambda t, dc=dc, hc=hc, bidx=bidx, rhs=rhs: t.matmul(
                    PB[bidx],
                    U1[:, WO + hc * 1024 + dc * 128:WO + hc * 1024 + (dc + 1) * 128],
                    rhs, start=(hc == 0), stop=(hc == 7), skip_group_check=True),
                    w, sig=(hc == 7))
            tres = V(lambda v, dc=dc, sl=sl, bidx=bidx, s_=s_: v.scalar_tensor_tensor(
                XT[:, sl * 4096 + dc * 512:sl * 4096 + (dc + 1) * 512],
                PB[bidx], DERV[:, 16 + dc * 2 + s_:16 + dc * 2 + s_ + 1],
                XT[:, sl * 4096 + dc * 512:sl * 4096 + (dc + 1) * 512],
                ALU.mult, ALU.add), [tmm, t_mod_done], sig=True)
            pb_free[bidx] = tres
            res_toks.append(tres)

        def emit_wo_fin(gp, res_toks):
            if gp is None:
                return
            sl = gp % 2
            Y(lambda y, sl=sl, g=gp: y.dma_start(
                out=xscr[:, :, g * TILE:(g + 1) * TILE],
                in_=XT[:, sl * 4096:(sl + 1) * 4096].rearrange("p (fc t) -> p fc t", fc=8)),
              res_toks, dma=yxp[sl])
            atn["xpost"][gp] = (yxp[sl], yxp[sl].n)
            if cfg.n_ffn_tiles > gp:
                xap = XT[:, sl * 4096:(sl + 1) * 4096].rearrange(
                    "p (fc t) -> p fc t", fc=8)
                tstat, _ = emit_stats("f", gp, xap, res_toks[-1], 1)
                xt_free[sl] = [(yxp[sl], yxp[sl].n), tstat]
            else:
                xt_free[sl] = (yxp[sl], yxp[sl].n)

        GBKS = [4, 5, 0, 1]

        def proj_acc(dst_base, wbase, t_h):
            t_ev = [None] * 8
            tq_last = [None] * 8

            def acc_group(grp):
                for kc in range(8):
                    for i4 in range(4):
                        dc = grp * 4 + i4
                        bidx = GBKS[i4]
                        w = [t_h[kc]]
                        if kc == 0:
                            w.append(pb_free[bidx])
                        tq = T(lambda t, dc=dc, kc=kc, bidx=bidx, wbase=wbase: t.matmul(
                            PB[bidx],
                            U1[:, wbase + kc * 1024 + dc * 128:wbase + kc * 1024 + (dc + 1) * 128],
                            U2[:, HBr + kc * 512:HBr + (kc + 1) * 512],
                            start=(kc == 0), stop=(kc == 7), skip_group_check=True),
                            w, sig=(kc == 7))
                        if kc == 7:
                            tq_last[dc] = tq

            def evac(dc):
                bidx = GBKS[dc % 4]
                t_ev[dc] = V(lambda v, dc=dc, bidx=bidx, dst_base=dst_base: v.tensor_copy(
                    U2[:, dst_base + dc * 512:dst_base + (dc + 1) * 512], PB[bidx]),
                    [tq_last[dc]], sig=True)
                pb_free[bidx] = t_ev[dc]

            acc_group(0)
            for dc in range(4):
                evac(dc)
            acc_group(1)
            for dc in range(4, 8):
                evac(dc)
            return t_ev

        fmst = {"last2": None, "last3": None}

        def fm_chain(dst_base, t_ev, t_comb, t_a1, dc):
            isq = dst_base == QP
            fmo = 0 if isq else 256
            bias1 = SMALL[:, 0:1] if isq else SMALL[:, 2:3]
            bias2 = SMALL[:, 1:2] if isq else SMALL[:, 3:4]
            if True:
                bidx = 2 + dc % 2
                w = [t_ev[dc], t_fmw, t_b1q, pb_free[bidx]]
                tp1 = T(lambda t, dc=dc, bidx=bidx, fmo=fmo, dst_base=dst_base: t.matmul(
                    PB[bidx], FMW[:, fmo:fmo + 128],
                    U2[:, dst_base + dc * 512:dst_base + (dc + 1) * 512],
                    start=True, stop=True, skip_group_check=True), w, sig=True)
                w = [tp1, fmst["a1w" + str(dc % 2)] if ("a1w" + str(dc % 2)) in fmst else None]
                t_a1[dc] = S(lambda s, dc=dc, bidx=bidx, b1=bias1: s.activation(
                    U2[:, A1SL + (dc % 2) * 512:A1SL + (dc % 2 + 1) * 512],
                    PB[bidx], AF.Identity, bias=b1), w, sig=True)
                tp2 = T(lambda t, dc=dc, bidx=bidx, fmo=fmo, dst_base=dst_base: t.matmul(
                    PB[bidx], FMW[:, fmo + 128:fmo + 256],
                    U2[:, dst_base + dc * 512:dst_base + (dc + 1) * 512],
                    start=True, stop=True, skip_group_check=True),
                    [t_a1[dc]], sig=True)
                t_comb[dc] = V(lambda v, dc=dc, bidx=bidx, b2=bias2, dst_base=dst_base:
                               v.scalar_tensor_tensor(
                    U2[:, dst_base + dc * 512:dst_base + (dc + 1) * 512],
                    PB[bidx], b2,
                    U2[:, A1SL + (dc % 2) * 512:A1SL + (dc % 2 + 1) * 512],
                    ALU.add, ALU.mult), [tp2], sig=True)
                pb_free[bidx] = t_comb[dc]
                fmst["a1w" + str(dc % 2)] = t_comb[dc]

        apply_pend = {}
        for g in range(cfg.n_attn_tiles):
            s_ = g // 4
            sl = g % 2
            base = (g % 4) * CPT
            x_ready = (qx[sl], qx[sl].n)
            xt_ap = XT[:, sl * 4096:(sl + 1) * 4096].rearrange("p (fc t) -> p fc t", fc=8)

            if g == 0:
                t_h = emit_apply("a", 0, xt_ap, U2, HBr, 0, 0, s_,
                                 x_ready, PB[2], 2, PB[3], 3, atn["h_last"],
                                 aff_on_s=True)
            else:
                t_h = apply_pend[g]

            if g == 0:
                dump("h0", U2[:, HBr:HBr + 4096], [t_h[7]])
            q_ev = proj_acc(QP, WQ, t_h)
            k_ev = proj_acc(KP, WK, t_h)
            tq_comb = [None] * 8
            tk_comb = [None] * 8
            tq_a1 = [None] * 8
            tk_a1 = [None] * 8
            t_vev = [None] * 8
            t_ktev = [None] * 4

            def vproj_block(i):
                st_, nh = i // 2, i % 2
                bidx = 4 + i % 2
                tv = None
                for kc in range(8):
                    w = [t_h[kc]]
                    if kc == 0:
                        w.append(pb_free[bidx])
                    tv = T(lambda t, st_=st_, nh=nh, kc=kc, bidx=bidx: t.matmul(
                        PB[bidx],
                        U2[:, HBr + kc * 512 + st_ * 128:HBr + kc * 512 + (st_ + 1) * 128],
                        U1[:, WV + kc * 1024 + nh * 512:WV + kc * 1024 + (nh + 1) * 512],
                        start=(kc == 0), stop=(kc == 7), skip_group_check=True),
                        w, sig=(kc == 7))
                t_vev[i] = S(lambda s, st_=st_, nh=nh, bidx=bidx: s.copy(
                    U2[:, VTr + st_ * 1024 + nh * 512:VTr + st_ * 1024 + (nh + 1) * 512],
                    PB[bidx]), [tv], sig=True)
                pb_free[bidx] = t_vev[i]

            def kt_batch(batch):
                ttr = None
                for j in range(8):
                    hh = (batch * 8 + j) // 4
                    pr = (batch * 8 + j) % 4
                    w = [tk_comb[hh], t_const]
                    if j == 0:
                        w.append(pb_free[3])
                    ttr = T(lambda t, hh=hh, pr=pr, j=j: t.transpose(
                        PB[3].bitcast(BF16)[:, j * 128:(j + 1) * 128],
                        U2[:, KP + hh * 512 + pr * 128:KP + hh * 512 + (pr + 1) * 128],
                        CON[:, C_ID:C_ID + 128]), w, sig=(j == 7))
                t_ktev[batch] = S(lambda s, batch=batch: s.copy(
                    U2[:, KTr + batch * 1024:KTr + (batch + 1) * 1024],
                    PB[3].bitcast(BF16)[:, 0:1024]), [ttr], sig=True)
                pb_free[3] = t_ktev[batch]

            for dc in range(8):
                fm_chain(QP, q_ev, tq_comb, tq_a1, dc)
                vproj_block(dc)
            wo_res = []
            for dc in range(8):
                fm_chain(KP, k_ev, tk_comb, tk_a1, dc)
                if dc % 2 == 1:
                    kt_batch(dc // 2)
                emit_wo_dc(atn["g"], dc, wo_res)
            emit_wo_fin(atn["g"], wo_res)
            t_kt_done = t_ktev[3]
            t_v_done = t_vev[7]
            atn["h_last"] = t_v_done

            if g + 1 < cfg.n_attn_tiles:
                load_xtile_attn(g + 1)
            emit_staging(0 if getattr(cfg, 'no_stg_il', False) else 10)
            if g + 1 < cfg.n_attn_tiles:
                sl1 = (g + 1) % 2
                emit_stats("a", g + 1,
                           XT[:, sl1 * 4096:(sl1 + 1) * 4096].rearrange(
                               "p (fc t) -> p fc t", fc=8),
                           (qx[sl1], qx[sl1].n), 1)

            if base == 0:
                for bi in (6, 7):
                    T(lambda t, bi=bi: t.matmul(
                        PB[bi], CON[0:1, C_ZERO:C_ZERO + 128],
                        CON[0:1, C_OROW:C_OROW + 512],
                        start=True, stop=False, skip_group_check=True),
                        [t_const], sig=False)

            a_free = {2: pb_free[2], 3: pb_free[3]}
            o_free = {4: pb_free[4], 5: pb_free[5]}
            t_mask = {}
            rms = {"sq": {}, "msq_tok": {}, "small": {}, "rsb": {},
                   "bc_war": pb_free[0], "tr_war": pb_free[1],
                   "rsb_war": None, "msq_war": pb_free[1]}

            def kv_mms(c):
                n = base + c
                tkv = None
                for h in range(8):
                    w = []
                    if h == 0:
                        w = [t_kt_done, t_v_done]
                        if n % NCHS != 0:
                            w.append(atn["cp_tok"].get(n))
                    tkv = T(lambda t, c=c, h=h: t.matmul(
                        PB[6 + h // 4][:, (h % 4) * 128:(h % 4 + 1) * 128],
                        U2[:, KTr + h * 512 + c * 128:KTr + h * 512 + (c + 1) * 128],
                        U2[:, VTr + c * 1024 + h * 128:VTr + c * 1024 + (h + 1) * 128],
                        start=False, stop=False, skip_group_check=True),
                        w, sig=(h == 7))
                atn["t_kv"][n] = tkv

            def state_copy(n):
                if n % NCHS == 0:
                    return
                par = n % 2
                w = [atn["t_kv"][n - 1], atn["t_o"].get(n - 2)]
                S(lambda s, par=par: s.copy(SBFS[:, par * 1024:par * 1024 + 512], PB[6]),
                  w, sig=True)
                atn["cp_tok"][n] = S(lambda s, par=par: s.copy(
                    SBFS[:, par * 1024 + 512:par * 1024 + 1024], PB[7]), [], sig=True)

            def a_mms(c):
                toks = {}
                for h in range(8):
                    bidx = 2 + h // 4
                    w = [tq_comb[h], tk_comb[h]]
                    if h % 4 == 0:
                        w.append(a_free[bidx])
                    ta = T(lambda t, c=c, h=h, bidx=bidx: t.matmul(
                        PB[bidx][:, (h % 4) * 128:(h % 4 + 1) * 128],
                        U2[:, KP + h * 512 + c * 128:KP + h * 512 + (c + 1) * 128],
                        U2[:, QP + h * 512 + c * 128:QP + h * 512 + (c + 1) * 128],
                        start=True, stop=True, skip_group_check=True),
                        w, sig=(h == 3 or h == 7))
                    if h in (3, 7):
                        toks[bidx] = ta
                return toks

            def mask_op(c, toks):
                n = base + c
                atsl = ATSL + (c % 2) * 1024
                tms = []
                for bi in (2, 3):
                    w = [toks[bi], t_const, atn["t_o"].get(n - 2)]
                    tm = V(lambda v, bi=bi, atsl=atsl: v.tensor_tensor(
                        U2[:, atsl + (bi - 2) * 512:atsl + (bi - 1) * 512],
                        PB[bi], CON[:, C_MSK:C_MSK + 512], ALU.mult),
                        w, sig=True)
                    a_free[bi] = tm
                    tms.append(tm)
                t_mask[n] = tms

            def o_mms(c):
                n = base + c
                if n % NCHS != 0:
                    for h in range(8):
                        bidx = 4 + h // 4
                        w = [tq_comb[h]]
                        if h % 4 == 0:
                            w += [o_free[bidx], atn["cp_tok"][n]]
                        T(lambda t, c=c, h=h, n=n, bidx=bidx: t.matmul(
                            PB[bidx][:, (h % 4) * 128:(h % 4 + 1) * 128],
                            SBFS[:, (n % 2) * 1024 + h * 128:(n % 2) * 1024 + (h + 1) * 128],
                            U2[:, QP + h * 512 + c * 128:QP + h * 512 + (c + 1) * 128],
                            start=(h % 4 == 0), stop=False, skip_group_check=True),
                            w, sig=False)
                to = {}
                st_flag = (n % NCHS == 0)
                for h in range(8):
                    bidx = 4 + h // 4
                    w = [t_mask[n][h // 4]]
                    if st_flag and h % 4 == 0:
                        w.append(o_free[bidx])
                    to[bidx] = T(lambda t, c=c, h=h, bidx=bidx, st_flag=st_flag: t.matmul(
                        PB[bidx][:, (h % 4) * 128:(h % 4 + 1) * 128],
                        U2[:, VTr + c * 1024 + h * 128:VTr + c * 1024 + (h + 1) * 128],
                        U2[:, ATSL + (c % 2) * 1024 + h * 128:ATSL + (c % 2) * 1024 + (h + 1) * 128],
                        start=(st_flag and h % 4 == 0), stop=True, skip_group_check=True),
                        w, sig=(h == 3 or h == 7))
                atn["t_o"][n] = to[5]
                atn["t_o_b"][n] = (to[4], to[5])

            def o_evac(c):
                n = base + c
                onb = ONr + c * 1024
                o2b = O2SL + (c % 2) * 1024
                t4, t5 = atn["t_o_b"][n]
                cp4 = V(lambda v, onb=onb: v.tensor_copy(U2[:, onb:onb + 512], PB[4]),
                        [t4], sig=True)
                cp5 = V(lambda v, onb=onb: v.tensor_copy(
                    U2[:, onb + 512:onb + 1024], PB[5]), [t5], sig=True)
                sq4 = V(lambda v, onb=onb, o2b=o2b: v.tensor_tensor(
                    U2[:, o2b:o2b + 512], U2[:, onb:onb + 512], U2[:, onb:onb + 512],
                    ALU.mult), [cp4, rms["msq_tok"].get(c - 2)], sig=True)
                sq5 = V(lambda v, onb=onb, o2b=o2b: v.tensor_tensor(
                    U2[:, o2b + 512:o2b + 1024], U2[:, onb + 512:onb + 1024],
                    U2[:, onb + 512:onb + 1024], ALU.mult), [cp5], sig=True)
                o_free[4] = cp4
                o_free[5] = cp5
                rms["sq"][c] = (sq4, sq5)

            def rms_msq(c):
                o2b = O2SL + (c % 2) * 1024
                sq4, sq5 = rms["sq"][c]
                mc = 300 + (c % 2) * 8
                tq = None
                for h in range(8):
                    w = [sq4 if h < 4 else sq5]
                    if h == 0:
                        w.append(rms["msq_war"])
                    tq = T(lambda t, h=h, mc=mc, o2b=o2b: t.matmul(
                        PB[1][:, mc + h:mc + h + 1],
                        U2[:, o2b + h * 128:o2b + (h + 1) * 128],
                        CON[:, C_O128:C_O128 + 1],
                        start=True, stop=True, skip_group_check=True),
                        w, sig=(h == 7))
                rms["msq_tok"][c] = tq

            def rms_small(c):
                mc = 300 + (c % 2) * 8
                gc_ = GS + 24 + (c % 2) * 8
                t1 = V(lambda v, mc=mc, gc_=gc_: v.tensor_scalar_add(
                    GATE[:, gc_:gc_ + 8], PB[1][:, mc:mc + 8], EPS),
                    [rms["msq_tok"][c]], sig=True)
                t2 = V(lambda v, gc_=gc_: v.reciprocal(
                    GATE[:, gc_:gc_ + 8], GATE[:, gc_:gc_ + 8]), [t1], sig=True)
                t3 = S(lambda s, gc_=gc_: s.sqrt(
                    GATE[:, gc_:gc_ + 8], GATE[:, gc_:gc_ + 8]), [t2], sig=True)
                rms["small"][c] = t3
                rms["msq_war"] = t1

            def rms_tr(c):
                gc_ = GS + 24 + (c % 2) * 8
                ttr = T(lambda t, gc_=gc_: t.transpose(
                    PB[1][0:8, 384:512], GATE[:, gc_:gc_ + 8], IDFB[:, :]),
                    [rms["small"][c], rms["tr_war"], t_const], sig=True)
                trsb = S(lambda s: s.copy(RW8[0:8, 0:128],
                                          PB[1][0:8, 384:512]),
                         [ttr, rms["rsb_war"]], sig=True)
                rms["rsb"][c] = trsb
                rms["tr_war"] = trsb

            def rms_bcast(c):
                trsb = rms["rsb"][c]
                last_mm = None
                for r in range(2):
                    mm = None
                    for hh in range(4):
                        h = r * 4 + hh
                        w = [trsb]
                        if hh == 0:
                            w.append(rms["bc_war"])
                        mm = T(lambda t, h=h, hh=hh: t.matmul(
                            PB[0][:, hh * 128:(hh + 1) * 128],
                            CONF8[0:8, h * 128:(h + 1) * 128],
                            RW8[0:8, 0:128],
                            start=True, stop=True, skip_group_check=True),
                            w, sig=(hh == 3))
                    ton = V(lambda v, c=c, r=r: v.tensor_tensor(
                        U2[:, ONr + c * 1024 + r * 512:ONr + c * 1024 + (r + 1) * 512],
                        U2[:, ONr + c * 1024 + r * 512:ONr + c * 1024 + (r + 1) * 512],
                        PB[0], ALU.mult), [mm], sig=True)
                    rms["bc_war"] = ton
                    last_mm = mm
                rms["rsb_war"] = last_mm
                atn["t_on_last"] = rms["bc_war"]

            for c in range(CPT):
                kv_mms(c)
                toks = a_mms(c)
                state_copy(base + c + 1)
                mask_op(c, toks)
                if c >= 2:
                    rms_tr(c - 2)
                o_mms(c)
                if c >= 2:
                    rms_bcast(c - 2)
                o_evac(c)
                if c >= 1:
                    rms_msq(c - 1)
                    rms_small(c - 1)
            rms_msq(CPT - 1)
            rms_small(CPT - 1)
            rms_tr(CPT - 2)
            rms_bcast(CPT - 2)
            rms_tr(CPT - 1)

            pb_free[2] = a_free[2]
            pb_free[3] = a_free[3]
            pb_free[4] = o_free[4]
            pb_free[5] = o_free[5]

            if g + 1 < cfg.n_attn_tiles:
                sl1 = (g + 1) % 2
                apply_pend[g + 1] = emit_apply(
                    "a", g + 1,
                    XT[:, sl1 * 4096:(sl1 + 1) * 4096].rearrange(
                        "p (fc t) -> p fc t", fc=8),
                    U2, HBr, 0, 0, (g + 1) // 4, (qx[sl1], qx[sl1].n),
                    PB[2], 2, PB[3], 3, atn["h_last"], aff_on_s=True)

            rms_bcast(CPT - 1)
            pb_free[0] = rms["bc_war"]
            pb_free[1] = rms["rsb"][CPT - 1]
            if g == 0:
                dump("vt0", U2[:, VTr:VTr + 4096], [t_v_done])
                dump("on0", U2[:, ONr:ONr + 4096], [atn["t_on_last"]])

            atn["g"] = g

        wo_res = []
        for dc in range(8):
            emit_wo_dc(atn["g"], dc, wo_res)
        emit_wo_fin(atn["g"], wo_res)
        emit_staging(100)

        t_attn_T = (mk.ec["T"], mk.ec["T"].n)
        t_attn_V = (mk.ec["V"], mk.ec["V"].n)
        t_attn_S = (mk.ec["S"], mk.ec["S"].n)
        stg_done = sst["wr"][-2:] if sst["wr"] else []

        if cfg.n_ffn_tiles > 0:
            T(lambda t: t.ldweights(CON[:, C_ID:C_ID + 128]),
              [t_attn_V, t_attn_S], sig=True)
            V(lambda v: v.memset(GATE[:, GS + 16:GS + 17], 0.0),
              [t_attn_T, t_attn_S], sig=True)
            S(lambda s: s.memzero(SMALL[:, 14:15]),
              [t_attn_T, t_attn_V], sig=True)

        ffn = {"t_h2": {}, "t_gt": {}, "gt_used": {}, "t_dn_bank": {},
               "ws_free": {0: None, 1: None, 2: None, 3: None},
               "xf_tok": {}, "xf_used": {}, "act_ready": {}, "h2_used": {}}

        def load_xtile_ffn(g):
            sl = g % 2
            dst = XT[:, sl * 4096:(sl + 1) * 4096].rearrange("p (fc t) -> p fc t", fc=8)
            w = [atn["xpost"][g] if g < len(atn["xpost"]) else None,
                 ffn["xf_used"].get(g - 2)]
            w += xt_free[sl] if isinstance(xt_free[sl], list) else [xt_free[sl]]
            if getattr(cfg, "xload_y", False):
                Y(lambda y, d=dst, g=g: y.dma_start(
                    out=d, in_=xscr[:, :, g * TILE:(g + 1) * TILE]), w, dma=yx[sl])
            else:
                Q(lambda q, d=dst, g=g: q.dma_start(
                    out=d, in_=xscr[:, :, g * TILE:(g + 1) * TILE]), w, dma=yx[sl])
            ffn["xf_tok"][g] = (yx[sl], yx[sl].n)

        def emit_lnb(g):
            if g >= cfg.n_ffn_tiles:
                return
            sl = g % 2
            b_ = g // 4
            xt_ap = XT[:, sl * 4096:(sl + 1) * 4096].rearrange("p (fc t) -> p fc t", fc=8)
            ffn["t_h2"][g] = emit_apply(
                "f", g, xt_ap, U1, H2r[sl], 32, 48, b_, ffn["xf_tok"][g],
                PB[0], 0, PB[2], 2, ffn["h2_used"].get(g - 2))

        def emit_router(g):
            if g >= cfg.n_ffn_tiles:
                return
            sl = g % 2
            tl = None
            for st_ in range(4):
                for kc in range(8):
                    w = [ffn["t_h2"][g][kc], t_qld]
                    if st_ == 0 and kc == 0:
                        w.append(pb_free[3])
                    tl = T(lambda t, st_=st_, kc=kc, sl=sl: t.matmul(
                        PB[3][:, st_ * 16:(st_ + 1) * 16],
                        U1[:, H2r[sl] + kc * 512 + st_ * 128:H2r[sl] + kc * 512 + (st_ + 1) * 128],
                        RT[:, kc * 16:(kc + 1) * 16],
                        start=(kc == 0), stop=(kc == 7), skip_group_check=True),
                        w, sig=(st_ == 3 and kc == 7))
            lseg = PB[3][:, 0:64].rearrange("p (gg n) -> p gg n", gg=4)
            t1 = V(lambda v: v.reduce_max(GATE[:, GS:GS + 4], lseg, axis=AX),
                   [tl], sig=True)
            eseg = GATE[:, GE:GE + 64].rearrange("p (gg n) -> p gg n", gg=4)
            t2 = V(lambda v: v.tensor_tensor(
                eseg, lseg, GATE[:, GS:GS + 4].unsqueeze(2).broadcast_to([128, 4, 16]),
                ALU.subtract), [t1], sig=True)
            pb_free[3] = t2
            t3 = S(lambda s: s.activation(GATE[:, GE:GE + 64], GATE[:, GE:GE + 64],
                                          AF.Exp), [t2], sig=True)
            t4 = V(lambda v: v.reduce_sum(GATE[:, GS + 4:GS + 8], eseg, axis=AX),
                   [t3], sig=True)
            t5 = V(lambda v: v.reciprocal(GATE[:, GS + 8:GS + 12],
                                          GATE[:, GS + 4:GS + 8]), [t4], sig=True)
            tcur = V(lambda v: v.tensor_copy(GATE[:, GC:GC + 64], GATE[:, GE:GE + 64]),
                     [t3], sig=True)
            cseg = GATE[:, GC:GC + 64].rearrange("p (gg n) -> p gg n", gg=4)
            mseg = GATE[:, GM:GM + 64].rearrange("p (gg n) -> p gg n", gg=4)
            for r in range(4):
                tcur = V(lambda v: v.reduce_max(GATE[:, GS + 12:GS + 16], cseg, axis=AX),
                         [tcur], sig=True)
                if r < 3:
                    tm = V(lambda v: v.tensor_tensor(
                        mseg, cseg,
                        GATE[:, GS + 12:GS + 16].unsqueeze(2).broadcast_to([128, 4, 16]),
                        ALU.is_ge), [tcur], sig=True)
                    tcur = V(lambda v: v.scalar_tensor_tensor(
                        GATE[:, GC:GC + 64], GATE[:, GM:GM + 64], -1e9,
                        GATE[:, GC:GC + 64], ALU.mult, ALU.add), [tm], sig=True)
            tg1 = V(lambda v: v.tensor_tensor(
                mseg, eseg,
                GATE[:, GS + 12:GS + 16].unsqueeze(2).broadcast_to([128, 4, 16]),
                ALU.is_ge), [tcur], sig=True)
            tg2 = V(lambda v: v.tensor_tensor(GATE[:, GM:GM + 64], GATE[:, GM:GM + 64],
                                              GATE[:, GE:GE + 64], ALU.mult),
                    [tg1], sig=True)
            gseg = GATE[:, GE:GE + 64].rearrange("p (gg n) -> p gg n", gg=4)
            tg3 = V(lambda v: v.tensor_tensor(
                gseg, mseg,
                GATE[:, GS + 8:GS + 12].unsqueeze(2).broadcast_to([128, 4, 16]),
                ALU.mult), [tg2, t5], sig=True)
            tg4 = V(lambda v: v.tensor_copy(GATB[:, :], GATE[:, GE:GE + 64]),
                    [tg3], sig=True)
            ffn["tg4"] = (tg4, g)

        def emit_router_gt(g):
            if g >= cfg.n_ffn_tiles:
                return
            sl = g % 2
            tg4, g_chk = ffn["tg4"]
            assert g_chk == g
            ttr = None
            for st_ in range(4):
                w = [tg4, t_const]
                if st_ == 0:
                    w.append(pb_free[5])
                    w.append(ffn["gt_used"].get(g - 2))
                ttr = T(lambda t, st_=st_: t.transpose(
                    PB[5].bitcast(BF16)[0:16, st_ * 128:(st_ + 1) * 128],
                    GATB[:, st_ * 16:(st_ + 1) * 16],
                    CON[:, C_ID:C_ID + 128]), w, sig=(st_ == 3))
            tge = S(lambda s, sl=sl: s.copy(GTT[0:16, sl * 512:(sl + 1) * 512],
                                            PB[5].bitcast(BF16)[0:16, 0:512]),
                    [ttr], sig=True)
            pb_free[5] = tge
            ffn["t_gt"][g] = tge

        def stream_w(x_, g, which, base):
            i = x_ % 3
            src = (wup if which == "up" else wdn)[x_]
            ctr = upc if which == "up" else dnc
            w = [(ctr, base + x_ - 2) if base + x_ - 2 > 0 else None]
            if g == 0:
                stg_idx = 2 * x_ + 1 if which == "up" else 2 * NX + 2 * x_ + 1
                w.append(sst["wr"][stg_idx] if stg_idx < len(sst["wr"]) else None)
                if x_ < 3:
                    w += [t_attn_T, t_attn_V, t_attn_S]
            if which == "up":
                Y(lambda y, d=U1[:, USL[i]:USL[i] + 4096], s=src: y.dma_start(
                    out=d, in_=s[:, :]), w, dma=yws[i])
                return (yws[i], yws[i].n)
            if i < 2:
                q_ = qws[i]
                Q(lambda q, d=U1[:, DSL[i]:DSL[i] + 4096], s=src: q.dma_start(
                    out=d, in_=s[:, :]), w, dma=q_)
                return (q_, q_.n)
            Y(lambda y, d=U1[:, DSL[2]:DSL[2] + 4096], s=src: y.dma_start(
                out=d, in_=s[:, :]), w, dma=yws[3])
            return (yws[3], yws[3].n)

        for g in range(cfg.n_ffn_tiles):
            sl = g % 2
            b_ = g // 4
            if g == 0:
                load_xtile_ffn(0)
                if cfg.n_ffn_tiles > 1:
                    load_xtile_ffn(1)
                emit_lnb(0)
                dump("h2t0", U1[:, H2r[0]:H2r[0] + 4096], [ffn["t_h2"][0][7]])
                emit_router(0)
                emit_router_gt(0)
            up_base, dn_base = upc.n, dnc.n
            lazy = getattr(cfg, "lazy_streams", False)
            if not lazy:
                up_toks = [stream_w(x_, g, "up", up_base) for x_ in range(NX)]
                dn_toks = [stream_w(x_, g, "dn", dn_base) for x_ in range(NX)]
            BUP = (6, 7, 1)
            for x_ in range(NX):
                t_ws = stream_w(x_, g, "up", up_base) if lazy else up_toks[x_]
                tup_last = None
                gel = [None] * ECH
                for ec in range(ECH):
                    bidx = BUP[(x_ * ECH + ec) % 3]
                    tup = None
                    for kc in range(8):
                        w = [ffn["t_h2"][g][kc]]
                        if kc == 0:
                            w += [t_ws, pb_free[bidx]]
                        if kc == 7 and ec == ECH - 1:
                            tup = mk.emit("T", lambda t, x_=x_, ec=ec, kc=kc, bidx=bidx,
                                          i=x_ % 3, sl=sl: t.matmul(
                                PB[bidx],
                                U1[:, USL[i] + kc * 512 + ec * 128:USL[i] + kc * 512 + (ec + 1) * 128],
                                U1[:, H2r[sl] + kc * 512:H2r[sl] + (kc + 1) * 512],
                                start=(kc == 0), stop=(kc == 7), skip_group_check=True),
                                w, inc=(upc, 1))
                        else:
                            tup = T(lambda t, x_=x_, ec=ec, kc=kc, bidx=bidx, i=x_ % 3, sl=sl:
                                    t.matmul(
                                PB[bidx],
                                U1[:, USL[i] + kc * 512 + ec * 128:USL[i] + kc * 512 + (ec + 1) * 128],
                                U1[:, H2r[sl] + kc * 512:H2r[sl] + (kc + 1) * 512],
                                start=(kc == 0), stop=(kc == 7), skip_group_check=True),
                                w, sig=(kc == 7))
                    gel[ec] = S(lambda s, x_=x_, ec=ec, bidx=bidx: s.activation(
                        U2[:, x_ * 2048 + ec * 512:x_ * 2048 + (ec + 1) * 512],
                        PB[bidx], AF.Gelu), [tup], sig=True)
                    pb_free[bidx] = gel[ec]
                    tup_last = tup
                if x_ >= 2:
                    e_idx = x_ - 2
                    tbc = T(lambda t, e_idx=e_idx, sl=sl: t.matmul(
                        PB[4], CON[0:16, C_SEL + e_idx * 128:C_SEL + (e_idx + 1) * 128],
                        GTT[0:16, sl * 512:(sl + 1) * 512],
                        start=True, stop=True, skip_group_check=True),
                        [ffn["t_gt"][g], pb_free[4], t_const], sig=True)
                    ffn["gt_used"][g] = tbc
                    tgb = S(lambda s: s.copy(GBS[:, :], PB[4]),
                            [tbc, ffn["act_ready"].get(x_ - 1)], sig=True)
                    if g == 0 and x_ == 2:
                        dump("gbs0", GBS, [tgb])
                        dump("gtt0", GTT[0:16, 0:512], [ffn["t_gt"][g]])
                    pb_free[4] = tgb
                    tmul = None
                    for ec in range(ECH):
                        tmul = V(lambda v, x_=x_, ec=ec: v.tensor_tensor(
                            U2[:, x_ * 2048 + ec * 512:x_ * 2048 + (ec + 1) * 512],
                            GBS[:, :],
                            U2[:, x_ * 2048 + ec * 512:x_ * 2048 + (ec + 1) * 512],
                            ALU.mult), [gel[ec], tgb], sig=(ec == ECH - 1))
                    ffn["act_ready"][x_] = tmul
                    if g == 0 and x_ == 2:
                        dump("act2", U2[:, 2 * 2048:3 * 2048], [tmul])
                else:
                    ffn["act_ready"][x_] = gel[ECH - 1]
                    if g == 0 and x_ == 0:
                        dump("act0", U2[:, 0:2048], [gel[ECH - 1]])
                if x_ == 3:
                    emit_lnb(g + 1)
                if x_ == 11:
                    emit_router(g + 1)
                if x_ == 15:
                    emit_router_gt(g + 1)
            ffn["h2_used"][g] = (upc, upc.n)
            for x_ in range(NX):
                t_ws = stream_w(x_, g, "dn", dn_base) if lazy else dn_toks[x_]
                tdn = None
                for hc in range(8):
                    for ec in range(ECH):
                        w = []
                        if ec == 0:
                            w = [t_ws, ffn["act_ready"][x_]]
                            if x_ == 0:
                                w.append(pb_free[hc])
                        if hc == 7 and ec == ECH - 1:
                            tdn = mk.emit("T", lambda t, x_=x_, hc=hc, ec=ec, i=x_ % 3:
                                          t.matmul(
                                PB[hc],
                                U1[:, DSL[i] + ec * 1024 + hc * 128:DSL[i] + ec * 1024 + (hc + 1) * 128],
                                U2[:, x_ * 2048 + ec * 512:x_ * 2048 + (ec + 1) * 512],
                                start=(x_ == 0 and ec == 0),
                                stop=(x_ == NX - 1 and ec == ECH - 1),
                                skip_group_check=True), w, inc=(dnc, 1))
                        else:
                            tdn = T(lambda t, x_=x_, hc=hc, ec=ec, i=x_ % 3: t.matmul(
                                PB[hc],
                                U1[:, DSL[i] + ec * 1024 + hc * 128:DSL[i] + ec * 1024 + (hc + 1) * 128],
                                U2[:, x_ * 2048 + ec * 512:x_ * 2048 + (ec + 1) * 512],
                                start=(x_ == 0 and ec == 0),
                                stop=(x_ == NX - 1 and ec == ECH - 1),
                                skip_group_check=True),
                                w, sig=(ec == ECH - 1 and x_ == NX - 1))
                        if x_ == NX - 1 and ec == ECH - 1:
                            ffn["t_dn_bank"][hc] = tdn
            ev_order = [6, 7, 0, 1, 2, 3, 4, 5]
            for k, hc in enumerate(ev_order):
                par = k % 2
                w = [ffn["t_dn_bank"][hc], ffn["xf_tok"][g], t_mod_done,
                     (so[par], so[par].n)]
                tev = V(lambda v, hc=hc, par=par, sl=sl, b_=b_: v.scalar_tensor_tensor(
                    SCRF[:, OST[par]:OST[par] + 512],
                    PB[hc], DERV[:, 48 + hc * 2 + b_:48 + hc * 2 + b_ + 1],
                    XT[:, sl * 4096 + hc * 512:sl * 4096 + (hc + 1) * 512],
                    ALU.mult, ALU.add), w, sig=True)
                pb_free[hc] = tev
                mk.emit("S", lambda s, hc=hc, par=par, g=g: s.dma_start(
                    out=outT[hc * 128:(hc + 1) * 128, g * TILE:(g + 1) * TILE],
                    in_=SCRF[:, OST[par]:OST[par] + 512]), [tev], dma=so[par])
            ffn["xf_used"][g] = pb_free[5]
            if g + 2 < cfg.n_ffn_tiles:
                load_xtile_ffn(g + 2)

        mk.emit("S", lambda s: s.memzero(SMALL[:, 15:16]),
                [(so[0], so[0].n), (so[1], so[1].n)])

        mk.check()
        for c in mk.ctrs:
            c.sem = st.enter_context(nc.semaphore(c.name))
        block = st.enter_context(nc.Block())
        mk.replay(block)
    return nc, dbg_names


def host_prep(inputs):
    c = np.ascontiguousarray
    x = inputs["L0_attn_hidden_states"]
    common = {
        "wqT": c(inputs["L0_attn_Wq"].T), "wkT": c(inputs["L0_attn_Wk"].T),
        "wvT": c(inputs["L0_attn_Wv"].T), "woT": c(inputs["L0_attn_Wo"].T),
        "fm1qT": c(inputs["L0_attn_fmq_w1"].T), "fm2qT": c(inputs["L0_attn_fmq_w2"].T),
        "fm1kT": c(inputs["L0_attn_fmk_w1"].T), "fm2kT": c(inputs["L0_attn_fmk_w2"].T),
        "fmq_b1": c(inputs["L0_attn_fmq_b1"].reshape(DK, 1)),
        "fmq_b2": c(inputs["L0_attn_fmq_b2"].reshape(DK, 1)),
        "fmk_b1": c(inputs["L0_attn_fmk_b1"].reshape(DK, 1)),
        "fmk_b2": c(inputs["L0_attn_fmk_b2"].reshape(DK, 1)),
        "rms_w": c(inputs["L0_attn_rms_w"].reshape(DK, 1)),
        "suT": c(inputs["L0_ffn_shared_up"].transpose(0, 2, 1)),
        "sdT": c(inputs["L0_ffn_shared_down"].transpose(0, 2, 1)),
        "ruT": c(inputs["L0_ffn_routed_up"].transpose(0, 2, 1)),
        "rdT": c(inputs["L0_ffn_routed_down"].transpose(0, 2, 1)),
        "routerT": c(inputs["L0_ffn_router_w"].T),
        "wmod": c(inputs["L0_wmod"]),
    }
    per_core = []
    for cc in range(8):
        d = dict(common)
        d["xT"] = c(x[2 * cc:2 * cc + 2].reshape(2 * TSEQ, HID).T)
        d["cT"] = c(inputs["L0_c"][2 * cc:2 * cc + 2].T)
        per_core.append(d)
    return per_core


def kernel(**inputs):
    from concourse.bass_utils import run_bass_kernel_spmd
    cfg = Cfg()
    nc, _ = build(cfg)
    in_maps = host_prep(inputs)
    res = run_bass_kernel_spmd(nc, in_maps, core_ids=list(range(8)))
    outs = []
    for cc in range(8):
        o = res.results[cc]["outT"]
        outs.append(np.ascontiguousarray(np.asarray(o).T).reshape(2, TSEQ, HID))
    return np.concatenate(outs, axis=0).astype(np.float32)


# version 16 (final)
# speedup vs baseline: 1.0513x; 1.0055x over previous
import numpy as np
import ml_dtypes
import concourse.bass as bass
import concourse.mybir as mybir
from contextlib import ExitStack

F32, BF16 = mybir.dt.float32, mybir.dt.bfloat16
AF, ALU = mybir.ActivationFunctionType, mybir.AluOpType
AX = mybir.AxisListType.X
bfd = ml_dtypes.bfloat16

HID, FC, H, DK, CH, TILE = 1024, 8, 8, 128, 128, 512
NSEQ, TSEQ = 2, 2048
CPT = TILE // CH
NAT, NFT = 8, 8
E, ECH, NX = 512, 4, 18
EPS = 1e-5
SCL = float(DK) ** -0.5
NCHS = 16

QP, KP, KTr, VTr, ONr, HBr = 0, 4096, 8192, 12288, 16384, 20480
A1SL, ATSL, O2SL, STG, XSQA = 24576, 25600, 27648, 29696, 33792
U2N = 36864
WQ, WK, WV, WO = 0, 8192, 16384, 24576
USL = [0, 4096, 24576]
DSL = [8192, 12288, 28672]
H2r = [16384, 20480]
U1N = 32768
TA = [0, 512]
TB = [1024, 1536]
RC = [2048, 2560]
OST = [3072, 3584]
OCP = [4096, 4608]
SCRFN = 5120
GE, GC, GM, GS = 0, 64, 128, 192
GATEN = 256
C_ID, C_O1024, C_O128, C_MSK = 0, 128, 256, 384
C_EPS, C_OROW, C_OK1, C_ZERO, C_ONES = 896, 1024, 1536, 1664, 1792
C_SEL = 1920
CONSTN = 4096


class Cfg:
    def __init__(self, n_attn_tiles=NAT, n_ffn_tiles=NFT, dbg=False):
        self.n_attn_tiles = n_attn_tiles
        self.n_ffn_tiles = n_ffn_tiles
        self.dbg = dbg


class Ctr:
    __slots__ = ("name", "n", "sem")
    def __init__(self, name):
        self.name, self.n, self.sem = name, 0, None


class MK:
    def __init__(self):
        self.ops = {e: [] for e in "TVSQY"}
        self.ctrs = []
        self.last_wait = {e: {} for e in "TVSQY"}
        self.ec = {e: self.ctr("ec" + e) for e in "TVSQY"}

    def ctr(self, name):
        c = Ctr(name)
        self.ctrs.append(c)
        return c

    def emit(self, eng, fn, waits=(), sig=False, dma=None, inc=None):
        ws = []
        lw = self.last_wait[eng]
        for t in waits:
            if t is None:
                continue
            c, v = t
            if v <= 0 or lw.get(c, 0) >= v:
                continue
            lw[c] = v
            ws.append((c, v))
        if inc is not None:
            inc[0].n += inc[1]
            inc = (inc[0], inc[1])
        elif dma is not None:
            dma.n += 16
            inc = (dma, 16)
        elif sig:
            c = self.ec[eng]
            c.n += 1
            inc = (c, 1)
        self.ops[eng].append((fn, ws, inc))
        if inc is not None:
            return (inc[0], inc[0].n)
        if dma is not None:
            return (dma, dma.n)
        if sig:
            return (self.ec[eng], self.ec[eng].n)
        return None

    def check(self):
        vals = {id(c): 0 for c in self.ctrs}
        pos = {e: 0 for e in self.ops}
        progress = True
        while progress:
            progress = False
            for e, lst in self.ops.items():
                while pos[e] < len(lst):
                    fn, ws, inc = lst[pos[e]]
                    if any(vals[id(c)] < v for c, v in ws):
                        break
                    if inc is not None:
                        vals[id(inc[0])] += inc[1]
                    pos[e] += 1
                    progress = True
        stuck = {e: p for e, p in pos.items() if p < len(self.ops[e])}
        if stuck:
            lines = []
            for e, p in stuck.items():
                fn, ws, inc = self.ops[e][p]
                blocked = [(c.name, v, vals[id(c)]) for c, v in ws if vals[id(c)] < v]
                lines.append(f"engine {e} op#{p}/{len(self.ops[e])} blocked on {blocked} inc={inc[0].name if inc else None}")
            for c in self.ctrs:
                if c.name in ("ecT", "ecS", "ecV", "upc", "dnc"):
                    lines.append(f"ctr {c.name}: final_n={c.n} sim_val={vals[id(c)]}")
            raise RuntimeError("schedule deadlock:\n" + "\n".join(lines))

    def replay(self, block):
        def runner(e):
            def run(eng):
                for fn, ws, inc in self.ops[e]:
                    for c, v in ws:
                        eng.wait_ge(c.sem, v)
                    ins = fn(eng)
                    if inc is not None:
                        ins.then_inc(inc[0].sem, inc[1])
            return run
        block.gpsimd(runner("Q"))
        block.sync(runner("Y"))
        block.tensor(runner("T"))
        block.vector(runner("V"))
        block.scalar(runner("S"))


def consts_arrays():
    out = np.zeros((128, CONSTN), dtype=bfd)
    out[:, C_ID:C_ID + 128] = np.eye(128, dtype=bfd)
    out[:, C_O1024:C_O1024 + 128] = np.full((128, 128), 1.0 / 1024.0, dtype=bfd)
    out[:, C_O128:C_O128 + 128] = np.full((128, 128), 1.0 / 128.0, dtype=bfd)
    mt = np.tile(np.triu(np.ones((128, 128), np.float32)), (1, 4)).astype(bfd)
    out[:, C_MSK:C_MSK + 512] = mt
    out[0:1, C_EPS:C_EPS + 128] = np.full((1, 128), EPS, np.float32).astype(bfd)
    out[0:1, C_OROW:C_OROW + 512] = np.ones((1, 512), dtype=bfd)
    out[0:1, C_OK1:C_OK1 + 128] = np.ones((1, 128), dtype=bfd)
    out[:, C_ONES:C_ONES + 128] = np.ones((128, 128), dtype=bfd)
    sel = np.zeros((16, 16 * 128), dtype=bfd)
    for e in range(16):
        sel[e, e * 128:(e + 1) * 128] = 1.0
    out[0:16, C_SEL:C_SEL + 2048] = sel
    return out


def build(cfg):
    mk = MK()
    nc = bass.Bass("TRN2", target_bir_lowering=False, debug=False,
                   detect_race_conditions=False)

    def dram_in(name, shape):
        return nc.dram_tensor(name, list(shape), F32, kind="ExternalInput").ap()

    xTh = dram_in("xT", [HID, NSEQ * TSEQ])
    cTh = dram_in("cT", [HID, NSEQ])
    wqTh, wkTh = dram_in("wqT", [HID, HID]), dram_in("wkT", [HID, HID])
    wvTh, woTh = dram_in("wvT", [HID, HID]), dram_in("woT", [HID, HID])
    fm1qh, fm2qh = dram_in("fm1qT", [DK, DK]), dram_in("fm2qT", [DK, DK])
    fm1kh, fm2kh = dram_in("fm1kT", [DK, DK]), dram_in("fm2kT", [DK, DK])
    fbq1h, fbq2h = dram_in("fmq_b1", [DK, 1]), dram_in("fmq_b2", [DK, 1])
    fbk1h, fbk2h = dram_in("fmk_b1", [DK, 1]), dram_in("fmk_b2", [DK, 1])
    rmswh = dram_in("rms_w", [DK, 1])
    suTh, sdTh = dram_in("suT", [2, HID, E]), dram_in("sdT", [2, E, HID])
    ruTh, rdTh = dram_in("ruT", [16, HID, E]), dram_in("rdT", [16, E, HID])
    rwTh = dram_in("routerT", [HID, 16])
    wmodh = dram_in("wmod", [HID, 6 * HID])

    outT = nc.dram_tensor("outT", [HID, NSEQ * TSEQ], F32, kind="ExternalOutput").ap()
    scr_kind = "ExternalOutput" if cfg.dbg else "Internal"
    wup = nc.dram_tensor("wup", [NX, 128, FC * E], BF16, kind="Internal").ap()
    wdn = nc.dram_tensor("wdn", [NX, 128, ECH * HID], BF16, kind="Internal").ap()
    xscr = nc.dram_tensor("xscr", [128, FC, NSEQ * TSEQ], BF16, kind=scr_kind).ap()
    constd = nc.inline_tensor(consts_arrays(), "constd").ap()
    idf2 = nc.inline_tensor(np.eye(2, dtype=np.float32), "idf2").ap()
    idf128 = nc.inline_tensor(np.eye(128, dtype=np.float32), "idf128").ap()
    self4 = np.zeros((4, 512), np.float32)
    for e in range(4):
        self4[e, e * 128:(e + 1) * 128] = 1.0
    self4 = nc.inline_tensor(self4, "self4").ap()
    self8 = np.zeros((8, 1024), dtype=bfd)
    for e in range(8):
        self8[e, e * 128:(e + 1) * 128] = 1.0
    self8 = nc.inline_tensor(self8, "self8").ap()

    with ExitStack() as st:
        U1 = st.enter_context(nc.sbuf_tensor("U1", [128, U1N], BF16))[:]
        U2 = st.enter_context(nc.sbuf_tensor("U2", [128, U2N], BF16))[:]
        XT = st.enter_context(nc.sbuf_tensor("XTb", [128, 2 * FC * TILE], BF16))[:]
        SCRF = st.enter_context(nc.sbuf_tensor("SCRF", [128, SCRFN], F32))[:]
        GATE = st.enter_context(nc.sbuf_tensor("GATE", [128, GATEN], F32))[:]
        CON = st.enter_context(nc.sbuf_tensor("CON", [128, CONSTN], BF16))[:]
        MODF = st.enter_context(nc.sbuf_tensor("MODF", [128, 96], F32))[:]
        DERV = st.enter_context(nc.sbuf_tensor("DERV", [128, 64], F32))[:]
        SMALL = st.enter_context(nc.sbuf_tensor("SMALL", [128, 16], F32))[:]
        CTF = st.enter_context(nc.sbuf_tensor("CTF", [128, 16], F32))[:]
        CTB = st.enter_context(nc.sbuf_tensor("CTB", [128, 16], BF16))[:]
        RT = st.enter_context(nc.sbuf_tensor("RT", [128, 128], BF16))[:]
        FMW = st.enter_context(nc.sbuf_tensor("FMW", [128, 512], BF16))[:]
        MODTMP = st.enter_context(nc.sbuf_tensor("MODTMP", [2, 512], F32))[:]
        SILF = st.enter_context(nc.sbuf_tensor("SILF", [128, 16], F32))[:]
        IDF = st.enter_context(nc.sbuf_tensor("IDF", [2, 2], F32))[:]
        IDFB = st.enter_context(nc.sbuf_tensor("IDFB", [128, 128], F32))[:]
        SBFS = st.enter_context(nc.sbuf_tensor("SBFS", [128, 2048], BF16))[:]
        GTT = st.enter_context(nc.sbuf_tensor("GTT", [16, 1024], BF16))[:]
        GBS = st.enter_context(nc.sbuf_tensor("GBS", [128, 512], BF16))[:]
        GATB = st.enter_context(nc.sbuf_tensor("GATB", [128, 64], BF16))[:]
        STATS = st.enter_context(nc.sbuf_tensor("STATS", [128, 128], F32))[:]
        ROWB = st.enter_context(nc.sbuf_tensor("ROWB", [4, 2560], F32))[:]
        RW8 = st.enter_context(nc.sbuf_tensor("RW8", [8, 128], BF16))[:]
        CONF8 = st.enter_context(nc.sbuf_tensor("CONF8", [8, 1024], BF16))[:]
        SELF4 = st.enter_context(nc.sbuf_tensor("SELF4", [4, 512], F32))[:]
        PB = [st.enter_context(nc.psum_tensor(f"PB{i}", [128, 512], F32))[:]
              for i in range(8)]

        qld = mk.ctr("qld")
        yconst = mk.ctr("yconst")
        qx = [mk.ctr("qx0"), mk.ctr("qx1")]
        yx = [mk.ctr("yx0"), mk.ctr("yx1")]
        yxp = [mk.ctr("yxp0"), mk.ctr("yxp1")]
        wmq = [mk.ctr("wmq0"), mk.ctr("wmq1")]
        qsc = [mk.ctr("qs0"), mk.ctr("qs1")]
        ysm = [mk.ctr("ys0"), mk.ctr("ys1")]
        yws = [mk.ctr(f"yws{i}") for i in range(4)]
        qws = [mk.ctr(f"qws{i}") for i in range(2)]
        upc = mk.ctr("upc")
        dnc = mk.ctr("dnc")
        so = [mk.ctr("so0"), mk.ctr("so1")]
        ydbg = mk.ctr("ydbg")

        def T(fn, waits=(), sig=False):
            return mk.emit("T", fn, waits, sig=sig)
        def V(fn, waits=(), sig=False):
            return mk.emit("V", fn, waits, sig=sig)
        def S(fn, waits=(), sig=False):
            return mk.emit("S", fn, waits, sig=sig)
        def Q(fn, waits=(), dma=None):
            return mk.emit("Q", fn, waits, dma=dma)
        def Y(fn, waits=(), dma=None):
            return mk.emit("Y", fn, waits, dma=dma)

        dbg_names = []
        def dump(name, ap, waits):
            if not cfg.dbg:
                return
            t = nc.dram_tensor("dbg_" + name, list(ap.shape), ap.dtype,
                               kind="ExternalOutput").ap()
            dbg_names.append("dbg_" + name)
            idx = tuple(slice(None) for _ in ap.shape)
            Y(lambda y, o=t, i=ap: y.dma_start(out=o[idx] if o.ndim > 1 else o, in_=i),
              waits, dma=ydbg)

        pb_free = {i: None for i in range(8)}
        xt_free = [None, None]
        lnp = {"aff": [None] * 8, "msq6": None, "msq7": None}

        Y(lambda y: y.dma_start(out=CON[:, :], in_=constd[:, :]), dma=yconst)
        Y(lambda y: y.dma_start(out=IDF[:, :], in_=idf2[:, :]), dma=yconst)
        Y(lambda y: y.dma_start(out=IDFB[:, :], in_=idf128[:, :]), dma=yconst)
        Y(lambda y: y.dma_start(out=SELF4[:, :], in_=self4[:, :]), dma=yconst)
        Y(lambda y: y.dma_start(out=CONF8[:, :], in_=self8[:, :]), dma=yconst)
        t_const = (yconst, yconst.n)

        def cast_load(dst, src):
            return Q(lambda q, d=dst, s=src: q.dma_start(out=d, in_=s), dma=qld)

        cast_load(U1[:, WQ:WQ + 8192].rearrange("p (kc d) -> p kc d", kc=8),
                  wqTh.rearrange("(kc p) d -> p kc d", p=128))
        cast_load(U1[:, WK:WK + 8192].rearrange("p (kc d) -> p kc d", kc=8),
                  wkTh.rearrange("(kc p) d -> p kc d", p=128))
        cast_load(U1[:, WV:WV + 8192].rearrange("p (kc d) -> p kc d", kc=8),
                  wvTh.rearrange("(kc p) d -> p kc d", p=128))
        cast_load(U1[:, WO:WO + 8192].rearrange("p (kc d) -> p kc d", kc=8),
                  woTh.rearrange("(kc p) d -> p kc d", p=128))
        cast_load(FMW[:, 0:128], fm1qh[:, :])
        cast_load(FMW[:, 128:256], fm2qh[:, :])
        cast_load(FMW[:, 256:384], fm1kh[:, :])
        cast_load(FMW[:, 384:512], fm2kh[:, :])
        cast_load(RT[:, 0:128].rearrange("p (kc e) -> p kc e", kc=8),
                  rwTh.rearrange("(kc p) e -> p kc e", p=128))
        for dst, src in ((SMALL[:, 0:1], fbq1h), (SMALL[:, 1:2], fbq2h),
                         (SMALL[:, 2:3], fbk1h), (SMALL[:, 3:4], fbk2h),
                         (SMALL[:, 4:5], rmswh)):
            Q(lambda q, d=dst, s=src: q.dma_start(out=d, in_=s[:, :]), dma=qld)
        Q(lambda q: q.dma_start(out=CTF[:, :].rearrange("p (ic b) -> p ic b", ic=8),
                                in_=cTh.rearrange("(ic p) b -> p ic b", p=128)), dma=qld)
        t_qld = (qld, qld.n)

        def load_xtile_attn(g):
            sl = g % 2
            dst = XT[:, sl * 4096:(sl + 1) * 4096].rearrange("p (fc t) -> p fc t", fc=8)
            src = xTh.rearrange("(fc p) t -> p fc t", p=128)[:, :, g * TILE:(g + 1) * TILE]
            wfree = xt_free[sl] if isinstance(xt_free[sl], list) else [xt_free[sl]]
            Q(lambda q, d=dst, s=src: q.dma_start(out=d, in_=s),
              wfree, dma=qx[sl])

        if cfg.n_attn_tiles > 0:
            load_xtile_attn(0)
        if cfg.n_attn_tiles > 1:
            load_xtile_attn(1)
        need_stats0 = cfg.n_attn_tiles > 0

        t_siluf = S(lambda s: s.activation(SILF[:, :], CTF[:, :], AF.Silu),
                    [t_qld], sig=True)
        t_silu = V(lambda v: v.tensor_copy(CTB[:, :], SILF[:, :]),
                   [t_siluf], sig=True)
        t_fmw = V(lambda v: v.tensor_scalar_mul(FMW[:, 0:128], FMW[:, 0:128], SCL),
                  [t_qld], sig=True)
        t_b1q = V(lambda v: v.tensor_scalar_mul(SMALL[:, 0:1], SMALL[:, 0:1], SCL),
                  [t_qld], sig=True)
        t_wo_scaled = None
        for kc in range(8):
            t_wo_scaled = S(lambda s, kc=kc: s.activation(
                U1[:, WO + kc * 1024:WO + (kc + 1) * 1024],
                U1[:, WO + kc * 1024:WO + (kc + 1) * 1024],
                AF.Identity, scale=SMALL[:, 4:5]), [t_qld], sig=True)

        slab_last = {}
        t_modf = None
        t_ev = None
        ttr_all = None
        for pc in range(12):
            isf = pc in (4, 5, 10, 11)
            par = pc % 2
            slab_id = ("F" if isf else "B") + str(par)
            src_p = wmodh.rearrange("(ic p) j -> p ic j", p=128)[
                :, :, pc * 512:(pc + 1) * 512]
            if isf:
                base = 8192 + par * 8192
                slab = U2[:, base:base + 8192].bitcast(F32).rearrange(
                    "p (ic j) -> p ic j", ic=8)
                slab_mm = U2[:, base:base + 8192].bitcast(F32)
            else:
                base = par * 4096
                slab = U2[:, base:base + 4096].rearrange("p (ic j) -> p ic j", ic=8)
                slab_mm = U2[:, base:base + 4096]
            wq_waits = [slab_last.get(slab_id)]
            Q(lambda q, d=slab, s=src_p: q.dma_start(out=d, in_=s),
              wq_waits, dma=wmq[par])
            t_wm = (wmq[par], wmq[par].n)
            tmm = None
            for ic in range(8):
                w = [t_wm, t_silu, t_siluf]
                if ic == 0 and t_ev is not None:
                    w.append(t_ev)
                tmm = T(lambda t, ic=ic, isf=isf, sm=slab_mm: t.matmul(
                    PB[6][0:2, :],
                    (SILF if isf else CTB)[:, ic * 2:ic * 2 + 2],
                    sm[:, ic * 512:(ic + 1) * 512],
                    start=(ic == 0), stop=(ic == 7), skip_group_check=True),
                    w, sig=(ic == 7))
            slab_last[slab_id] = tmm
            t_ev = S(lambda s: s.copy(MODTMP[0:2, :], PB[6][0:2, :]), [tmm], sig=True)
            for k in range(4):
                w = [t_ev, t_const]
                ttr_all = T(lambda t, k=k, pc=pc: t.transpose(
                    PB[7][:, (pc * 4 + k) * 2:(pc * 4 + k) * 2 + 2],
                    MODTMP[0:2, k * 128:(k + 1) * 128],
                    IDF[0:2, 0:2]), w, sig=(k == 3))
        t_modf = S(lambda s: s.copy(MODF[:, 0:96], PB[7][:, 0:96]),
                   [ttr_all], sig=True)
        t_d = []
        t_d.append(V(lambda v: v.tensor_scalar_add(DERV[:, 0:16], MODF[:, 16:32], 1.0),
                     [t_modf], sig=True))
        t_d.append(V(lambda v: v.tensor_scalar_add(DERV[:, 32:48], MODF[:, 64:80], 1.0),
                     [t_modf], sig=True))
        t_mod_done = None
        for (src0, dst0) in ((32, 16), (80, 48)):
            ta = S(lambda s, src0=src0: s.activation(
                GATE[:, GE:GE + 16], MODF[:, src0:src0 + 16], AF.Abs),
                [t_modf] + ([t_mod_done] if t_mod_done else []), sig=True)
            tb = V(lambda v: v.tensor_scalar_max(GATE[:, GE:GE + 16],
                                                 GATE[:, GE:GE + 16], 0.25),
                   [ta] + t_d, sig=True)
            tc_ = V(lambda v, src0=src0: v.tensor_scalar(
                GATE[:, GC:GC + 16], MODF[:, src0:src0 + 16], 0.0, None, ALU.is_ge),
                [t_modf], sig=True)
            td = V(lambda v: v.tensor_scalar(GATE[:, GC:GC + 16], GATE[:, GC:GC + 16],
                                             2.0, -1.0, ALU.mult, ALU.add),
                   [tc_], sig=True)
            t_mod_done = V(lambda v, dst0=dst0: v.tensor_tensor(
                DERV[:, dst0:dst0 + 16], GATE[:, GE:GE + 16], GATE[:, GC:GC + 16],
                ALU.mult), [tb, td], sig=True)

        dump("modf", MODF, [t_modf])
        dump("derv", DERV, [t_mod_done])

        stage_list = []
        for x in range(NX):
            src = (suTh[x] if x < 2 else ruTh[x - 2]).rearrange("(kc p) e -> p kc e", p=128)
            stage_list.append((src[:, 0:4], wup[x][:, 0:2048], 4))
            stage_list.append((src[:, 4:8], wup[x][:, 2048:4096], 4))
        for x in range(NX):
            src = (sdTh[x] if x < 2 else rdTh[x - 2]).rearrange("(ec p) h -> p ec h", p=128)
            stage_list.append((src[:, 0:2], wdn[x][:, 0:2048], 2))
            stage_list.append((src[:, 2:4], wdn[x][:, 2048:4096], 2))
        sst = {"i": 0, "wr": []}

        def emit_staging(nmax):
            cnt = 0
            while sst["i"] < len(stage_list) and cnt < nmax:
                i = sst["i"]
                src, dst, a = stage_list[i]
                par = i % 2
                slab = U2[:, STG + par * 2048:STG + (par + 1) * 2048]
                w = [sst["wr"][i - 2]] if i >= 2 else []
                Q(lambda q, d=slab, s=src, a=a: q.dma_start(
                    out=d.rearrange("p (a b) -> p a b", a=a), in_=s), w, dma=qsc[par])
                t_rd = (qsc[par], qsc[par].n)
                Y(lambda y, d=dst, s=slab: y.dma_start(out=d, in_=s),
                  [t_rd], dma=ysm[par])
                sst["wr"].append((ysm[par], ysm[par].n))
                sst["i"] += 1
                cnt += 1

        xsqwar = {0: None, 1: None}
        rowb_war = {}
        stat_toks = {}

        def emit_stats(kind, g, xt_ap, x_ready, bank_idx, extra=()):
            pb = PB[bank_idx]
            mcol = (0 if kind == "a" else 64) + g * 4
            rcol = (32 if kind == "a" else 96) + g * 4
            tz = T(lambda t: t.matmul(
                pb[:, 0:8], CON[0:1, C_ZERO:C_ZERO + 128],
                CON[0:1, C_OROW:C_OROW + 8],
                start=True, stop=False, skip_group_check=True),
                [t_const, pb_free[bank_idx]], sig=False)
            t_msqf = [None] * 8
            tmm = None
            for fc in range(8):
                for st_ in range(4):
                    w = [x_ready] if (fc == 0 and st_ == 0) else []
                    tmm = T(lambda t, st_=st_, fc=fc: t.matmul(
                        pb[:, st_:st_ + 1],
                        xt_ap[:, fc, st_ * 128:(st_ + 1) * 128],
                        CON[:, C_O1024:C_O1024 + 1],
                        start=False, stop=False, skip_group_check=True),
                        w, sig=(st_ == 3 and fc == 7))
            t_m_done = tmm
            for fc in range(8):
                sl = fc % 2
                w = [x_ready, xsqwar[sl] if fc < 2 else t_msqf[fc - 2]] + list(extra)
                t_xsq = V(lambda v, fc=fc, sl=sl: v.tensor_tensor(
                    U2[:, XSQA + sl * 512:XSQA + (sl + 1) * 512],
                    xt_ap[:, fc], xt_ap[:, fc], ALU.mult), w, sig=True)
                for st_ in range(4):
                    w = [t_xsq] if st_ == 0 else []
                    tmm = T(lambda t, st_=st_, fc=fc, sl=sl: t.matmul(
                        pb[:, 4 + st_:5 + st_],
                        U2[:, XSQA + sl * 512 + st_ * 128:XSQA + sl * 512 + (st_ + 1) * 128],
                        CON[:, C_O1024:C_O1024 + 1],
                        start=False, stop=False, skip_group_check=True),
                        w, sig=(st_ == 3))
                t_msqf[fc] = tmm
            xsqwar[0], xsqwar[1] = t_msqf[6], t_msqf[7]
            t_mc = V(lambda v, mcol=mcol: v.tensor_copy(STATS[:, mcol:mcol + 4],
                                                        pb[:, 0:4]),
                     [t_m_done, t_msqf[7]], sig=True)
            t_m2 = V(lambda v, mcol=mcol: v.tensor_tensor(
                GATE[:, GS + 16:GS + 20],
                STATS[:, mcol:mcol + 4], STATS[:, mcol:mcol + 4], ALU.mult),
                [t_mc], sig=True)
            t_vv = V(lambda v: v.scalar_tensor_tensor(
                GATE[:, GS + 20:GS + 24], pb[:, 4:8], EPS,
                GATE[:, GS + 16:GS + 20], ALU.add, ALU.subtract),
                [t_m2, t_msqf[7]], sig=True)
            t_rc = V(lambda v: v.reciprocal(GATE[:, GS + 20:GS + 24],
                                            GATE[:, GS + 20:GS + 24]), [t_vv], sig=True)
            pb_free[bank_idx] = t_rc
            t_sq = S(lambda s, rcol=rcol: s.sqrt(STATS[:, rcol:rcol + 4],
                                                 GATE[:, GS + 20:GS + 24]),
                     [t_rc], sig=True)
            rb = (g % 2) * 256 if kind == "a" else 512 + g * 256
            ttm = T(lambda t, mcol=mcol: t.transpose(
                pb[0:4, 8:136], STATS[:, mcol:mcol + 4], IDFB[:, :]),
                [t_mc, t_vv], sig=True)
            tem = S(lambda s, rb=rb: s.copy(ROWB[0:4, rb:rb + 128], pb[0:4, 8:136]),
                    [ttm, rowb_war.get((kind, g % 2))], sig=True)
            ttr = T(lambda t, rcol=rcol: t.transpose(
                pb[0:4, 136:264], STATS[:, rcol:rcol + 4], IDFB[:, :]),
                [t_sq], sig=True)
            ter = S(lambda s, rb=rb: s.copy(ROWB[0:4, rb + 128:rb + 256],
                                            pb[0:4, 136:264]), [ttr], sig=True)
            pb_free[bank_idx] = ter
            stat_toks[(kind, g)] = (tem, ter, rb)
            return tmm, ter

        def emit_apply(kind, g, xt_ap, hbuf, hbase, onep_base, sh_base, b_,
                       x_ready, pb_m, pb_m_idx, pb_v, pb_v_idx, h_war,
                       aff_on_s=False):
            tem, ter, rb = stat_toks[(kind, g)]
            tbm = None
            tbr = None
            for st_ in range(4):
                w = [tem]
                if st_ == 0:
                    w.append(pb_free[pb_m_idx])
                tbm = T(lambda t, st_=st_, rb=rb: t.matmul(
                    pb_m[:, st_ * 128:(st_ + 1) * 128],
                    SELF4[0:4, st_ * 128:(st_ + 1) * 128],
                    ROWB[0:4, rb:rb + 128],
                    start=True, stop=True, skip_group_check=True),
                    w, sig=(st_ == 3))
                w = [ter]
                if st_ == 0:
                    w.append(pb_free[pb_v_idx])
                tbr = T(lambda t, st_=st_, rb=rb: t.matmul(
                    pb_v[:, st_ * 128:(st_ + 1) * 128],
                    SELF4[0:4, st_ * 128:(st_ + 1) * 128],
                    ROWB[0:4, rb + 128:rb + 256],
                    start=True, stop=True, skip_group_check=True),
                    w, sig=(st_ == 3))
            rowb_war[(kind, g % 2)] = tbr
            t_aff = [None] * 8
            t_sub_l = [None] * 8
            t_mul_l = [None] * 8
            for fc in range(8):
                sl = fc % 2
                w = [tbm, x_ready]
                if fc >= 2:
                    w.append(t_aff[fc - 2])
                else:
                    w.append(lnp["aff"][6 + fc])
                t_sub = V(lambda v, fc=fc, sl=sl: v.tensor_tensor(
                    SCRF[:, TA[sl]:TA[sl] + 512], xt_ap[:, fc], pb_m, ALU.subtract),
                    w, sig=True)
                t_sub_l[fc] = t_sub
                t_mul = V(lambda v, fc=fc, sl=sl: v.tensor_tensor(
                    SCRF[:, TB[sl]:TB[sl] + 512], SCRF[:, TA[sl]:TA[sl] + 512],
                    pb_v, ALU.mult), [tbr, t_sub], sig=True)
                t_mul_l[fc] = t_mul
                if aff_on_s:
                    t_aff[fc] = S(lambda s, fc=fc, sl=sl: s.activation(
                        hbuf[:, hbase + fc * 512:hbase + (fc + 1) * 512],
                        SCRF[:, TB[sl]:TB[sl] + 512], AF.Identity,
                        bias=MODF[:, sh_base + fc * 2 + b_:sh_base + fc * 2 + b_ + 1],
                        scale=DERV[:, onep_base + fc * 2 + b_:onep_base + fc * 2 + b_ + 1]),
                        [t_mul, t_mod_done, h_war if fc == 0 else None], sig=True)
                else:
                    t_aff[fc] = V(lambda v, fc=fc, sl=sl: v.tensor_scalar(
                        hbuf[:, hbase + fc * 512:hbase + (fc + 1) * 512],
                        SCRF[:, TB[sl]:TB[sl] + 512],
                        DERV[:, onep_base + fc * 2 + b_:onep_base + fc * 2 + b_ + 1],
                        MODF[:, sh_base + fc * 2 + b_:sh_base + fc * 2 + b_ + 1],
                        ALU.mult, ALU.add),
                        [t_mod_done, h_war if fc == 0 else None], sig=True)
            lnp["aff"] = t_aff
            pb_free[pb_m_idx] = t_sub_l[7]
            pb_free[pb_v_idx] = t_mul_l[7]
            return t_aff

        atn = {"g": None, "t_on_last": None, "cp_tok": {}, "t_kv": {},
               "t_o": {}, "t_o_b": {}, "xpost": [None] * NAT,
               "h_last": None}
        if need_stats0:
            emit_stats("a", 0, XT[:, 0:4096].rearrange("p (fc t) -> p fc t", fc=8),
                       (qx[0], 16), 1)

        def emit_wo_dc(gp, dc, res_toks):
            if gp is None:
                return
            sl = gp % 2
            s_ = gp // 4
            bidx = dc % 2
            tmm = None
            for hc in range(8):
                w = [atn["t_on_last"], t_wo_scaled]
                if hc == 0:
                    w.append(pb_free[bidx])
                rhs = U2[:, ONr:ONr + 4096].rearrange(
                    "p (c h t) -> p c h t", c=4, h=8)[:, :, hc, :]
                tmm = T(lambda t, dc=dc, hc=hc, bidx=bidx, rhs=rhs: t.matmul(
                    PB[bidx],
                    U1[:, WO + hc * 1024 + dc * 128:WO + hc * 1024 + (dc + 1) * 128],
                    rhs, start=(hc == 0), stop=(hc == 7), skip_group_check=True),
                    w, sig=(hc == 7))
            tres = V(lambda v, dc=dc, sl=sl, bidx=bidx, s_=s_: v.scalar_tensor_tensor(
                XT[:, sl * 4096 + dc * 512:sl * 4096 + (dc + 1) * 512],
                PB[bidx], DERV[:, 16 + dc * 2 + s_:16 + dc * 2 + s_ + 1],
                XT[:, sl * 4096 + dc * 512:sl * 4096 + (dc + 1) * 512],
                ALU.mult, ALU.add), [tmm, t_mod_done], sig=True)
            pb_free[bidx] = tres
            res_toks.append(tres)

        def emit_wo_fin(gp, res_toks):
            if gp is None:
                return
            sl = gp % 2
            Y(lambda y, sl=sl, g=gp: y.dma_start(
                out=xscr[:, :, g * TILE:(g + 1) * TILE],
                in_=XT[:, sl * 4096:(sl + 1) * 4096].rearrange("p (fc t) -> p fc t", fc=8)),
              res_toks, dma=yxp[sl])
            atn["xpost"][gp] = (yxp[sl], yxp[sl].n)
            if cfg.n_ffn_tiles > gp:
                xap = XT[:, sl * 4096:(sl + 1) * 4096].rearrange(
                    "p (fc t) -> p fc t", fc=8)
                tstat, _ = emit_stats("f", gp, xap, res_toks[-1], 1)
                xt_free[sl] = [(yxp[sl], yxp[sl].n), tstat]
            else:
                xt_free[sl] = (yxp[sl], yxp[sl].n)

        GBKS = [4, 5, 0, 1]

        def proj_acc(dst_base, wbase, t_h):
            t_ev = [None] * 8
            tq_last = [None] * 8

            def acc_group(grp):
                for kc in range(8):
                    for i4 in range(4):
                        dc = grp * 4 + i4
                        bidx = GBKS[i4]
                        w = [t_h[kc]]
                        if kc == 0:
                            w.append(pb_free[bidx])
                        tq = T(lambda t, dc=dc, kc=kc, bidx=bidx, wbase=wbase: t.matmul(
                            PB[bidx],
                            U1[:, wbase + kc * 1024 + dc * 128:wbase + kc * 1024 + (dc + 1) * 128],
                            U2[:, HBr + kc * 512:HBr + (kc + 1) * 512],
                            start=(kc == 0), stop=(kc == 7), skip_group_check=True),
                            w, sig=(kc == 7))
                        if kc == 7:
                            tq_last[dc] = tq

            def evac(dc):
                bidx = GBKS[dc % 4]
                t_ev[dc] = V(lambda v, dc=dc, bidx=bidx, dst_base=dst_base: v.tensor_copy(
                    U2[:, dst_base + dc * 512:dst_base + (dc + 1) * 512], PB[bidx]),
                    [tq_last[dc]], sig=True)
                pb_free[bidx] = t_ev[dc]

            acc_group(0)
            for dc in range(4):
                evac(dc)
            acc_group(1)
            for dc in range(4, 8):
                evac(dc)
            return t_ev

        fmst = {"last2": None, "last3": None}

        def fm_chain(dst_base, t_ev, t_comb, t_a1, dc):
            isq = dst_base == QP
            fmo = 0 if isq else 256
            bias1 = SMALL[:, 0:1] if isq else SMALL[:, 2:3]
            bias2 = SMALL[:, 1:2] if isq else SMALL[:, 3:4]
            if True:
                bidx = 2 + dc % 2
                w = [t_ev[dc], t_fmw, t_b1q, pb_free[bidx]]
                tp1 = T(lambda t, dc=dc, bidx=bidx, fmo=fmo, dst_base=dst_base: t.matmul(
                    PB[bidx], FMW[:, fmo:fmo + 128],
                    U2[:, dst_base + dc * 512:dst_base + (dc + 1) * 512],
                    start=True, stop=True, skip_group_check=True), w, sig=True)
                w = [tp1, fmst["a1w" + str(dc % 2)] if ("a1w" + str(dc % 2)) in fmst else None]
                if dc % 2 == 0:
                    t_a1[dc] = S(lambda s, dc=dc, bidx=bidx, b1=bias1: s.activation(
                        U2[:, A1SL + (dc % 2) * 512:A1SL + (dc % 2 + 1) * 512],
                        PB[bidx], AF.Identity, bias=b1), w, sig=True)
                else:
                    t_a1[dc] = V(lambda v, dc=dc, bidx=bidx, b1=bias1: v.tensor_scalar(
                        U2[:, A1SL + (dc % 2) * 512:A1SL + (dc % 2 + 1) * 512],
                        PB[bidx], b1, None, ALU.add), w, sig=True)
                tp2 = T(lambda t, dc=dc, bidx=bidx, fmo=fmo, dst_base=dst_base: t.matmul(
                    PB[bidx], FMW[:, fmo + 128:fmo + 256],
                    U2[:, dst_base + dc * 512:dst_base + (dc + 1) * 512],
                    start=True, stop=True, skip_group_check=True),
                    [t_a1[dc]], sig=True)
                t_comb[dc] = V(lambda v, dc=dc, bidx=bidx, b2=bias2, dst_base=dst_base:
                               v.scalar_tensor_tensor(
                    U2[:, dst_base + dc * 512:dst_base + (dc + 1) * 512],
                    PB[bidx], b2,
                    U2[:, A1SL + (dc % 2) * 512:A1SL + (dc % 2 + 1) * 512],
                    ALU.add, ALU.mult), [tp2], sig=True)
                pb_free[bidx] = t_comb[dc]
                fmst["a1w" + str(dc % 2)] = t_comb[dc]

        apply_pend = {}
        for g in range(cfg.n_attn_tiles):
            s_ = g // 4
            sl = g % 2
            base = (g % 4) * CPT
            x_ready = (qx[sl], qx[sl].n)
            xt_ap = XT[:, sl * 4096:(sl + 1) * 4096].rearrange("p (fc t) -> p fc t", fc=8)

            if g == 0:
                t_h = emit_apply("a", 0, xt_ap, U2, HBr, 0, 0, s_,
                                 x_ready, PB[2], 2, PB[3], 3, atn["h_last"],
                                 aff_on_s=True)
            else:
                t_h = apply_pend[g]

            if g == 0:
                dump("h0", U2[:, HBr:HBr + 4096], [t_h[7]])
            q_ev = proj_acc(QP, WQ, t_h)
            k_ev = proj_acc(KP, WK, t_h)
            tq_comb = [None] * 8
            tk_comb = [None] * 8
            tq_a1 = [None] * 8
            tk_a1 = [None] * 8
            t_vev = [None] * 8
            t_ktev = [None] * 4

            def vproj_block(i):
                st_, nh = i // 2, i % 2
                bidx = 4 + i % 2
                tv = None
                for kc in range(8):
                    w = [t_h[kc]]
                    if kc == 0:
                        w.append(pb_free[bidx])
                    tv = T(lambda t, st_=st_, nh=nh, kc=kc, bidx=bidx: t.matmul(
                        PB[bidx],
                        U2[:, HBr + kc * 512 + st_ * 128:HBr + kc * 512 + (st_ + 1) * 128],
                        U1[:, WV + kc * 1024 + nh * 512:WV + kc * 1024 + (nh + 1) * 512],
                        start=(kc == 0), stop=(kc == 7), skip_group_check=True),
                        w, sig=(kc == 7))
                t_vev[i] = S(lambda s, st_=st_, nh=nh, bidx=bidx: s.copy(
                    U2[:, VTr + st_ * 1024 + nh * 512:VTr + st_ * 1024 + (nh + 1) * 512],
                    PB[bidx]), [tv], sig=True)
                pb_free[bidx] = t_vev[i]

            def kt_batch(batch):
                ttr = None
                for j in range(8):
                    hh = (batch * 8 + j) // 4
                    pr = (batch * 8 + j) % 4
                    w = [tk_comb[hh], t_const]
                    if j == 0:
                        w.append(pb_free[3])
                    ttr = T(lambda t, hh=hh, pr=pr, j=j: t.transpose(
                        PB[3].bitcast(BF16)[:, j * 128:(j + 1) * 128],
                        U2[:, KP + hh * 512 + pr * 128:KP + hh * 512 + (pr + 1) * 128],
                        CON[:, C_ID:C_ID + 128]), w, sig=(j == 7))
                t_ktev[batch] = S(lambda s, batch=batch: s.copy(
                    U2[:, KTr + batch * 1024:KTr + (batch + 1) * 1024],
                    PB[3].bitcast(BF16)[:, 0:1024]), [ttr], sig=True)
                pb_free[3] = t_ktev[batch]

            for dc in range(8):
                fm_chain(QP, q_ev, tq_comb, tq_a1, dc)
                vproj_block(dc)
            wo_res = []
            for dc in range(8):
                fm_chain(KP, k_ev, tk_comb, tk_a1, dc)
                if dc % 2 == 1:
                    kt_batch(dc // 2)
                emit_wo_dc(atn["g"], dc, wo_res)
            emit_wo_fin(atn["g"], wo_res)
            t_kt_done = t_ktev[3]
            t_v_done = t_vev[7]
            atn["h_last"] = t_v_done

            if g + 1 < cfg.n_attn_tiles:
                load_xtile_attn(g + 1)
            emit_staging(0 if getattr(cfg, 'no_stg_il', False) else 10)
            if g + 1 < cfg.n_attn_tiles:
                sl1 = (g + 1) % 2
                emit_stats("a", g + 1,
                           XT[:, sl1 * 4096:(sl1 + 1) * 4096].rearrange(
                               "p (fc t) -> p fc t", fc=8),
                           (qx[sl1], qx[sl1].n), 1)

            if base == 0:
                for bi in (6, 7):
                    T(lambda t, bi=bi: t.matmul(
                        PB[bi], CON[0:1, C_ZERO:C_ZERO + 128],
                        CON[0:1, C_OROW:C_OROW + 512],
                        start=True, stop=False, skip_group_check=True),
                        [t_const], sig=False)

            a_free = {2: pb_free[2], 3: pb_free[3]}
            o_free = {4: pb_free[4], 5: pb_free[5]}
            t_mask = {}
            rms = {"sq": {}, "msq_tok": {}, "small": {}, "rsb": {},
                   "bc_war": pb_free[0], "tr_war": pb_free[1],
                   "rsb_war": None, "msq_war": pb_free[1]}

            def kv_mms(c):
                n = base + c
                tkv = None
                for h in range(8):
                    w = []
                    if h == 0:
                        w = [t_kt_done, t_v_done]
                        if n % NCHS != 0:
                            w.append(atn["cp_tok"].get(n))
                    tkv = T(lambda t, c=c, h=h: t.matmul(
                        PB[6 + h // 4][:, (h % 4) * 128:(h % 4 + 1) * 128],
                        U2[:, KTr + h * 512 + c * 128:KTr + h * 512 + (c + 1) * 128],
                        U2[:, VTr + c * 1024 + h * 128:VTr + c * 1024 + (h + 1) * 128],
                        start=False, stop=False, skip_group_check=True),
                        w, sig=(h == 7))
                atn["t_kv"][n] = tkv

            def state_copy(n):
                if n % NCHS == 0:
                    return
                par = n % 2
                w = [atn["t_kv"][n - 1], atn["t_o"].get(n - 2)]
                S(lambda s, par=par: s.copy(SBFS[:, par * 1024:par * 1024 + 512], PB[6]),
                  w, sig=True)
                atn["cp_tok"][n] = S(lambda s, par=par: s.copy(
                    SBFS[:, par * 1024 + 512:par * 1024 + 1024], PB[7]), [], sig=True)

            def a_mms(c):
                toks = {}
                for h in range(8):
                    bidx = 2 + h // 4
                    w = [tq_comb[h], tk_comb[h]]
                    if h % 4 == 0:
                        w.append(a_free[bidx])
                    ta = T(lambda t, c=c, h=h, bidx=bidx: t.matmul(
                        PB[bidx][:, (h % 4) * 128:(h % 4 + 1) * 128],
                        U2[:, KP + h * 512 + c * 128:KP + h * 512 + (c + 1) * 128],
                        U2[:, QP + h * 512 + c * 128:QP + h * 512 + (c + 1) * 128],
                        start=True, stop=True, skip_group_check=True),
                        w, sig=(h == 3 or h == 7))
                    if h in (3, 7):
                        toks[bidx] = ta
                return toks

            def mask_op(c, toks):
                n = base + c
                atsl = ATSL + (c % 2) * 1024
                tms = []
                for bi in (2, 3):
                    w = [toks[bi], t_const, atn["t_o"].get(n - 2)]
                    tm = V(lambda v, bi=bi, atsl=atsl: v.tensor_tensor(
                        U2[:, atsl + (bi - 2) * 512:atsl + (bi - 1) * 512],
                        PB[bi], CON[:, C_MSK:C_MSK + 512], ALU.mult),
                        w, sig=True)
                    a_free[bi] = tm
                    tms.append(tm)
                t_mask[n] = tms

            def o_mms(c):
                n = base + c
                if n % NCHS != 0:
                    for h in range(8):
                        bidx = 4 + h // 4
                        w = [tq_comb[h]]
                        if h % 4 == 0:
                            w += [o_free[bidx], atn["cp_tok"][n]]
                        T(lambda t, c=c, h=h, n=n, bidx=bidx: t.matmul(
                            PB[bidx][:, (h % 4) * 128:(h % 4 + 1) * 128],
                            SBFS[:, (n % 2) * 1024 + h * 128:(n % 2) * 1024 + (h + 1) * 128],
                            U2[:, QP + h * 512 + c * 128:QP + h * 512 + (c + 1) * 128],
                            start=(h % 4 == 0), stop=False, skip_group_check=True),
                            w, sig=False)
                to = {}
                st_flag = (n % NCHS == 0)
                for h in range(8):
                    bidx = 4 + h // 4
                    w = [t_mask[n][h // 4]]
                    if st_flag and h % 4 == 0:
                        w.append(o_free[bidx])
                    to[bidx] = T(lambda t, c=c, h=h, bidx=bidx, st_flag=st_flag: t.matmul(
                        PB[bidx][:, (h % 4) * 128:(h % 4 + 1) * 128],
                        U2[:, VTr + c * 1024 + h * 128:VTr + c * 1024 + (h + 1) * 128],
                        U2[:, ATSL + (c % 2) * 1024 + h * 128:ATSL + (c % 2) * 1024 + (h + 1) * 128],
                        start=(st_flag and h % 4 == 0), stop=True, skip_group_check=True),
                        w, sig=(h == 3 or h == 7))
                atn["t_o"][n] = to[5]
                atn["t_o_b"][n] = (to[4], to[5])

            def o_evac(c):
                n = base + c
                onb = ONr + c * 1024
                o2b = O2SL + (c % 2) * 1024
                t4, t5 = atn["t_o_b"][n]
                cp4 = S(lambda s, onb=onb: s.copy(U2[:, onb:onb + 512], PB[4]),
                        [t4], sig=True)
                cp5 = S(lambda s, onb=onb: s.copy(
                    U2[:, onb + 512:onb + 1024], PB[5]), [t5], sig=True)
                sq4 = V(lambda v, onb=onb, o2b=o2b: v.tensor_tensor(
                    U2[:, o2b:o2b + 512], U2[:, onb:onb + 512], U2[:, onb:onb + 512],
                    ALU.mult), [cp4, rms["msq_tok"].get(c - 2)], sig=True)
                sq5 = V(lambda v, onb=onb, o2b=o2b: v.tensor_tensor(
                    U2[:, o2b + 512:o2b + 1024], U2[:, onb + 512:onb + 1024],
                    U2[:, onb + 512:onb + 1024], ALU.mult), [cp5], sig=True)
                o_free[4] = cp4
                o_free[5] = cp5
                rms["sq"][c] = (sq4, sq5)

            def rms_msq(c):
                o2b = O2SL + (c % 2) * 1024
                sq4, sq5 = rms["sq"][c]
                mc = 300 + (c % 2) * 8
                tq = None
                for h in range(8):
                    w = [sq4 if h < 4 else sq5]
                    if h == 0:
                        w.append(rms["msq_war"])
                    tq = T(lambda t, h=h, mc=mc, o2b=o2b: t.matmul(
                        PB[1][:, mc + h:mc + h + 1],
                        U2[:, o2b + h * 128:o2b + (h + 1) * 128],
                        CON[:, C_O128:C_O128 + 1],
                        start=True, stop=True, skip_group_check=True),
                        w, sig=(h == 7))
                rms["msq_tok"][c] = tq

            def rms_small(c):
                mc = 300 + (c % 2) * 8
                gc_ = GS + 24 + (c % 2) * 8
                t1 = V(lambda v, mc=mc, gc_=gc_: v.tensor_scalar_add(
                    GATE[:, gc_:gc_ + 8], PB[1][:, mc:mc + 8], EPS),
                    [rms["msq_tok"][c]], sig=True)
                t2 = V(lambda v, gc_=gc_: v.reciprocal(
                    GATE[:, gc_:gc_ + 8], GATE[:, gc_:gc_ + 8]), [t1], sig=True)
                t3 = S(lambda s, gc_=gc_: s.sqrt(
                    GATE[:, gc_:gc_ + 8], GATE[:, gc_:gc_ + 8]), [t2], sig=True)
                rms["small"][c] = t3
                rms["msq_war"] = t1

            def rms_tr(c):
                gc_ = GS + 24 + (c % 2) * 8
                ttr = T(lambda t, gc_=gc_: t.transpose(
                    PB[1][0:8, 384:512], GATE[:, gc_:gc_ + 8], IDFB[:, :]),
                    [rms["small"][c], rms["tr_war"], t_const], sig=True)
                trsb = S(lambda s: s.copy(RW8[0:8, 0:128],
                                          PB[1][0:8, 384:512]),
                         [ttr, rms["rsb_war"]], sig=True)
                rms["rsb"][c] = trsb
                rms["tr_war"] = trsb

            def rms_bcast(c):
                trsb = rms["rsb"][c]
                last_mm = None
                for r in range(2):
                    mm = None
                    for hh in range(4):
                        h = r * 4 + hh
                        w = [trsb]
                        if hh == 0:
                            w.append(rms["bc_war"])
                        mm = T(lambda t, h=h, hh=hh: t.matmul(
                            PB[0][:, hh * 128:(hh + 1) * 128],
                            CONF8[0:8, h * 128:(h + 1) * 128],
                            RW8[0:8, 0:128],
                            start=True, stop=True, skip_group_check=True),
                            w, sig=(hh == 3))
                    ton = V(lambda v, c=c, r=r: v.tensor_tensor(
                        U2[:, ONr + c * 1024 + r * 512:ONr + c * 1024 + (r + 1) * 512],
                        U2[:, ONr + c * 1024 + r * 512:ONr + c * 1024 + (r + 1) * 512],
                        PB[0], ALU.mult), [mm], sig=True)
                    rms["bc_war"] = ton
                    last_mm = mm
                rms["rsb_war"] = last_mm
                atn["t_on_last"] = rms["bc_war"]

            for c in range(CPT):
                kv_mms(c)
                toks = a_mms(c)
                state_copy(base + c + 1)
                mask_op(c, toks)
                if c >= 2:
                    rms_tr(c - 2)
                o_mms(c)
                if c >= 2:
                    rms_bcast(c - 2)
                o_evac(c)
                if c >= 1:
                    rms_msq(c - 1)
                    rms_small(c - 1)
            rms_msq(CPT - 1)
            rms_small(CPT - 1)
            rms_tr(CPT - 2)
            rms_bcast(CPT - 2)
            rms_tr(CPT - 1)

            pb_free[2] = a_free[2]
            pb_free[3] = a_free[3]
            pb_free[4] = o_free[4]
            pb_free[5] = o_free[5]

            if g + 1 < cfg.n_attn_tiles:
                sl1 = (g + 1) % 2
                apply_pend[g + 1] = emit_apply(
                    "a", g + 1,
                    XT[:, sl1 * 4096:(sl1 + 1) * 4096].rearrange(
                        "p (fc t) -> p fc t", fc=8),
                    U2, HBr, 0, 0, (g + 1) // 4, (qx[sl1], qx[sl1].n),
                    PB[2], 2, PB[3], 3, atn["h_last"], aff_on_s=True)

            rms_bcast(CPT - 1)
            pb_free[0] = rms["bc_war"]
            pb_free[1] = rms["rsb"][CPT - 1]
            if g == 0:
                dump("vt0", U2[:, VTr:VTr + 4096], [t_v_done])
                dump("on0", U2[:, ONr:ONr + 4096], [atn["t_on_last"]])

            atn["g"] = g

        wo_res = []
        for dc in range(8):
            emit_wo_dc(atn["g"], dc, wo_res)
        emit_wo_fin(atn["g"], wo_res)
        emit_staging(100)

        t_attn_T = (mk.ec["T"], mk.ec["T"].n)
        t_attn_V = (mk.ec["V"], mk.ec["V"].n)
        t_attn_S = (mk.ec["S"], mk.ec["S"].n)
        stg_done = sst["wr"][-2:] if sst["wr"] else []

        if cfg.n_ffn_tiles > 0:
            T(lambda t: t.ldweights(CON[:, C_ID:C_ID + 128]),
              [t_attn_V, t_attn_S], sig=True)
            V(lambda v: v.memset(GATE[:, GS + 16:GS + 17], 0.0),
              [t_attn_T, t_attn_S], sig=True)
            S(lambda s: s.memzero(SMALL[:, 14:15]),
              [t_attn_T, t_attn_V], sig=True)

        ffn = {"t_h2": {}, "t_gt": {}, "gt_used": {}, "t_dn_bank": {},
               "ws_free": {0: None, 1: None, 2: None, 3: None},
               "xf_tok": {}, "xf_used": {}, "act_ready": {}, "h2_used": {}}

        def load_xtile_ffn(g):
            sl = g % 2
            dst = XT[:, sl * 4096:(sl + 1) * 4096].rearrange("p (fc t) -> p fc t", fc=8)
            w = [atn["xpost"][g] if g < len(atn["xpost"]) else None,
                 ffn["xf_used"].get(g - 2)]
            w += xt_free[sl] if isinstance(xt_free[sl], list) else [xt_free[sl]]
            if getattr(cfg, "xload_y", False):
                Y(lambda y, d=dst, g=g: y.dma_start(
                    out=d, in_=xscr[:, :, g * TILE:(g + 1) * TILE]), w, dma=yx[sl])
            else:
                Q(lambda q, d=dst, g=g: q.dma_start(
                    out=d, in_=xscr[:, :, g * TILE:(g + 1) * TILE]), w, dma=yx[sl])
            ffn["xf_tok"][g] = (yx[sl], yx[sl].n)

        def emit_lnb(g):
            if g >= cfg.n_ffn_tiles:
                return
            sl = g % 2
            b_ = g // 4
            xt_ap = XT[:, sl * 4096:(sl + 1) * 4096].rearrange("p (fc t) -> p fc t", fc=8)
            ffn["t_h2"][g] = emit_apply(
                "f", g, xt_ap, U1, H2r[sl], 32, 48, b_, ffn["xf_tok"][g],
                PB[0], 0, PB[2], 2, ffn["h2_used"].get(g - 2))

        def emit_router(g):
            if g >= cfg.n_ffn_tiles:
                return
            sl = g % 2
            tl = None
            for st_ in range(4):
                for kc in range(8):
                    w = [ffn["t_h2"][g][kc], t_qld]
                    if st_ == 0 and kc == 0:
                        w.append(pb_free[3])
                    tl = T(lambda t, st_=st_, kc=kc, sl=sl: t.matmul(
                        PB[3][:, st_ * 16:(st_ + 1) * 16],
                        U1[:, H2r[sl] + kc * 512 + st_ * 128:H2r[sl] + kc * 512 + (st_ + 1) * 128],
                        RT[:, kc * 16:(kc + 1) * 16],
                        start=(kc == 0), stop=(kc == 7), skip_group_check=True),
                        w, sig=(st_ == 3 and kc == 7))
            lseg = PB[3][:, 0:64].rearrange("p (gg n) -> p gg n", gg=4)
            t1 = V(lambda v: v.reduce_max(GATE[:, GS:GS + 4], lseg, axis=AX),
                   [tl], sig=True)
            eseg = GATE[:, GE:GE + 64].rearrange("p (gg n) -> p gg n", gg=4)
            t2 = V(lambda v: v.tensor_tensor(
                eseg, lseg, GATE[:, GS:GS + 4].unsqueeze(2).broadcast_to([128, 4, 16]),
                ALU.subtract), [t1], sig=True)
            pb_free[3] = t2
            t3 = S(lambda s: s.activation(GATE[:, GE:GE + 64], GATE[:, GE:GE + 64],
                                          AF.Exp), [t2], sig=True)
            t4 = V(lambda v: v.reduce_sum(GATE[:, GS + 4:GS + 8], eseg, axis=AX),
                   [t3], sig=True)
            t5 = V(lambda v: v.reciprocal(GATE[:, GS + 8:GS + 12],
                                          GATE[:, GS + 4:GS + 8]), [t4], sig=True)
            tcur = V(lambda v: v.tensor_copy(GATE[:, GC:GC + 64], GATE[:, GE:GE + 64]),
                     [t3], sig=True)
            cseg = GATE[:, GC:GC + 64].rearrange("p (gg n) -> p gg n", gg=4)
            mseg = GATE[:, GM:GM + 64].rearrange("p (gg n) -> p gg n", gg=4)
            for r in range(4):
                tcur = V(lambda v: v.reduce_max(GATE[:, GS + 12:GS + 16], cseg, axis=AX),
                         [tcur], sig=True)
                if r < 3:
                    tm = V(lambda v: v.tensor_tensor(
                        mseg, cseg,
                        GATE[:, GS + 12:GS + 16].unsqueeze(2).broadcast_to([128, 4, 16]),
                        ALU.is_ge), [tcur], sig=True)
                    tcur = V(lambda v: v.scalar_tensor_tensor(
                        GATE[:, GC:GC + 64], GATE[:, GM:GM + 64], -1e9,
                        GATE[:, GC:GC + 64], ALU.mult, ALU.add), [tm], sig=True)
            tg1 = V(lambda v: v.tensor_tensor(
                mseg, eseg,
                GATE[:, GS + 12:GS + 16].unsqueeze(2).broadcast_to([128, 4, 16]),
                ALU.is_ge), [tcur], sig=True)
            tg2 = V(lambda v: v.tensor_tensor(GATE[:, GM:GM + 64], GATE[:, GM:GM + 64],
                                              GATE[:, GE:GE + 64], ALU.mult),
                    [tg1], sig=True)
            gseg = GATE[:, GE:GE + 64].rearrange("p (gg n) -> p gg n", gg=4)
            tg3 = V(lambda v: v.tensor_tensor(
                gseg, mseg,
                GATE[:, GS + 8:GS + 12].unsqueeze(2).broadcast_to([128, 4, 16]),
                ALU.mult), [tg2, t5], sig=True)
            tg4 = V(lambda v: v.tensor_copy(GATB[:, :], GATE[:, GE:GE + 64]),
                    [tg3], sig=True)
            ffn["tg4"] = (tg4, g)

        def emit_router_gt(g):
            if g >= cfg.n_ffn_tiles:
                return
            sl = g % 2
            tg4, g_chk = ffn["tg4"]
            assert g_chk == g
            ttr = None
            for st_ in range(4):
                w = [tg4, t_const]
                if st_ == 0:
                    w.append(pb_free[5])
                    w.append(ffn["gt_used"].get(g - 2))
                ttr = T(lambda t, st_=st_: t.transpose(
                    PB[5].bitcast(BF16)[0:16, st_ * 128:(st_ + 1) * 128],
                    GATB[:, st_ * 16:(st_ + 1) * 16],
                    CON[:, C_ID:C_ID + 128]), w, sig=(st_ == 3))
            tge = S(lambda s, sl=sl: s.copy(GTT[0:16, sl * 512:(sl + 1) * 512],
                                            PB[5].bitcast(BF16)[0:16, 0:512]),
                    [ttr], sig=True)
            pb_free[5] = tge
            ffn["t_gt"][g] = tge

        def stream_w(x_, g, which, base):
            i = x_ % 3
            src = (wup if which == "up" else wdn)[x_]
            ctr = upc if which == "up" else dnc
            w = [(ctr, base + x_ - 2) if base + x_ - 2 > 0 else None]
            if g == 0:
                stg_idx = 2 * x_ + 1 if which == "up" else 2 * NX + 2 * x_ + 1
                w.append(sst["wr"][stg_idx] if stg_idx < len(sst["wr"]) else None)
                if x_ < 3:
                    w += [t_attn_T, t_attn_V, t_attn_S]
            if which == "up":
                Y(lambda y, d=U1[:, USL[i]:USL[i] + 4096], s=src: y.dma_start(
                    out=d, in_=s[:, :]), w, dma=yws[i])
                return (yws[i], yws[i].n)
            if i < 2:
                q_ = qws[i]
                Q(lambda q, d=U1[:, DSL[i]:DSL[i] + 4096], s=src: q.dma_start(
                    out=d, in_=s[:, :]), w, dma=q_)
                return (q_, q_.n)
            Y(lambda y, d=U1[:, DSL[2]:DSL[2] + 4096], s=src: y.dma_start(
                out=d, in_=s[:, :]), w, dma=yws[3])
            return (yws[3], yws[3].n)

        for g in range(cfg.n_ffn_tiles):
            sl = g % 2
            b_ = g // 4
            if g == 0:
                load_xtile_ffn(0)
                if cfg.n_ffn_tiles > 1:
                    load_xtile_ffn(1)
                emit_lnb(0)
                dump("h2t0", U1[:, H2r[0]:H2r[0] + 4096], [ffn["t_h2"][0][7]])
                emit_router(0)
                emit_router_gt(0)
            up_base, dn_base = upc.n, dnc.n
            lazy = getattr(cfg, "lazy_streams", False)
            if not lazy:
                up_toks = [stream_w(x_, g, "up", up_base) for x_ in range(NX)]
                dn_toks = [stream_w(x_, g, "dn", dn_base) for x_ in range(NX)]
            BUP = (6, 7, 1)
            for x_ in range(NX):
                t_ws = stream_w(x_, g, "up", up_base) if lazy else up_toks[x_]
                tup_last = None
                gel = [None] * ECH
                for ec in range(ECH):
                    bidx = BUP[(x_ * ECH + ec) % 3]
                    tup = None
                    for kc in range(8):
                        w = [ffn["t_h2"][g][kc]]
                        if kc == 0:
                            w += [t_ws, pb_free[bidx]]
                        if kc == 7 and ec == ECH - 1:
                            tup = mk.emit("T", lambda t, x_=x_, ec=ec, kc=kc, bidx=bidx,
                                          i=x_ % 3, sl=sl: t.matmul(
                                PB[bidx],
                                U1[:, USL[i] + kc * 512 + ec * 128:USL[i] + kc * 512 + (ec + 1) * 128],
                                U1[:, H2r[sl] + kc * 512:H2r[sl] + (kc + 1) * 512],
                                start=(kc == 0), stop=(kc == 7), skip_group_check=True),
                                w, inc=(upc, 1))
                        else:
                            tup = T(lambda t, x_=x_, ec=ec, kc=kc, bidx=bidx, i=x_ % 3, sl=sl:
                                    t.matmul(
                                PB[bidx],
                                U1[:, USL[i] + kc * 512 + ec * 128:USL[i] + kc * 512 + (ec + 1) * 128],
                                U1[:, H2r[sl] + kc * 512:H2r[sl] + (kc + 1) * 512],
                                start=(kc == 0), stop=(kc == 7), skip_group_check=True),
                                w, sig=(kc == 7))
                    gel[ec] = S(lambda s, x_=x_, ec=ec, bidx=bidx: s.activation(
                        U2[:, x_ * 2048 + ec * 512:x_ * 2048 + (ec + 1) * 512],
                        PB[bidx], AF.Gelu), [tup], sig=True)
                    pb_free[bidx] = gel[ec]
                    tup_last = tup
                if x_ >= 2:
                    e_idx = x_ - 2
                    tbc = T(lambda t, e_idx=e_idx, sl=sl: t.matmul(
                        PB[4], CON[0:16, C_SEL + e_idx * 128:C_SEL + (e_idx + 1) * 128],
                        GTT[0:16, sl * 512:(sl + 1) * 512],
                        start=True, stop=True, skip_group_check=True),
                        [ffn["t_gt"][g], pb_free[4], t_const], sig=True)
                    ffn["gt_used"][g] = tbc
                    tgb = S(lambda s: s.copy(GBS[:, :], PB[4]),
                            [tbc, ffn["act_ready"].get(x_ - 1)], sig=True)
                    if g == 0 and x_ == 2:
                        dump("gbs0", GBS, [tgb])
                        dump("gtt0", GTT[0:16, 0:512], [ffn["t_gt"][g]])
                    pb_free[4] = tgb
                    tmul = None
                    for ec in range(ECH):
                        tmul = V(lambda v, x_=x_, ec=ec: v.tensor_tensor(
                            U2[:, x_ * 2048 + ec * 512:x_ * 2048 + (ec + 1) * 512],
                            GBS[:, :],
                            U2[:, x_ * 2048 + ec * 512:x_ * 2048 + (ec + 1) * 512],
                            ALU.mult), [gel[ec], tgb], sig=(ec == ECH - 1))
                    ffn["act_ready"][x_] = tmul
                    if g == 0 and x_ == 2:
                        dump("act2", U2[:, 2 * 2048:3 * 2048], [tmul])
                else:
                    ffn["act_ready"][x_] = gel[ECH - 1]
                    if g == 0 and x_ == 0:
                        dump("act0", U2[:, 0:2048], [gel[ECH - 1]])
                if x_ == 3:
                    emit_lnb(g + 1)
                if x_ == 11:
                    emit_router(g + 1)
                if x_ == 15:
                    emit_router_gt(g + 1)
            ffn["h2_used"][g] = (upc, upc.n)
            for x_ in range(NX):
                t_ws = stream_w(x_, g, "dn", dn_base) if lazy else dn_toks[x_]
                tdn = None
                for hc in range(8):
                    for ec in range(ECH):
                        w = []
                        if ec == 0:
                            w = [t_ws, ffn["act_ready"][x_]]
                            if x_ == 0:
                                w.append(pb_free[hc])
                        if hc == 7 and ec == ECH - 1:
                            tdn = mk.emit("T", lambda t, x_=x_, hc=hc, ec=ec, i=x_ % 3:
                                          t.matmul(
                                PB[hc],
                                U1[:, DSL[i] + ec * 1024 + hc * 128:DSL[i] + ec * 1024 + (hc + 1) * 128],
                                U2[:, x_ * 2048 + ec * 512:x_ * 2048 + (ec + 1) * 512],
                                start=(x_ == 0 and ec == 0),
                                stop=(x_ == NX - 1 and ec == ECH - 1),
                                skip_group_check=True), w, inc=(dnc, 1))
                        else:
                            tdn = T(lambda t, x_=x_, hc=hc, ec=ec, i=x_ % 3: t.matmul(
                                PB[hc],
                                U1[:, DSL[i] + ec * 1024 + hc * 128:DSL[i] + ec * 1024 + (hc + 1) * 128],
                                U2[:, x_ * 2048 + ec * 512:x_ * 2048 + (ec + 1) * 512],
                                start=(x_ == 0 and ec == 0),
                                stop=(x_ == NX - 1 and ec == ECH - 1),
                                skip_group_check=True),
                                w, sig=(ec == ECH - 1 and x_ == NX - 1))
                        if x_ == NX - 1 and ec == ECH - 1:
                            ffn["t_dn_bank"][hc] = tdn
            ev_order = [6, 7, 0, 1, 2, 3, 4, 5]
            for k, hc in enumerate(ev_order):
                par = k % 2
                w = [ffn["t_dn_bank"][hc], ffn["xf_tok"][g], t_mod_done,
                     (so[par], so[par].n)]
                tev = V(lambda v, hc=hc, par=par, sl=sl, b_=b_: v.scalar_tensor_tensor(
                    SCRF[:, OST[par]:OST[par] + 512],
                    PB[hc], DERV[:, 48 + hc * 2 + b_:48 + hc * 2 + b_ + 1],
                    XT[:, sl * 4096 + hc * 512:sl * 4096 + (hc + 1) * 512],
                    ALU.mult, ALU.add), w, sig=True)
                pb_free[hc] = tev
                mk.emit("S", lambda s, hc=hc, par=par, g=g: s.dma_start(
                    out=outT[hc * 128:(hc + 1) * 128, g * TILE:(g + 1) * TILE],
                    in_=SCRF[:, OST[par]:OST[par] + 512]), [tev], dma=so[par])
            ffn["xf_used"][g] = pb_free[5]
            if g + 2 < cfg.n_ffn_tiles:
                load_xtile_ffn(g + 2)

        mk.emit("S", lambda s: s.memzero(SMALL[:, 15:16]),
                [(so[0], so[0].n), (so[1], so[1].n)])

        mk.check()
        for c in mk.ctrs:
            c.sem = st.enter_context(nc.semaphore(c.name))
        block = st.enter_context(nc.Block())
        mk.replay(block)
    return nc, dbg_names


def host_prep(inputs):
    c = np.ascontiguousarray
    x = inputs["L0_attn_hidden_states"]
    common = {
        "wqT": c(inputs["L0_attn_Wq"].T), "wkT": c(inputs["L0_attn_Wk"].T),
        "wvT": c(inputs["L0_attn_Wv"].T), "woT": c(inputs["L0_attn_Wo"].T),
        "fm1qT": c(inputs["L0_attn_fmq_w1"].T), "fm2qT": c(inputs["L0_attn_fmq_w2"].T),
        "fm1kT": c(inputs["L0_attn_fmk_w1"].T), "fm2kT": c(inputs["L0_attn_fmk_w2"].T),
        "fmq_b1": c(inputs["L0_attn_fmq_b1"].reshape(DK, 1)),
        "fmq_b2": c(inputs["L0_attn_fmq_b2"].reshape(DK, 1)),
        "fmk_b1": c(inputs["L0_attn_fmk_b1"].reshape(DK, 1)),
        "fmk_b2": c(inputs["L0_attn_fmk_b2"].reshape(DK, 1)),
        "rms_w": c(inputs["L0_attn_rms_w"].reshape(DK, 1)),
        "suT": c(inputs["L0_ffn_shared_up"].transpose(0, 2, 1)),
        "sdT": c(inputs["L0_ffn_shared_down"].transpose(0, 2, 1)),
        "ruT": c(inputs["L0_ffn_routed_up"].transpose(0, 2, 1)),
        "rdT": c(inputs["L0_ffn_routed_down"].transpose(0, 2, 1)),
        "routerT": c(inputs["L0_ffn_router_w"].T),
        "wmod": c(inputs["L0_wmod"]),
    }
    per_core = []
    for cc in range(8):
        d = dict(common)
        d["xT"] = c(x[2 * cc:2 * cc + 2].reshape(2 * TSEQ, HID).T)
        d["cT"] = c(inputs["L0_c"][2 * cc:2 * cc + 2].T)
        per_core.append(d)
    return per_core


def kernel(**inputs):
    from concourse.bass_utils import run_bass_kernel_spmd
    cfg = Cfg()
    nc, _ = build(cfg)
    in_maps = host_prep(inputs)
    res = run_bass_kernel_spmd(nc, in_maps, core_ids=list(range(8)))
    outs = []
    for cc in range(8):
        o = res.results[cc]["outT"]
        outs.append(np.ascontiguousarray(np.asarray(o).T).reshape(2, TSEQ, HID))
    return np.concatenate(outs, axis=0).astype(np.float32)
